# version 11
# baseline (speedup 1.0000x reference)
import math
import numpy as np
import ml_dtypes
import concourse.bass as bass
import concourse.mybir as mybir
from concourse.bass_utils import run_bass_kernel_spmd

F32 = mybir.dt.float32
BF16 = mybir.dt.bfloat16
AF = mybir.ActivationFunctionType
ALU = mybir.AluOpType
AX = mybir.AxisListType.X

S = 2048
D = 2048
H = 16
HD = 128
NT = S // 128
NK = D // 128
NU = 20
EI = 1408
NJ = EI // 128
BASE_LR_INV = float(0.01 + math.log(-math.expm1(-0.01)))
NS_COEFS = [(4.0848, -6.8946, 2.927), (3.9505, -6.3029, 2.6377),
            (3.7418, -5.5913, 2.3037), (2.8769, -3.1427, 1.2046),
            (2.8366, -3.0525, 1.2012)]

ENGS = ("tensor", "scalar", "vector", "sync", "gpsimd")


class Emit:

    def __init__(self, nc):
        self.nc = nc
        self.ops = {e: [] for e in ENGS}
        self.semv = {}
        self.sems = {}
        self.hw = {e: {} for e in ENGS}

    def sem(self, name):
        self.sems[name] = self.nc.alloc_semaphore(name)
        self.semv[name] = 0

    def _op(self, eng, fn, waits, inc):
        tgt = None
        if inc is not None:
            s, n = inc
            self.semv[s] += n
            tgt = self.semv[s]
        ws = []
        for s, v in waits or ():
            if v is None or v <= 0:
                continue
            if self.hw[eng].get(s, -1) >= v:
                continue
            self.hw[eng][s] = v
            ws.append((self.sems[s], v))
        inch = (self.sems[inc[0]], inc[1]) if inc is not None else None

        def run(e, ws=ws, fn=fn, inch=inch):
            for sh, v in ws:
                e.wait_ge(sh, v)
            ins = fn(e)
            if inch is not None:
                ins.then_inc(inch[0], inch[1])
        self.ops[eng].append(run)
        return tgt

    def pe(self, fn, waits=None, inc=None):
        return self._op("tensor", fn, waits, inc)

    def act(self, fn, waits=None, inc=None):
        return self._op("scalar", fn, waits, inc)

    def dve(self, fn, waits=None, inc=None):
        return self._op("vector", fn, waits, inc)

    def syn(self, fn, waits=None, inc=None):
        return self._op("sync", fn, waits, inc)

    def gps(self, fn, waits=None, inc=None):
        return self._op("gpsimd", fn, waits, inc)

    def v(self, s):
        return self.semv[s]

    def phase_barrier(self):
        ws = [(s, self.semv[s]) for s in ("SP", "SA", "SD", "SW", "SX", "SC")]
        for eng in ENGS:
            self._op(eng, lambda e: e.nop(), list(ws), None)

    def barrier_sst(self):
        tgt = self.semv["SST"]
        for eng in ENGS:
            self._op(eng, lambda e: e.nop(), [("SST", tgt)], None)


def build(n_heads=H, n_units=NU, debug=False):
    nc = bass.Bass("TRN2", target_bir_lowering=False)

    x_d = nc.dram_tensor("x", [S, D], F32, kind="ExternalInput").ap()
    wqkv_d = nc.dram_tensor("wqkv_t", [D, 3 * D], F32, kind="ExternalInput").ap()
    wo_d = nc.dram_tensor("wo_t", [D, D], F32, kind="ExternalInput").ap()
    wlr_d = nc.dram_tensor("wlr_t", [D, 48], F32, kind="ExternalInput").ap()
    wnat_d = nc.dram_tensor("w_nat", [3, H, 128, 128], F32, kind="ExternalInput").ap()
    wt_d = nc.dram_tensor("w_t", [2, H, 128, 128], F32, kind="ExternalInput").ap()
    onw_d = nc.dram_tensor("onw", [128, 1], F32, kind="ExternalInput").ap()
    wgate_d = nc.dram_tensor("wgate_t", [D, 32], F32, kind="ExternalInput").ap()
    wgu_d = nc.dram_tensor("wgu_t", [NU, D, 2816], F32, kind="ExternalInput").ap()
    wdn_d = nc.dram_tensor("wdn_t", [NU, EI, D], F32, kind="ExternalInput").ap()
    out_d = nc.dram_tensor("out", [S, D], F32, kind="ExternalOutput").ap()

    scr_kind = "ExternalOutput" if debug else "Internal"
    acc_d = nc.dram_tensor("acc", [S, D], F32, kind=scr_kind).ap()
    oscr_d = nc.dram_tensor("o_scr", [H, 128, S], BF16, kind="Internal").ap()
    if debug:
        xn_dbg = nc.dram_tensor("xn_dbg", [NK, 128, S], F32, kind="ExternalOutput").ap()
        oscr_dbg = nc.dram_tensor("o_scr_dbg", [H, 128, S], F32, kind="ExternalOutput").ap()
        call_dbg = nc.dram_tensor("call_dbg", [128, NT * 17], F32, kind="ExternalOutput").ap()
        lr_dbg = nc.dram_tensor("lr_dbg", [128, NT * 48], F32, kind="ExternalOutput").ap()
        qkv_dbg = nc.dram_tensor("qkv_dbg", [3, 128, S], F32, kind="ExternalOutput").ap()
        e16_dbg = nc.dram_tensor("e16_dbg", [128, NT * 16], F32, kind="ExternalOutput").ap()
        rt_dbg = nc.dram_tensor("rt_dbg", [128, NT * 17], F32, kind="ExternalOutput").ap()
        wg_dbg = nc.dram_tensor("wg_dbg", [128, NK * 32], F32, kind="ExternalOutput").ap()
        nrm_dbg = nc.dram_tensor("nrm_dbg", [128, 64], F32, kind="ExternalOutput").ap()
        qtm_dbg = nc.dram_tensor("qtm_dbg", [128, S], F32, kind="ExternalOutput").ap()
        dw_dbg = nc.dram_tensor("dw_dbg", [3, 128, 128], F32, kind="ExternalOutput").ap()
        ns_dbg = nc.dram_tensor("ns_dbg", [3, 128, 128], F32, kind="ExternalOutput").ap()
        wn_dbg = nc.dram_tensor("wn_dbg", [3, 128, 128], F32, kind="ExternalOutput").ap()
        o_dbg = nc.dram_tensor("o_dbg", [128, S], F32, kind="ExternalOutput").ap()
        h_dbg = nc.dram_tensor("h_dbg", [128, NJ * S], F32, kind="ExternalOutput").ap()

    ident = nc.inline_tensor(np.eye(128, dtype=ml_dtypes.bfloat16).view(np.uint16),
                             name="ident_bf")
    identf = nc.inline_tensor(np.eye(128, dtype=np.float32), name="ident_f32")
    onesr = nc.inline_tensor(np.ones((1, 128), dtype=ml_dtypes.bfloat16).view(np.uint16),
                             name="ones_row_bf")
    onesc = nc.inline_tensor(np.ones((128, 1), dtype=ml_dtypes.bfloat16).view(np.uint16),
                             name="ones_col_bf")
    cst_np = np.tile(np.array([[1.0 / D, 1e-5, BASE_LR_INV, 1e-7]], np.float32), (128, 1))
    cst_t = nc.inline_tensor(cst_np, name="cst_f32")

    A = nc.alloc_sbuf_tensor
    xn_fm = A("xn_fm", [128, NK * S], BF16)
    c_all = A("c_all", [128, NT * 17], F32)
    lr_tm = A("lr_tm", [128, NT * 48], F32)


    wlr_sb = A("wlr_sb", [128, NK * 48], BF16)
    onw_sb = A("onw_sb", [128, 1], F32)
    id_sb = A("id_sb", [128, 128], BF16)
    idf_sb = A("idf_sb", [128, 128], F32)
    wgf_sb = A("wgf_sb", [128, NK * 32], F32)
    onr_sb = A("onr_sb", [1, 128], BF16)
    onc_sb = A("onc_sb", [128, 1], BF16)
    cst_sb = A("cst_sb", [128, 4], F32)
    rt_sb = A("rt_sb", [128, 17], F32)
    rt_all = A("rt_all", [128, NT * 17], F32)
    e16a = A("e16a", [128, 16], F32)
    e16b = A("e16b", [128, 16], F32)
    msk = A("msk", [128, 16], F32)
    nm_sb = A("nm_sb", [128, 1], F32)
    mr_sb = A("mr_sb", [128, 1], F32)
    mr2_sb = A("mr2_sb", [128, 1], F32)
    ns_sb = A("ns_sb", [128, 1], F32)
    tng = A("tng", [128, 1], F32)
    st_s = [A(f"st_s{i}", [128, 1], F32) for i in range(2)]
    st_m = [A(f"st_m{i}", [128, 1], F32) for i in range(2)]
    st_v = [A(f"st_v{i}", [128, 1], F32) for i in range(2)]
    st_r = [A(f"st_r{i}", [128, 1], F32) for i in range(2)]
    sq_scrap = A("sq_scrap", [128, 2048], F32)

    k = Emit(nc)
    for s in ("SX", "SW", "SP", "SA", "SD", "SC", "SST"):
        k.sem(s)

    bank = [nc.alloc_psum_tensor(f"bank{i}", [128, 512], F32) for i in range(8)]
    bank_free = [("SP", 0)] * 8

    def mm(b, out_ap_fn, lhsT_fn, rhs_fn, start, stop, waits=None, inc=None):
        def f(e):
            return e.matmul(out_ap_fn(), lhsT_fn(), rhs_fn(), start=start, stop=stop)
        return k.pe(f, waits=waits, inc=inc)

    k.syn(lambda e: e.dma_start(out=id_sb[:, :], in_=ident.ap().bitcast(BF16)), inc=("SX", 16))
    k.syn(lambda e: e.dma_start(out=idf_sb[:, :], in_=identf.ap()), inc=("SX", 16))
    k.syn(lambda e: e.dma_start(
        out=wgf_sb[:, :].rearrange("p (c m) -> p c m", c=NK),
        in_=wgate_d.rearrange("(c p) m -> p c m", p=128)), inc=("SX", 16))
    k.syn(lambda e: e.dma_start(out=onr_sb[:, :], in_=onesr.ap().bitcast(BF16)), inc=("SX", 16))
    k.syn(lambda e: e.dma_start(out=onc_sb[:, :], in_=onesc.ap().bitcast(BF16)), inc=("SX", 16))
    k.syn(lambda e: e.dma_start(out=onw_sb[:, :], in_=onw_d), inc=("SX", 16))
    k.syn(lambda e: e.dma_start(out=cst_sb[:, :], in_=cst_t.ap()), inc=("SX", 16))
    k.gps(lambda e: e.dma_start(
        out=wlr_sb[:, :].rearrange("p (c m) -> p c m", c=NK),
        in_=wlr_d.rearrange("(c p) m -> p c m", p=128)), inc=("SW", 16))
    sx_const = k.v("SX")
    sw_const = k.v("SW")

    import contextlib
    ln_es = contextlib.ExitStack()
    xt = [ln_es.enter_context(nc.sbuf_tensor(f"xt{i}", [128, 2048], F32)) for i in range(2)]
    xcb = [ln_es.enter_context(nc.sbuf_tensor(f"xcb{i}", [128, 2048], F32)) for i in range(2)]
    xnt = [ln_es.enter_context(nc.sbuf_tensor(f"xnt{i}", [128, 2048], BF16)) for i in range(2)]
    lnsq = [ln_es.enter_context(nc.sbuf_tensor(f"lnsq{i}", [128, 2048], F32)) for i in range(2)]
    xnf32 = ln_es.enter_context(nc.sbuf_tensor("xnf32", [128, NK * 128], F32))

    xt_free = [0, 0]
    rtf_done = 0
    rt_ready = [0] * NT
    st_m_free = 0
    xt_free_d = [0, 0]
    xcb_free = [0, 0]
    xcb_free_a = [0, 0]
    xnt_free = [0, 0]
    xn_ready = [0] * NT

    for tt in range(NT):
        p = tt % 2
        sx = k.syn(lambda e, tt=tt, p=p: e.dma_start(out=xt[p][:, :], in_=x_d[tt * 128:(tt + 1) * 128, :]),
                   waits=[("SD", xt_free[p])], inc=("SX", 16))
        sd = k.dve(lambda e, p=p: e.tensor_reduce(st_s[p][:, :], xt[p][:, :], AX, ALU.add, negate=True),
                   waits=[("SX", sx), ("SA", st_m_free)], inc=("SD", 1))
        sa = k.act(lambda e, p=p: e.activation(st_m[p][:, :], st_s[p][:, :], AF.Identity,
                                          scale=cst_sb[:, 0:1]),
                   waits=[("SD", sd), ("SX", sx_const)], inc=("SA", 1))
        sd = k.dve(lambda e, p=p: e.tensor_scalar(xcb[p][:, :], xt[p][:, :], st_m[p][:, 0:1],
                                                  None, ALU.add),
                   waits=[("SA", sa), ("SA", xcb_free_a[p])], inc=("SD", 1))
        xt_free[p] = sd
        st_m_free = k.v("SA")
        sa = k.act(lambda e, p=p: e.activation(lnsq[p][:, :], xcb[p][:, :], AF.Square),
                   waits=[("SD", sd)], inc=("SA", 1))
        sd = k.dve(lambda e, p=p: e.tensor_reduce(st_v[p][:, :], lnsq[p][:, :], AX, ALU.add),
                   waits=[("SA", sa)], inc=("SD", 1))
        sa = k.act(lambda e, p=p: e.activation(st_v[p][:, :], st_v[p][:, :], AF.Sqrt,
                                          scale=cst_sb[:, 0:1], bias=cst_sb[:, 1:2]),
                   waits=[("SD", sd)], inc=("SA", 1))
        sd = k.dve(lambda e, p=p: e.reciprocal(st_r[p][:, :], st_v[p][:, :]), waits=[("SA", sa)], inc=("SD", 1))
        sa = k.act(lambda e, p=p: e.activation(xnt[p][:, :], xcb[p][:, :], AF.Identity,
                                               scale=st_r[p][:, 0:1]),
                   waits=[("SD", sd), ("SP", xnt_free[p])], inc=("SA", 1))
        xcb_free[p] = k.v("SD")
        xcb_free_a[p] = sa
        xt_free_d[p] = sd
        last_sp = 0
        for g in range(4):
            b = g % 2
            for j in range(4):
                kc = g * 4 + j
                sp = mm(b, lambda b=b, j=j: bank[b][:, j * 128:(j + 1) * 128],
                        lambda p=p, kc=kc: xnt[p][:, kc * 128:(kc + 1) * 128],
                        lambda: id_sb[:, :], True, True,
                        waits=[("SA", sa), (bank_free[b][0], bank_free[b][1])],
                        inc=("SP", 1) if j == 3 else None)
            k.dve(lambda e, b=b, g=g, tt=tt: e.tensor_copy(
                xn_fm[:, :].rearrange("p (c s) -> p c s", c=NK)[:, g * 4:(g + 1) * 4, tt * 128:(tt + 1) * 128],
                bank[b][:, :].rearrange("p (c s) -> p c s", c=4)),
                waits=[("SP", sp), ("SP", rtf_done)], inc=("SD", 1))
            sdv = k.dve(lambda e, b=b, g=g: e.tensor_copy(
                xnf32[:, g * 512:(g + 1) * 512], bank[b][:, :]),
                inc=("SD", 1))
            bank_free[b] = ("SD", sdv)
            last_sp = sp
        xnt_free[p] = last_sp
        xn_ready[tt] = sdv
        for kc in range(NK):
            rtf_done = mm(2, lambda: bank[2][:, 0:32],
                          lambda kc=kc: xnf32[:, kc * 128:(kc + 1) * 128],
                          lambda kc=kc: wgf_sb[:, kc * 32:(kc + 1) * 32],
                          kc == 0, kc == NK - 1,
                          waits=[("SD", sdv), ("SX", sx_const),
                                 (bank_free[2][0], bank_free[2][1])],
                          inc=("SP", 1) if kc == NK - 1 else None)
        sa_rt = k.act(lambda e, tt=tt: e.activation(rt_all[:, tt * 17:(tt + 1) * 17],
                                                    bank[2][:, 0:17], AF.Identity),
                      waits=[("SP", rtf_done)], inc=("SA", 1))
        bank_free[2] = ("SA", sa_rt)
        rt_ready[tt] = sa_rt
    ln_es.close()
    k.phase_barrier()

    if debug:
        for kc in range(NK):
            k.gps(lambda e, kc=kc: e.dma_start(out=xn_dbg[kc], in_=xn_fm[:, kc * S:(kc + 1) * S]),
                  waits=[("SD", xn_ready[NT - 1])], inc=("SST", 16))

    rt_state = {"free_sd": 0}

    def emit_router_tile(tt):
        sa = k.act(lambda e, tt=tt: e.activation(rt_sb[:, :], rt_all[:, tt * 17:(tt + 1) * 17],
                                                 AF.Identity),
                   waits=[("SA", rt_ready[tt]), ("SD", rt_state["free_sd"])], inc=("SA", 1))
        sd = k.dve(lambda e: e.tensor_reduce(nm_sb[:, :], rt_sb[:, 0:16], AX, ALU.max, negate=True),
                   waits=[("SA", sa)], inc=("SD", 1))
        sa = k.act(lambda e: e.activation(e16a[:, :], rt_sb[:, 0:16], AF.Exp, bias=nm_sb[:, 0:1]),
                   waits=[("SD", sd)], inc=("SA", 1))
        sa = k.act(lambda e: e.activation(tng[:, :], rt_sb[:, 16:17], AF.Tanh, scale=0.5),
                   inc=("SA", 1))
        csl = lambda tt=tt: c_all[:, tt * 17: tt * 17 + 16]
        k.dve(lambda e, tt=tt: e.tensor_scalar(c_all[:, tt * 17 + 16: tt * 17 + 17], tng[:, :],
                                               0.5, 0.5, ALU.mult, ALU.add),
              waits=[("SA", sa)])
        k.dve(lambda e, csl=csl: e.memset(csl(), 0.0))
        ebufs = [e16a, e16b]
        for r in range(4):
            ecur = ebufs[r % 2]
            enxt = ebufs[(r + 1) % 2]
            sd = k.dve(lambda e, ecur=ecur: e.tensor_reduce(mr_sb[:, :], ecur[:, :], AX, ALU.max),
                       inc=("SD", 1))
            sa = k.act(lambda e: e.activation(mr2_sb[:, :], mr_sb[:, :], AF.Identity),
                       waits=[("SD", sd)], inc=("SA", 1))
            k.dve(lambda e, ecur=ecur: e.tensor_scalar(msk[:, :], ecur[:, :], mr2_sb[:, 0:1],
                                                       None, ALU.is_ge),
                  waits=[("SA", sa)])
            k.dve(lambda e, csl=csl: e.scalar_tensor_tensor(csl(), msk[:, :], mr2_sb[:, 0:1], csl(),
                                                            ALU.mult, ALU.add))
            if r == 0:
                sd_ns_acc = k.dve(lambda e: e.tensor_copy(ns_sb[:, :], mr2_sb[:, :]), inc=("SD", 1))
            else:
                sd_ns_acc = k.dve(lambda e: e.tensor_add(ns_sb[:, :], ns_sb[:, :], mr2_sb[:, :]),
                                  inc=("SD", 1))
            if r < 3:
                sd = k.dve(lambda e, ecur=ecur, enxt=enxt: e.scalar_tensor_tensor(
                    enxt[:, :], msk[:, :], -1e30, ecur[:, :], ALU.mult, ALU.add),
                    inc=("SD", 1))
                sa = k.act(lambda e, enxt=enxt: e.activation(enxt[:, :], enxt[:, :], AF.Identity),
                           waits=[("SD", sd)], inc=("SA", 1))
                k.dve(lambda e: e.nop(), waits=[("SA", sa)])
        sa = k.act(lambda e: e.activation(mr2_sb[:, :], ns_sb[:, :], AF.Identity),
                   waits=[("SD", sd_ns_acc)], inc=("SA", 1))
        sd = k.dve(lambda e: e.reciprocal(mr_sb[:, :], mr2_sb[:, :]),
                   waits=[("SA", sa)], inc=("SD", 1))
        k.act(lambda e, csl=csl: e.activation(csl(), csl(), AF.Identity,
                                              scale=mr_sb[:, 0:1]),
              waits=[("SD", sd)], inc=("SA", 1))
        rt_state["free_sd"] = k.v("SD")

    import contextlib as _ctx
    lr_es = _ctx.ExitStack()
    lr_fm = lr_es.enter_context(nc.sbuf_tensor("lr_fm", [48, S], F32))
    lr_scrap_free = 0
    for st in range(4):
        sp = None
        for kc in range(NK):
            sp = mm(3, lambda: bank[3][0:48, :],
                    lambda kc=kc: wlr_sb[:, kc * 48:(kc + 1) * 48],
                    lambda kc=kc, st=st: xn_fm[:, kc * S + st * 512: kc * S + (st + 1) * 512],
                    kc == 0, kc == NK - 1,
                    waits=[("SD", xn_ready[NT - 1]), ("SW", sw_const),
                           (bank_free[3][0], bank_free[3][1])],
                    inc=("SP", 1) if kc == NK - 1 else None)
        sa = k.act(lambda e: e.activation(sq_scrap[0:48, 0:512], bank[3][0:48, :], AF.Abs,
                                          bias=cst_sb[0:48, 2:3]),
                   waits=[("SP", sp), ("SD", lr_scrap_free if st > 0 else 0)], inc=("SA", 1))
        sa = k.act(lambda e: e.activation(sq_scrap[0:48, 512:1024], sq_scrap[0:48, 0:512], AF.Exp,
                                          scale=-1.0), waits=[("SA", sa)], inc=("SA", 1))
        sa = k.act(lambda e: e.activation(sq_scrap[0:48, 512:1024], sq_scrap[0:48, 512:1024], AF.Ln,
                                          bias=1.0), waits=[("SA", sa)], inc=("SA", 1))
        sd = k.dve(lambda e: e.tensor_scalar(sq_scrap[0:48, 1024:1536], bank[3][0:48, :],
                                             BASE_LR_INV, 0.0, ALU.add, ALU.max),
                   waits=[("SA", sa)], inc=("SD", 1))
        bank_free[3] = ("SD", sd)
        sd = k.dve(lambda e, st=st: e.tensor_add(lr_fm[:, st * 512:(st + 1) * 512],
                                                 sq_scrap[0:48, 1024:1536], sq_scrap[0:48, 512:1024]),
                   inc=("SD", 1))
        lr_scrap_free = sd
    lr_fm_ready = sd
    for g in range(2):
        b = 4 + g
        for j in range(8):
            tt = g * 8 + j
            sp = mm(b, lambda b=b, j=j: bank[b][:, j * 48:(j + 1) * 48],
                    lambda tt=tt: lr_fm[:, tt * 128:(tt + 1) * 128],
                    lambda: idf_sb[0:48, 0:48], True, True,
                    waits=[("SD", lr_fm_ready), ("SX", sx_const),
                           (bank_free[b][0], bank_free[b][1])],
                    inc=("SP", 1) if j == 7 else None)
        sd = k.dve(lambda e, b=b, g=g: e.tensor_copy(lr_tm[:, g * 8 * 48:(g + 1) * 8 * 48],
                                                     bank[b][:, 0:384]),
                   waits=[("SP", sp)], inc=("SD", 1))
        bank_free[b] = ("SD", sd)
    lr_ready = k.v("SD")
    lr_es.close()
    k.phase_barrier()
    if debug:
        k.syn(lambda e: e.dma_start(out=lr_dbg, in_=lr_tm[:, :]),
              waits=[("SD", lr_ready)], inc=("SST", 16))

    at_es = contextlib.ExitStack()
    C = at_es.enter_context
    qkvw = [C(nc.sbuf_tensor(f"qkvw{i}", [128, 3 * NK * 128], BF16)) for i in range(2)]
    wnat = [C(nc.sbuf_tensor(f"wnat{i}", [128, 3 * 128], F32)) for i in range(2)]
    wT = [C(nc.sbuf_tensor(f"wT{i}", [128, 2 * 128], BF16)) for i in range(2)]
    w1bf = [C(nc.sbuf_tensor(f"w1bf{i}", [128, 128], BF16)) for i in range(2)]
    q_fm2 = [C(nc.sbuf_tensor(f"q_fm{i}", [128, S], BF16)) for i in range(2)]
    k_fm = C(nc.sbuf_tensor("k_fm", [128, S], BF16))
    v_fm = C(nc.sbuf_tensor("v_fm", [128, S], BF16))
    q_tm = C(nc.sbuf_tensor("q_tm", [128, S], BF16))
    k_tm = C(nc.sbuf_tensor("k_tm", [128, S], BF16))
    v_tm = C(nc.sbuf_tensor("v_tm", [128, S], BF16))
    nrm32 = C(nc.sbuf_tensor("nrm32", [128, 32], F32))
    rr32 = C(nc.sbuf_tensor("rr32", [128, 32], F32))
    rr32b = C(nc.sbuf_tensor("rr32b", [128, 32], F32))
    tanb = C(nc.sbuf_tensor("tanb", [128, 512], F32))
    sigb = C(nc.sbuf_tensor("sigb", [128, 512], F32))
    ub = C(nc.sbuf_tensor("ub", [128, 512], F32))
    dgb = C(nc.sbuf_tensor("dgb", [128, 512], F32))
    sgb2 = [C(nc.sbuf_tensor(f"sgb{i}", [128, 512], F32)) for i in range(2)]
    hbb = C(nc.sbuf_tensor("hbb", [128, 512], F32))
    dhb = C(nc.sbuf_tensor("dhb", [128, 512], F32))
    hidb2 = [C(nc.sbuf_tensor(f"hidb{i}", [128, 512], BF16)) for i in range(2)]
    dbmb = C(nc.sbuf_tensor("dbmb", [128, 512], BF16))
    dgab = C(nc.sbuf_tensor("dgab", [128, 512], BF16))
    hl1t = C(nc.sbuf_tensor("hl1t", [128, 512], BF16))
    dgat = C(nc.sbuf_tensor("dgat", [128, 512], BF16))
    dbmt = C(nc.sbuf_tensor("dbmt", [128, 512], BF16))
    kl0t = C(nc.sbuf_tensor("kl0t", [128, 512], BF16))
    kl2t = C(nc.sbuf_tensor("kl2t", [128, 512], BF16))
    Xb = [C(nc.sbuf_tensor(f"Xb{i}", [128, 128], BF16)) for i in range(3)]
    XTb = [C(nc.sbuf_tensor(f"XTb{i}", [128, 128], BF16)) for i in range(3)]
    Ab = [C(nc.sbuf_tensor(f"Ab{i}", [128, 128], BF16)) for i in range(3)]
    Abf = [C(nc.sbuf_tensor(f"Abf{i}", [128, 128], F32)) for i in range(3)]
    Bb = [C(nc.sbuf_tensor(f"Bb{i}", [128, 128], BF16)) for i in range(3)]
    dwssb = [C(nc.sbuf_tensor(f"dwssb{i}", [128, 1], BF16)) for i in range(3)]
    facb = [C(nc.sbuf_tensor(f"facb{i}", [128, 1], F32)) for i in range(3)]
    smll = [C(nc.sbuf_tensor(f"smll{i}", [1, 8], F32)) for i in range(3)]
    smlb = [C(nc.sbuf_tensor(f"smlb{i}", [1, 1], BF16)) for i in range(3)]
    sqG = C(nc.sbuf_tensor("sqG", [128, 384], F32))
    wn_h2 = [C(nc.sbuf_tensor(f"wn_h{i}", [128, 3], F32)) for i in range(2)]
    wnsq = C(nc.sbuf_tensor("wnsq", [128, 384], F32))
    rn3 = C(nc.sbuf_tensor("rn3", [128, 3], F32))
    Wf32 = C(nc.sbuf_tensor("Wf32", [128, 128], F32))
    Wnb = [C(nc.sbuf_tensor(f"Wnb{i}", [128, 128], BF16)) for i in range(3)]
    WnT = [C(nc.sbuf_tensor(f"WnT{i}", [128, 128], BF16)) for i in range(3)]
    o_bf = C(nc.sbuf_tensor("o_bf", [128, S], BF16))
    o2_bf = C(nc.sbuf_tensor("o2_bf", [128, S], BF16))
    o_tm = C(nc.sbuf_tensor("o_tm", [128, S], BF16))
    dsum = C(nc.sbuf_tensor("dsum", [128, 16], F32))
    rmfac = C(nc.sbuf_tensor("rmfac", [128, 16], F32))
    rmfacb = C(nc.sbuf_tensor("rmfacb", [128, 16], F32))

    hw_free = [(0, 0, 0), (0, 0, 0)]
    wnh_free = [0, 0]
    oscr_done = []

    def head_dmas(h):
        p = h % 2
        w = [("SP", hw_free[p][0]), ("SA", hw_free[p][1]), ("SD", hw_free[p][2])]
        for c in range(3):
            k.gps(lambda e, c=c, h=h, p=p: e.dma_start(
                out=qkvw[p][:, c * 2048:(c + 1) * 2048].rearrange("p (c d) -> p c d", d=128),
                in_=wqkv_d[:, c * 2048 + h * 128: c * 2048 + (h + 1) * 128]
                .rearrange("(c p) d -> p c d", p=128)), waits=w, inc=("SW", 16))
        k.gps(lambda e, h=h, p=p: e.dma_start(
            out=wT[p][:, :].rearrange("p (i d) -> p i d", i=2),
            in_=wt_d[:, h].rearrange("i p d -> p i d")), waits=w, inc=("SW", 16))
        k.gps(lambda e, h=h, p=p: e.dma_start(out=w1bf[p][:, :], in_=wnat_d[1, h]),
              waits=w, inc=("SW", 16))
        k.syn(lambda e, h=h, p=p: e.dma_start(
            out=wnat[p][:, :].rearrange("p (i d) -> p i d", i=3),
            in_=wnat_d[:, h].rearrange("i p d -> p i d")), waits=w, inc=("SX", 16))
        return k.v("SW"), k.v("SX")

    def transpose_batch(src_fn, dst_fn, n_tiles, b0, b1, src_ready, fuse_scale=None,
                        dst_dtype_bf=True):
        nb = (n_tiles + 3) // 4
        last_sa = None
        for g in range(nb):
            b = (b0, b1)[g % 2]
            cnt = min(4, n_tiles - g * 4)
            for j in range(cnt):
                t = g * 4 + j
                sp = mm(b, lambda b=b, j=j: bank[b][:, j * 128:(j + 1) * 128],
                        lambda t=t: src_fn(t), lambda: id_sb[:, :], True, True,
                        waits=[("SD", src_ready[0] if isinstance(src_ready, tuple) else src_ready),
                               (bank_free[b][0], bank_free[b][1])] +
                              ([("SA", src_ready[1])] if isinstance(src_ready, tuple) else []),
                        inc=("SP", 1) if j == cnt - 1 else None)
            use_act = (nb >= 2 and g % 2 == 1)
            if fuse_scale is None:
                if use_act:
                    sa = k.act(lambda e, b=b, g=g, cnt=cnt: e.activation(
                        dst_fn(g * 4, cnt), bank[b][:, 0:cnt * 128], AF.Identity),
                        waits=[("SP", sp)], inc=("SA", 1))
                else:
                    sd = k.dve(lambda e, b=b, g=g, cnt=cnt: e.tensor_copy(
                        dst_fn(g * 4, cnt), bank[b][:, 0:cnt * 128]),
                        waits=[("SP", sp)], inc=("SD", 1))
            else:
                for j in range(cnt):
                    t = g * 4 + j
                    if use_act:
                        sa = k.act(lambda e, b=b, j=j, t=t: e.activation(
                            dst_fn(t, None), bank[b][:, j * 128:(j + 1) * 128],
                            AF.Identity, scale=fuse_scale(t)),
                            waits=[("SP", sp)], inc=("SA", 1))
                    else:
                        sd = k.dve(lambda e, b=b, j=j, t=t: e.tensor_scalar(
                            dst_fn(t, None), bank[b][:, j * 128:(j + 1) * 128],
                            fuse_scale(t), None, ALU.mult),
                            waits=[("SP", sp)], inc=("SD", 1))
            if use_act:
                bank_free[b] = ("SA", sa)
                last_sa = sa
            else:
                bank_free[b] = ("SD", sd)
        if last_sa is not None:
            k.dve(lambda e: e.nop(), waits=[("SA", last_sa)], inc=("SD", 1))
        return k.v("SD")

    sw0, sx0 = head_dmas(0)
    head_w_ready = {0: (sw0, sx0)}

    qfm_free = [0, 0]

    def emit_front(h, ctx):
        p = h % 2
        q_fm = q_fm2[p]
        sgb = sgb2[p]
        hidb = hidb2[p]
        wn_h = wn_h2[p]
        sw_h, sx_h = head_w_ready[h]

        dst = (q_fm, k_fm, v_fm)
        qkv_ready_a = None
        for st in range(4):
            for c in range(3):
                b = c
                for kc in range(NK):
                    sp = mm(b, lambda b=b: bank[b][:, :],
                            lambda p=p, c=c, kc=kc: qkvw[p][:, c * 2048 + kc * 128: c * 2048 + (kc + 1) * 128],
                            lambda kc=kc, st=st: xn_fm[:, kc * S + st * 512: kc * S + (st + 1) * 512],
                            kc == 0, kc == NK - 1,
                            waits=[("SW", sw_h), ("SD", xn_ready[st * 4 + 3]),
                                   (bank_free[b][0], bank_free[b][1])],
                            inc=("SP", 1) if kc == NK - 1 else None)
                sa = k.act(lambda e, c=c, st=st, b=b, dst=dst: e.activation(
                    dst[c][:, st * 512:(st + 1) * 512], bank[b][:, :], AF.Silu),
                    waits=[("SP", sp), ("SP", qfm_free[p])], inc=("SA", 1))
                bank_free[b] = ("SA", sa)
                yield
        qkv_ready_a = k.v("SA")

        for src, dstt in ((q_fm, q_tm), (k_fm, k_tm), (v_fm, v_tm)):
            tm_ready = transpose_batch(
                lambda t, src=src: src[:, t * 128:(t + 1) * 128],
                lambda t0, cnt, dstt=dstt: dstt[:, t0 * 128:(t0 + cnt) * 128],
                16, 3, 7, (0, qkv_ready_a))
            yield

        sdr = None
        for src_t, base in ((q_tm, 0), (k_tm, 16)):
            sa = k.act(lambda e, src_t=src_t: e.activation(sq_scrap[:, :], src_t[:, :], AF.Square),
                       waits=[("SD", tm_ready), ("SD", sdr or 0)], inc=("SA", 1))
            for t in range(16):
                sdr = k.dve(lambda e, t=t, base=base: e.tensor_reduce(
                    nrm32[:, base + t: base + t + 1], sq_scrap[:, t * 128:(t + 1) * 128],
                    AX, ALU.add),
                    waits=[("SA", sa)], inc=("SD", 1) if t == 15 else None)
        if debug and h == 0:
            k.gps(lambda e: e.dma_start(out=nrm_dbg[:, 0:32], in_=nrm32[:, :]),
                  waits=[("SD", sdr)], inc=("SST", 16))
            k.barrier_sst()
        sa = k.act(lambda e: e.activation(nrm32[:, :], nrm32[:, :], AF.Sqrt),
                   waits=[("SD", sdr)], inc=("SA", 1))
        yield
        sa = k.act(lambda e: e.activation(nrm32[:, :], nrm32[:, :], AF.Identity,
                                          bias=cst_sb[:, 1:2]),
                   waits=[("SA", sa)], inc=("SA", 1))
        sd = k.dve(lambda e: e.reciprocal(rr32b[:, :], nrm32[:, :]),
                   waits=[("SA", sa)], inc=("SD", 1))
        sa = k.act(lambda e: e.activation(rr32[:, :], rr32b[:, :], AF.Identity),
                   waits=[("SD", sd)], inc=("SA", 1))
        for t in range(16):
            k.dve(lambda e, t=t: e.tensor_scalar(q_tm[:, t * 128:(t + 1) * 128],
                                                 q_tm[:, t * 128:(t + 1) * 128],
                                                 rr32[:, t:t + 1], None, ALU.mult),
                  waits=[("SA", sa)])
            sd = k.dve(lambda e, t=t: e.tensor_scalar(k_tm[:, t * 128:(t + 1) * 128],
                                                      k_tm[:, t * 128:(t + 1) * 128],
                                                      rr32[:, 16 + t:17 + t], None, ALU.mult),
                       inc=("SD", 1))
        if debug and h == 0:
            k.gps(lambda e: e.dma_start(out=qtm_dbg, in_=q_tm[:, :]),
                  waits=[("SD", sd)], inc=("SST", 16))
            k.gps(lambda e: e.dma_start(out=nrm_dbg[:, 32:64], in_=rr32[:, :]),
                  waits=[("SA", sa)], inc=("SST", 16))
            k.barrier_sst()
        yield
        if h < NT:
            emit_router_tile(h)
        for src, dstf in ((q_tm, q_fm), (k_tm, k_fm)):
            fm_ready = transpose_batch(
                lambda t, src=src: src[:, t * 128:(t + 1) * 128],
                lambda t0, cnt, dstf=dstf: dstf[:, t0 * 128:(t0 + cnt) * 128],
                16, 3, 7, sd)
            yield

        if debug and h == 0:
            k.gps(lambda e: e.dma_start(out=qkv_dbg[0], in_=q_fm[:, :]),
                  waits=[("SD", fm_ready)], inc=("SST", 16))
            k.gps(lambda e: e.dma_start(out=qkv_dbg[1], in_=k_fm[:, :]),
                  waits=[("SD", fm_ready)], inc=("SST", 16))
            k.gps(lambda e: e.dma_start(out=qkv_dbg[2], in_=v_fm[:, :]),
                  waits=[("SD", fm_ready)], inc=("SST", 16))
            k.barrier_sst()

        sa_wn = k.act(lambda e, p=p: e.activation(wnsq[:, :], wnat[p][:, :], AF.Square),
                      waits=[("SX", sx_h), ("SD", wnh_free[p])], inc=("SA", 1))
        sd_wn = None
        for i in range(3):
            sd_wn = k.dve(lambda e, i=i, wn_h=wn_h: e.tensor_reduce(wn_h[:, i:i + 1],
                                                         wnsq[:, i * 128:(i + 1) * 128],
                                                         AX, ALU.add),
                          waits=[("SA", sa_wn)], inc=("SD", 1) if i == 2 else None)
        sa_wn = k.act(lambda e, wn_h=wn_h: e.activation(wn_h[:, :], wn_h[:, :], AF.Sqrt),
                      waits=[("SD", sd_wn)], inc=("SA", 1))
        yield

        yield "drain"
        if h + 1 < n_heads:
            head_w_ready[h + 1] = head_dmas(h + 1)
        dw_prev = 0
        sgrp_free = 0
        for sc in range(4):
            cs = slice(sc * 512, (sc + 1) * 512)
            spg = mm(0, lambda: bank[0][:, :], lambda p=p: wT[p][:, 0:128],
                     lambda cs=cs: k_fm[:, cs], True, True,
                     waits=[("SD", fm_ready), ("SW", sw_h),
                            (bank_free[0][0], bank_free[0][1])], inc=("SP", 1))
            sph = mm(1, lambda: bank[1][:, :], lambda p=p: wT[p][:, 128:256],
                     lambda cs=cs: k_fm[:, cs], True, True,
                     waits=[(bank_free[1][0], bank_free[1][1])], inc=("SP", 1))
            spd = mm(2, lambda: bank[2][:, :], lambda p=p: w1bf[p][:, :],
                     lambda cs=cs: v_fm[:, cs], True, True,
                     waits=[(bank_free[2][0], bank_free[2][1])], inc=("SP", 1))
            sa1 = k.act(lambda e: e.activation(sgb[:, :], bank[0][:, :], AF.Silu),
                        waits=[("SP", spg), ("SD", sgrp_free)], inc=("SA", 1))
            sa2 = k.act(lambda e: e.activation(tanb[:, :], bank[0][:, :], AF.Tanh, scale=0.5),
                        inc=("SA", 1))
            sa3 = k.act(lambda e: e.activation(hbb[:, :], bank[1][:, :], AF.Copy),
                        waits=[("SP", sph)], inc=("SA", 1))
            sa4 = k.act(lambda e: e.activation(dhb[:, :], bank[2][:, :], AF.Copy),
                        waits=[("SP", spd)], inc=("SA", 1))
            bank_free[1] = ("SA", sa3)
            bank_free[2] = ("SA", sa4)
            k.dve(lambda e: e.tensor_mul(hidb[:, :], sgb[:, :], hbb[:, :]),
                  waits=[("SA", sa3), ("SP", dw_prev)])
            k.dve(lambda e: e.tensor_tensor(dgb[:, :], dhb[:, :], hbb[:, :], ALU.mult),
                  waits=[("SA", sa4)])
            k.dve(lambda e: e.tensor_mul(dbmb[:, :], dhb[:, :], sgb[:, :]))
            k.dve(lambda e: e.tensor_scalar(sigb[:, :], tanb[:, :], 0.5, 0.5, ALU.mult, ALU.add))
            k.dve(lambda e: e.tensor_scalar(tanb[:, :], tanb[:, :], -1.0, 1.0, ALU.mult, ALU.add))
            sdu = k.dve(lambda e: e.scalar_tensor_tensor(ub[:, :], bank[0][:, :], 0.5, tanb[:, :],
                                                         ALU.mult, ALU.mult), inc=("SD", 1))
            bank_free[0] = ("SD", sdu)
            k.dve(lambda e: e.scalar_tensor_tensor(sigb[:, :], ub[:, :], 1.0, sigb[:, :],
                                                   ALU.add, ALU.mult))
            sd_dga = k.dve(lambda e: e.tensor_tensor(dgab[:, :], dgb[:, :], sigb[:, :], ALU.mult),
                           inc=("SD", 1))
            sd_hid = sd_dga
            sgrp_free = sd_dga
            for j in range(4):
                t = sc * 4 + j
                k.dve(lambda e, j=j, t=t, h=h: e.tensor_scalar(
                    kl0t[:, j * 128:(j + 1) * 128], k_tm[:, t * 128:(t + 1) * 128],
                    lr_tm[:, t * 48 + h * 3: t * 48 + h * 3 + 1], None, ALU.mult),
                    waits=[("SD", lr_ready), ("SP", dw_prev)])
                sd_kl = k.dve(lambda e, j=j, t=t, h=h: e.tensor_scalar(
                    kl2t[:, j * 128:(j + 1) * 128], k_tm[:, t * 128:(t + 1) * 128],
                    lr_tm[:, t * 48 + h * 3 + 2: t * 48 + h * 3 + 3], None, ALU.mult),
                    inc=("SD", 1))
            transpose_batch(lambda t: hidb[:, (t % 4) * 128:(t % 4 + 1) * 128],
                            lambda t, _ : hl1t[:, (t % 4) * 128:(t % 4 + 1) * 128],
                            4, 3, 7, sd_hid,
                            fuse_scale=lambda t, sc=sc, h=h: lr_tm[:, (sc * 4 + t) * 48 + h * 3 + 1:
                                                                   (sc * 4 + t) * 48 + h * 3 + 2])
            transpose_batch(lambda t: dgab[:, (t % 4) * 128:(t % 4 + 1) * 128],
                            lambda t0, cnt: dgat[:, t0 * 128:(t0 + cnt) * 128],
                            4, 3, 7, sd_dga)
            sd_t = transpose_batch(lambda t: dbmb[:, (t % 4) * 128:(t % 4 + 1) * 128],
                                   lambda t0, cnt: dbmt[:, t0 * 128:(t0 + cnt) * 128],
                                   4, 3, 7, sd_dga)
            for j in range(4):
                st_flag = (sc == 0 and j == 0)
                sp_flag = (sc == 3 and j == 3)
                w0w = [("SD", max(sd_t, sd_kl)), (bank_free[4][0], bank_free[4][1])]
                mm(4, lambda: bank[4][:, 0:128],
                   lambda j=j: dgat[:, j * 128:(j + 1) * 128],
                   lambda j=j: kl0t[:, j * 128:(j + 1) * 128], st_flag, sp_flag,
                   waits=w0w, inc=("SP", 1) if sp_flag else None)
                mm(5, lambda: bank[5][:, 0:128],
                   lambda sc=sc, j=j: v_tm[:, (sc * 4 + j) * 128:(sc * 4 + j + 1) * 128],
                   lambda j=j: hl1t[:, j * 128:(j + 1) * 128], st_flag, sp_flag,
                   waits=[(bank_free[5][0], bank_free[5][1])],
                   inc=("SP", 1) if sp_flag else None)
                last = (j == 3)
                sp_dw = mm(6, lambda: bank[6][:, 0:128],
                           lambda j=j: dbmt[:, j * 128:(j + 1) * 128],
                           lambda j=j: kl2t[:, j * 128:(j + 1) * 128], st_flag, sp_flag,
                           waits=[(bank_free[6][0], bank_free[6][1])],
                           inc=("SP", 1) if last else None)
                if last:
                    dw_prev = sp_dw
            yield
        ctx.update(p=p, q_fm=q_fm, sgb=sgb, hidb=hidb, wn_h=wn_h, sp_dw=sp_dw,
                   sa_wn=sa_wn, h=h)

    def emit_back(h, ctx):
        p = ctx["p"]
        q_fm = ctx["q_fm"]
        sgb = ctx["sgb"]
        hidb = ctx["hidb"]
        wn_h = ctx["wn_h"]
        sp_dw = ctx["sp_dw"]
        sa_wn = ctx["sa_wn"]
        nsb = [(3, 4), (5, 6), (7, 3)]
        sd_x = None
        for i in range(3):
            sd_x = k.dve(lambda e, i=i: e.tensor_copy(Xb[i][:, :], bank[4 + i][:, 0:128]),
                         waits=[("SP", sp_dw)], inc=("SD", 1))
        for i in range(3):
            bank_free[4 + i] = ("SD", sd_x)
        if debug and h == 0:
            for i in range(3):
                k.gps(lambda e, i=i: e.dma_start(out=dw_dbg[i], in_=Xb[i][:, :]),
                      waits=[("SD", sd_x)], inc=("SST", 16))
                k.barrier_sst()
        sa_scale = [0, 0, 0]
        for i in range(3):
            ba = nsb[i][0]
            sa = k.act(lambda e, i=i: e.activation(sqG[:, i * 128:(i + 1) * 128], Xb[i][:, :],
                                                   AF.Square),
                       waits=[("SD", sd_x)], inc=("SA", 1))
            sd = k.dve(lambda e, i=i: e.tensor_reduce(facb[i][:, :], sqG[:, i * 128:(i + 1) * 128],
                                                      AX, ALU.add),
                       waits=[("SA", sa)], inc=("SD", 1))
            sd = k.dve(lambda e, i=i: e.tensor_copy(dwssb[i][:, :], facb[i][:, :]),
                       waits=[("SD", sd)], inc=("SD", 1))
            sp = mm(ba, lambda ba=ba: bank[ba][0:1, 0:1], lambda i=i: dwssb[i][:, :],
                    lambda: onc_sb[:, :], True, True,
                    waits=[("SD", sd), (bank_free[ba][0], bank_free[ba][1])], inc=("SP", 1))
            sa = k.act(lambda e, i=i, ba=ba: e.activation(smll[i][0:1, 0:1], bank[ba][0:1, 0:1],
                                                          AF.Sqrt),
                       waits=[("SP", sp)], inc=("SA", 1))
            sa = k.act(lambda e, i=i: e.activation(smll[i][0:1, 0:1], smll[i][0:1, 0:1],
                                                   AF.Identity, bias=cst_sb[0:1, 3:4]),
                       waits=[("SA", sa)], inc=("SA", 1))
            sd = k.dve(lambda e, i=i: e.reciprocal(smll[i][0:1, 1:2], smll[i][0:1, 0:1]),
                       waits=[("SA", sa)], inc=("SD", 1))
            sd = k.dve(lambda e, i=i: e.tensor_copy(smlb[i][0:1, 0:1], smll[i][0:1, 1:2]),
                       waits=[("SD", sd)], inc=("SD", 1))
            sp = mm(ba, lambda ba=ba: bank[ba][:, 128:129], lambda: onr_sb[:, :],
                    lambda i=i: smlb[i][0:1, 0:1], True, True,
                    waits=[("SD", sd)], inc=("SP", 1))
            sd = k.dve(lambda e, i=i, ba=ba: e.tensor_copy(facb[i][:, :], bank[ba][:, 128:129]),
                       waits=[("SP", sp)], inc=("SD", 1))
            bank_free[ba] = ("SD", sd)
            sa_scale[i] = k.act(lambda e, i=i: e.activation(Xb[i][:, :], Xb[i][:, :], AF.Identity,
                                                            scale=facb[i][:, 0:1]),
                                waits=[("SD", sd)], inc=("SA", 1))
            yield
        chain = [("SA", sa_scale[i]) for i in range(3)]
        for it, (ca, cb, cc) in enumerate(NS_COEFS):
            for i in range(3):
                ba, bb2 = nsb[i]
                sp = mm(ba, lambda ba=ba: bank[ba][:, 0:128], lambda i=i: Xb[i][:, :],
                        lambda: id_sb[:, :], True, True,
                        waits=[chain[i], (bank_free[ba][0], bank_free[ba][1])], inc=("SP", 1))
                sd = k.dve(lambda e, i=i, ba=ba: e.tensor_copy(XTb[i][:, :], bank[ba][:, 0:128]),
                           waits=[("SP", sp)], inc=("SD", 1))
                bank_free[ba] = ("SD", sd)
                yield
                sp = mm(bb2, lambda bb2=bb2: bank[bb2][:, 0:128], lambda i=i: XTb[i][:, :],
                        lambda i=i: XTb[i][:, :], True, True,
                        waits=[("SD", sd), (bank_free[bb2][0], bank_free[bb2][1])], inc=("SP", 1))
                sd = k.dve(lambda e, i=i, bb2=bb2: e.tensor_copy(Ab[i][:, :], bank[bb2][:, 0:128]),
                           waits=[("SP", sp)], inc=("SD", 1))
                bank_free[bb2] = ("SD", sd)
                yield
                sp = mm(ba, lambda ba=ba: bank[ba][:, 0:128], lambda i=i: Ab[i][:, :],
                        lambda i=i: Ab[i][:, :], True, True,
                        waits=[("SD", sd), (bank_free[ba][0], bank_free[ba][1])], inc=("SP", 1))
                k.dve(lambda e, i=i, cb=cb: e.tensor_scalar(Abf[i][:, :], Ab[i][:, :], cb, None,
                                                            ALU.mult))
                sd = k.dve(lambda e, i=i, cc=cc, ba=ba: e.scalar_tensor_tensor(
                    Bb[i][:, :], bank[ba][:, 0:128], cc, Abf[i][:, :], ALU.mult, ALU.add),
                    waits=[("SP", sp)], inc=("SD", 1))
                bank_free[ba] = ("SD", sd)
                yield
                sp = mm(bb2, lambda bb2=bb2: bank[bb2][:, 0:128], lambda i=i: Bb[i][:, :],
                        lambda i=i: Xb[i][:, :], True, True,
                        waits=[("SD", sd), (bank_free[bb2][0], bank_free[bb2][1])], inc=("SP", 1))
                sd = k.dve(lambda e, i=i, ca=ca, bb2=bb2: e.scalar_tensor_tensor(
                    Xb[i][:, :], Xb[i][:, :], ca, bank[bb2][:, 0:128], ALU.mult, ALU.add),
                    waits=[("SP", sp)], inc=("SD", 1))
                bank_free[bb2] = ("SD", sd)
                chain[i] = ("SD", sd)
                yield

        for i in range(3):
            sd = k.dve(lambda e, i=i, p=p: e.tensor_tensor(
                Wf32[:, :], wnat[p][:, i * 128:(i + 1) * 128], Xb[i][:, :], ALU.add),
                inc=("SD", 1))
            sa = k.act(lambda e, i=i: e.activation(sqG[:, i * 128:(i + 1) * 128], Wf32[:, :],
                                                   AF.Square),
                       waits=[("SD", sd)], inc=("SA", 1))
            sd = k.dve(lambda e, i=i: e.tensor_reduce(rn3[:, 0:1], sqG[:, i * 128:(i + 1) * 128],
                                                      AX, ALU.add),
                       waits=[("SA", sa)], inc=("SD", 1))
            sa = k.act(lambda e: e.activation(rn3[:, 1:2], rn3[:, 0:1], AF.Sqrt),
                       waits=[("SD", sd)], inc=("SA", 1))
            sa = k.act(lambda e: e.activation(rn3[:, 1:2], rn3[:, 1:2], AF.Identity,
                                              bias=cst_sb[:, 1:2]),
                       waits=[("SA", sa)], inc=("SA", 1))
            sd = k.dve(lambda e: e.reciprocal(rn3[:, 2:3], rn3[:, 1:2]),
                       waits=[("SA", sa)], inc=("SD", 1))
            sd = k.dve(lambda e, i=i: e.tensor_tensor(rn3[:, 2:3], rn3[:, 2:3], wn_h[:, i:i + 1],
                                                      ALU.mult),
                       waits=[("SA", sa_wn), ("SD", sd)], inc=("SD", 1))
            wnh_free[p] = sd
            sa = k.act(lambda e, i=i: e.activation(Wnb[i][:, :], Wf32[:, :], AF.Identity,
                                                   scale=rn3[:, 2:3]),
                       waits=[("SD", sd)], inc=("SA", 1))
            bh = (3, 5, 7)[i]
            sp = mm(bh, lambda bh=bh: bank[bh][:, 0:128], lambda i=i: Wnb[i][:, :],
                    lambda: id_sb[:, :],
                    True, True, waits=[("SA", sa), (bank_free[bh][0], bank_free[bh][1])],
                    inc=("SP", 1))
            sd = k.dve(lambda e, i=i, bh=bh: e.tensor_copy(WnT[i][:, :], bank[bh][:, 0:128]),
                       waits=[("SP", sp)], inc=("SD", 1))
            bank_free[bh] = ("SD", sd)
            yield
        wn_ready = k.v("SD")
        if debug and h == 0:
            for i in range(3):
                k.gps(lambda e, i=i: e.dma_start(out=wn_dbg[i], in_=Wnb[i][:, :]),
                      waits=[("SA", k.v("SA"))], inc=("SST", 16))
            k.barrier_sst()

        for st in range(4):
            cs = slice(st * 512, (st + 1) * 512)
            sp0 = mm(4, lambda: bank[4][:, :], lambda: WnT[0][:, :], lambda cs=cs, q_fm=q_fm: q_fm[:, cs],
                     True, True, waits=[("SD", wn_ready), (bank_free[4][0], bank_free[4][1])],
                     inc=("SP", 1))
            sp2 = mm(5, lambda: bank[5][:, :], lambda: WnT[2][:, :], lambda cs=cs, q_fm=q_fm: q_fm[:, cs],
                     True, True, waits=[(bank_free[5][0], bank_free[5][1])], inc=("SP", 1))
            sa = k.act(lambda e: e.activation(sgb[:, :], bank[4][:, :], AF.Silu),
                       waits=[("SP", sp0), ("SD", wn_ready)], inc=("SA", 1))
            bank_free[4] = ("SA", sa)
            sd = k.dve(lambda e: e.tensor_tensor(hidb[:, :], sgb[:, :], bank[5][:, :], ALU.mult),
                       waits=[("SA", sa), ("SP", sp2)], inc=("SD", 1))
            bank_free[5] = ("SD", sd)
            sp = mm(6, lambda: bank[6][:, :], lambda: WnT[1][:, :], lambda: hidb[:, :],
                    True, True, waits=[("SD", sd), (bank_free[6][0], bank_free[6][1])],
                    inc=("SP", 1))
            sa = k.act(lambda e, cs=cs: e.activation(o2_bf[:, cs], bank[6][:, :], AF.Square),
                       waits=[("SP", sp)], inc=("SA", 1))
            sd = k.dve(lambda e, cs=cs: e.tensor_copy(o_bf[:, cs], bank[6][:, :]),
                       waits=[("SA", sa)], inc=("SD", 1))
            bank_free[6] = ("SD", sd)
            yield
        qfm_free[p] = k.v("SP")
        o_ready = (k.v("SD"), k.v("SA"))
        if debug and h == 0:
            k.gps(lambda e: e.dma_start(out=o_dbg, in_=o_bf[:, :]),
                  waits=[("SD", o_ready[0])], inc=("SST", 16))
            k.barrier_sst()
        for t in range(16):
            sp = mm(7, lambda t=t: bank[7][:, t:t + 1],
                    lambda t=t: o2_bf[:, t * 128:(t + 1) * 128], lambda: onc_sb[:, :],
                    True, True,
                    waits=[("SA", o_ready[1]), (bank_free[7][0], bank_free[7][1])],
                    inc=("SP", 1) if t == 15 else None)
        sd = k.dve(lambda e: e.tensor_scalar(dsum[:, :], bank[7][:, 0:16], 1.0 / 128, 1e-5,
                                             ALU.mult, ALU.add),
                   waits=[("SP", sp)], inc=("SD", 1))
        bank_free[7] = ("SD", sd)
        sa = k.act(lambda e: e.activation(dsum[:, :], dsum[:, :], AF.Sqrt),
                   waits=[("SD", sd)], inc=("SA", 1))
        sd = k.dve(lambda e: e.reciprocal(rmfac[:, :], dsum[:, :]), waits=[("SA", sa)],
                   inc=("SD", 1))
        sa = k.act(lambda e: e.activation(rmfacb[:, :], rmfac[:, :], AF.Identity),
                   waits=[("SD", sd)], inc=("SA", 1))
        rm_ready = sa
        yield
        transpose_batch(lambda t: o_bf[:, t * 128:(t + 1) * 128],
                        lambda t, _: o_tm[:, t * 128:(t + 1) * 128],
                        16, 3, 7, (o_ready[0], rm_ready),
                        fuse_scale=lambda t: rmfacb[:, t:t + 1])
        yield
        sd = transpose_batch(lambda t: o_tm[:, t * 128:(t + 1) * 128],
                             lambda t0, cnt: o2_bf[:, t0 * 128:(t0 + cnt) * 128],
                             16, 3, 7, k.v("SD"))
        yield
        sx = k.syn(lambda e, h=h: e.dma_start(out=oscr_d[h], in_=o2_bf[:, :]),
                   waits=[("SD", sd)], inc=("SX", 16))
        oscr_done.append(sx)
        if debug:
            k.gps(lambda e, h=h: e.dma_start(out=oscr_dbg[h], in_=o2_bf[:, :]),
                  waits=[("SD", sd)], inc=("SST", 16))
        hw_free[p] = (k.v("SP"), k.v("SA"), k.v("SD"))

    back_it = None
    for h in range(n_heads):
        ctx = {}
        for tok in emit_front(h, ctx):
            if tok == "drain" and back_it is not None:
                for _ in back_it:
                    pass
                back_it = None
            elif back_it is not None:
                for _ in range(3):
                    if next(back_it, "_end") == "_end":
                        back_it = None
                        break
        if back_it is not None:
            for _ in back_it:
                pass
            back_it = None
        back_it = emit_back(h, ctx)
    if back_it is not None:
        for _ in back_it:
            pass
    for tt in range(n_heads, NT):
        emit_router_tile(tt)
    c_ready = k.v("SA")

    at_es.close()
    k.phase_barrier()

    op_es = contextlib.ExitStack()
    C = op_es.enter_context
    wob = [C(nc.sbuf_tensor(f"wob{i}", [128, NK * 512], BF16)) for i in range(2)]
    ostr = [C(nc.sbuf_tensor(f"ostr{i}", [128, H * 128], BF16)) for i in range(2)]
    oyb = [C(nc.sbuf_tensor(f"oyb{i}", [128, 512], F32)) for i in range(2)]
    wob_free = [0, 0]
    ostr_free = [0, 0]
    oyb_free = [0, 0]
    oyb_free_sem = [("SC", 0), ("SC", 0)]
    oscr_all = oscr_done[-1] if oscr_done else 0

    for m2t in range(4):
        pw = m2t % 2
        sw = k.gps(lambda e, m2t=m2t, pw=pw: e.dma_start(
            out=wob[pw][:, :].rearrange("p (c m) -> p c m", c=NK),
            in_=wo_d[:, m2t * 512:(m2t + 1) * 512].rearrange("(c p) m -> p c m", p=128)),
            waits=[("SP", wob_free[pw])], inc=("SW", 16))
        sd_w = k.dve(lambda e, pw=pw: e.tensor_scalar(wob[pw][:, :], wob[pw][:, :],
                                                      onw_sb[:, 0:1], None, ALU.mult),
                     waits=[("SW", sw), ("SX", sx_const)], inc=("SD", 1))
        for tt in range(NT):
            po = tt % 2
            sxo = k.syn(lambda e, tt=tt, po=po: e.dma_start(
                out=ostr[po][:, :].rearrange("p (hh t) -> p hh t", hh=H),
                in_=oscr_d[:, :, tt * 128:(tt + 1) * 128].rearrange("hh d t -> d hh t")),
                waits=[("SX", oscr_all), ("SP", ostr_free[po])], inc=("SX", 16))
            b = tt % 2
            for kc in range(H):
                sp = mm(b, lambda b=b: bank[b][:, :],
                        lambda po=po, kc=kc: ostr[po][:, kc * 128:(kc + 1) * 128],
                        lambda pw=pw, kc=kc: wob[pw][:, kc * 512:(kc + 1) * 512],
                        kc == 0, kc == H - 1,
                        waits=[("SX", sxo), ("SD", sd_w), (bank_free[b][0], bank_free[b][1])],
                        inc=("SP", 1) if kc == H - 1 else None)
            ostr_free[po] = sp
            wob_free[pw] = sp
            sd = k.dve(lambda e, b=b, po=po: e.tensor_copy(oyb[po][:, :], bank[b][:, :]),
                       waits=[("SP", sp), oyb_free_sem[po]], inc=("SD", 1))
            bank_free[b] = ("SD", sd)
            sc = k.syn(lambda e, tt=tt, m2t=m2t, po=po: e.dma_start(
                out=acc_d[tt * 128:(tt + 1) * 128, m2t * 512:(m2t + 1) * 512], in_=oyb[po][:, :]),
                waits=[("SD", sd)], inc=("SC", 16))
            oyb_free_sem[po] = ("SC", sc)
    op_es.close()
    k.phase_barrier()
    attn_acc_done = k.v("SC")

    ff_es = contextlib.ExitStack()
    C = ff_es.enter_context
    h_sb = C(nc.sbuf_tensor("h_sb", [128, NJ * S], BF16))
    wdnq = [C(nc.sbuf_tensor(f"wdnq{i}", [128, NJ * 512], BF16)) for i in range(2)]
    gpair = [C(nc.sbuf_tensor(f"gpair{i}", [128, NK * 256], BF16)) for i in range(2)]
    fsg = [C(nc.sbuf_tensor(f"fsg{i}", [128, 512], F32)) for i in range(2)]
    fsg_free = [0, 0]
    facc = [C(nc.sbuf_tensor(f"facc{i}", [128, 512], F32)) for i in range(4)]
    gpair_free = [0, 0]
    wdnq_free = [0, 0]
    facc_free = [("SC", 0)] * 4
    acc_store = [("SC", attn_acc_done)] * (4 * NT)
    h_free = 0

    def gu_dma(u, j):
        pg = j % 2
        k.gps(lambda e, u=u, j=j, pg=pg: e.dma_start(
            out=gpair[pg][:, :].rearrange("p (c m) -> p c m", c=NK),
            in_=wgu_d[u, :, j * 256:(j + 1) * 256].rearrange("(c p) m -> p c m", p=128)),
            waits=[("SP", gpair_free[pg])], inc=("SW", 16))
        return k.v("SW")

    for u in range(n_units):
        sw_g = {0: gu_dma(u, 0)}
        for j in range(NJ):
            pg = j % 2
            if j + 1 < NJ:
                sw_g[j + 1] = gu_dma(u, j + 1)
            for st in range(4):
                bg, bu = st % 2, 2 + st % 2
                for kc in range(NK):
                    spg = mm(bg, lambda bg=bg: bank[bg][:, :],
                             lambda pg=pg, kc=kc: gpair[pg][:, kc * 256: kc * 256 + 128],
                             lambda kc=kc, st=st: xn_fm[:, kc * S + st * 512: kc * S + (st + 1) * 512],
                             kc == 0, kc == NK - 1,
                             waits=[("SW", sw_g[j]), (bank_free[bg][0], bank_free[bg][1])],
                             inc=("SP", 1) if kc == NK - 1 else None)
                for kc in range(NK):
                    spu = mm(bu, lambda bu=bu: bank[bu][:, :],
                             lambda pg=pg, kc=kc: gpair[pg][:, kc * 256 + 128: kc * 256 + 256],
                             lambda kc=kc, st=st: xn_fm[:, kc * S + st * 512: kc * S + (st + 1) * 512],
                             kc == 0, kc == NK - 1,
                             waits=[(bank_free[bu][0], bank_free[bu][1])],
                             inc=("SP", 1) if kc == NK - 1 else None)
                ps = st % 2
                sa = k.act(lambda e, ps=ps, bg=bg: e.activation(fsg[ps][:, :], bank[bg][:, :], AF.Silu),
                           waits=[("SP", spg), ("SD", fsg_free[ps])], inc=("SA", 1))
                bank_free[bg] = ("SA", sa)
                sd = k.dve(lambda e, ps=ps, bu=bu, j=j, st=st: e.tensor_tensor(
                    h_sb[:, j * S + st * 512: j * S + (st + 1) * 512], fsg[ps][:, :],
                    bank[bu][:, :], ALU.mult),
                    waits=[("SA", sa), ("SP", spu), ("SP", h_free)], inc=("SD", 1))
                bank_free[bu] = ("SD", sd)
                fsg_free[ps] = sd
            gpair_free[pg] = k.v("SP")
        h_ready = k.v("SD")
        if debug and u == 0:
            k.gps(lambda e: e.dma_start(out=h_dbg, in_=h_sb[:, :]),
                  waits=[("SD", h_ready)], inc=("SST", 16))
            k.barrier_sst()
        for m2t in range(4):
            pw = (u * 4 + m2t) % 2
            sw_d = k.gps(lambda e, u=u, m2t=m2t, pw=pw: e.dma_start(
                out=wdnq[pw][:, :].rearrange("p (c m) -> p c m", c=NJ),
                in_=wdn_d[u, :, m2t * 512:(m2t + 1) * 512].rearrange("(c p) m -> p c m", p=128)),
                waits=[("SP", wdnq_free[pw])], inc=("SW", 16))
            for tt in range(NT):
                b = 4 + tt % 2
                for j in range(NJ):
                    sp = mm(b, lambda b=b: bank[b][:, :],
                            lambda j=j, tt=tt: h_sb[:, j * S + tt * 128: j * S + (tt + 1) * 128],
                            lambda j=j, pw=pw: wdnq[pw][:, j * 512:(j + 1) * 512],
                            j == 0, j == NJ - 1,
                            waits=[("SW", sw_d), ("SD", h_ready),
                                   (bank_free[b][0], bank_free[b][1])],
                            inc=("SP", 1) if j == NJ - 1 else None)
                wdnq_free[pw] = sp
                py = tt % 4
                idx = m2t * NT + tt
                ccol = (lambda tt=tt, u=u: c_all[:, tt * 17 + u: tt * 17 + u + 1]) if u < 16 \
                    else (lambda tt=tt: c_all[:, tt * 17 + 16: tt * 17 + 17])
                sxl = k.syn(lambda e, tt=tt, m2t=m2t, py=py: e.dma_start(
                    out=facc[py][:, :],
                    in_=acc_d[tt * 128:(tt + 1) * 128, m2t * 512:(m2t + 1) * 512]),
                    waits=[acc_store[idx], facc_free[py]], inc=("SX", 16))
                sd = k.dve(lambda e, b=b, py=py, ccol=ccol: e.scalar_tensor_tensor(
                    facc[py][:, :], bank[b][:, :], ccol(), facc[py][:, :], ALU.mult, ALU.add),
                    waits=[("SP", sp), ("SA", c_ready), ("SX", sxl)], inc=("SD", 1))
                bank_free[b] = ("SD", sd)
                sc = k.syn(lambda e, tt=tt, m2t=m2t, py=py: e.dma_start(
                    out=acc_d[tt * 128:(tt + 1) * 128, m2t * 512:(m2t + 1) * 512],
                    in_=facc[py][:, :]),
                    waits=[("SD", sd)], inc=("SC", 16))
                acc_store[idx] = ("SC", sc)
                facc_free[py] = ("SC", sc)
        h_free = k.v("SP")
    ff_es.close()
    k.phase_barrier()
    acc_all = k.v("SC")

    fn_es = contextlib.ExitStack()
    C = fn_es.enter_context
    fx = [C(nc.sbuf_tensor(f"fx{i}", [128, 2048], F32)) for i in range(2)]
    fa = [C(nc.sbuf_tensor(f"fa{i}", [128, 2048], F32)) for i in range(2)]
    f_free = [0, 0]
    fa_free = [0, 0]
    for tt in range(NT):
        p = tt % 2
        sx1 = k.syn(lambda e, tt=tt, p=p: e.dma_start(out=fx[p][:, :],
                                                      in_=x_d[tt * 128:(tt + 1) * 128, :]),
                    waits=[("SST", f_free[p])], inc=("SX", 16))
        sx2 = k.syn(lambda e, tt=tt, p=p: e.dma_start(out=fa[p][:, :],
                                                      in_=acc_d[tt * 128:(tt + 1) * 128, :]),
                    waits=[("SC", acc_all), ("SD", fa_free[p])], inc=("SX", 16))
        sd = k.dve(lambda e, p=p: e.tensor_add(fx[p][:, :], fx[p][:, :], fa[p][:, :]),
                   waits=[("SX", sx2)], inc=("SD", 1))
        fa_free[p] = sd
        st = k.syn(lambda e, tt=tt, p=p: e.dma_start(out=out_d[tt * 128:(tt + 1) * 128, :],
                                                     in_=fx[p][:, :]),
                   waits=[("SD", sd)], inc=("SST", 16))
        f_free[p] = st
    fn_es.close()

    sst_final = k.v("SST")
    with nc.Block() as blk:
        blk.tensor(lambda e: [f(e) for f in k.ops["tensor"]])
        blk.scalar(lambda e: [f(e) for f in k.ops["scalar"]])
        blk.vector(lambda e: [f(e) for f in k.ops["vector"]])
        blk.gpsimd(lambda e: [f(e) for f in k.ops["gpsimd"]])

        def syncf(e):
            for f in k.ops["sync"]:
                f(e)
            e.wait_ge(k.sems["SST"], sst_final)
        blk.sync(syncf)
    return nc


def prep_inputs(L0_attn_x, L0_attn_to_qkv_w, L0_attn_o_proj_w, L0_attn_lr_proj_w,
                L0_attn_w0, L0_attn_w1, L0_attn_w2, L0_attn_o_norm_w,
                L0_ffn_gate_w, L0_ffn_expert_gate_up, L0_ffn_expert_down,
                L0_ffn_shared_gate_up, L0_ffn_shared_down, L0_ffn_shared_gate_vec):
    f = np.float32
    ac = np.ascontiguousarray
    wqkv_t = ac(np.asarray(L0_attn_to_qkv_w, f).T)
    wo_t = ac(np.asarray(L0_attn_o_proj_w, f).T)
    wlr_t = ac(np.asarray(L0_attn_lr_proj_w, f).T)
    w0 = np.asarray(L0_attn_w0, f)
    w1 = np.asarray(L0_attn_w1, f)
    w2 = np.asarray(L0_attn_w2, f)
    w_nat = ac(np.stack([w0, w1, w2], 0))
    w_t = ac(np.stack([w0.transpose(0, 2, 1), w2.transpose(0, 2, 1)], 0))
    onw = ac(np.asarray(L0_attn_o_norm_w, f).reshape(128, 1))
    wgate_t = np.zeros((D, 32), f)
    wgate_t[:, 0:16] = np.asarray(L0_ffn_gate_w, f).T
    wgate_t[:, 16:17] = np.asarray(L0_ffn_shared_gate_vec, f).T
    egu = np.asarray(L0_ffn_expert_gate_up, f)
    sgu = np.asarray(L0_ffn_shared_gate_up, f)
    edn = np.asarray(L0_ffn_expert_down, f)
    sdn = np.asarray(L0_ffn_shared_down, f)
    perm = np.empty(2816, np.int64)
    for j in range(NJ):
        perm[j * 256: j * 256 + 128] = np.arange(j * 128, (j + 1) * 128)
        perm[j * 256 + 128: (j + 1) * 256] = np.arange(EI + j * 128, EI + (j + 1) * 128)
    units_gu = np.empty((NU, D, 2816), f)
    units_dn = np.empty((NU, EI, D), f)
    for e in range(16):
        units_gu[e] = egu[e][perm].T
        units_dn[e] = edn[e].T
    for j in range(4):
        blk_g = sgu[j * EI:(j + 1) * EI]
        blk_u = sgu[5632 + j * EI: 5632 + (j + 1) * EI]
        gu = np.concatenate([blk_g, blk_u], 0)
        units_gu[16 + j] = gu[perm].T
        units_dn[16 + j] = sdn[:, j * EI:(j + 1) * EI].T
    shared = {
        "wqkv_t": wqkv_t, "wo_t": wo_t, "wlr_t": wlr_t, "w_nat": w_nat,
        "w_t": w_t, "onw": onw, "wgate_t": wgate_t,
        "wgu_t": ac(units_gu), "wdn_t": ac(units_dn),
    }
    xs = np.asarray(L0_attn_x, f)
    return xs, shared


def kernel(**inputs):
    xs, shared = prep_inputs(**inputs)
    nc = build()
    in_maps = [dict(shared, x=np.ascontiguousarray(xs[b])) for b in range(8)]
    res = run_bass_kernel_spmd(nc, in_maps, core_ids=list(range(8)))
    return np.stack([r["out"] for r in res.results], axis=0)


# version 12
# speedup vs baseline: 1.1037x; 1.1037x over previous
import math
import numpy as np
import ml_dtypes
import concourse.bass as bass
import concourse.mybir as mybir
from concourse.bass_utils import run_bass_kernel_spmd

F32 = mybir.dt.float32
BF16 = mybir.dt.bfloat16
AF = mybir.ActivationFunctionType
ALU = mybir.AluOpType
AX = mybir.AxisListType.X

S = 2048
D = 2048
H = 16
HD = 128
NT = S // 128
NK = D // 128
NU = 20
EI = 1408
NJ = EI // 128
BASE_LR_INV = float(0.01 + math.log(-math.expm1(-0.01)))
NS_COEFS = [(4.0848, -6.8946, 2.927), (3.9505, -6.3029, 2.6377),
            (3.7418, -5.5913, 2.3037), (2.8769, -3.1427, 1.2046),
            (2.8366, -3.0525, 1.2012)]

ENGS = ("tensor", "scalar", "vector", "sync", "gpsimd")


class Emit:

    def __init__(self, nc):
        self.nc = nc
        self.ops = {e: [] for e in ENGS}
        self.semv = {}
        self.sems = {}
        self.hw = {e: {} for e in ENGS}

    def sem(self, name):
        self.sems[name] = self.nc.alloc_semaphore(name)
        self.semv[name] = 0

    def _op(self, eng, fn, waits, inc):
        tgt = None
        if inc is not None:
            s, n = inc
            self.semv[s] += n
            tgt = self.semv[s]
        ws = []
        for s, v in waits or ():
            if v is None or v <= 0:
                continue
            if self.hw[eng].get(s, -1) >= v:
                continue
            self.hw[eng][s] = v
            ws.append((self.sems[s], v))
        inch = (self.sems[inc[0]], inc[1]) if inc is not None else None

        def run(e, ws=ws, fn=fn, inch=inch):
            for sh, v in ws:
                e.wait_ge(sh, v)
            ins = fn(e)
            if inch is not None:
                ins.then_inc(inch[0], inch[1])
        self.ops[eng].append(run)
        return tgt

    def pe(self, fn, waits=None, inc=None):
        return self._op("tensor", fn, waits, inc)

    def act(self, fn, waits=None, inc=None):
        return self._op("scalar", fn, waits, inc)

    def dve(self, fn, waits=None, inc=None):
        return self._op("vector", fn, waits, inc)

    def syn(self, fn, waits=None, inc=None):
        return self._op("sync", fn, waits, inc)

    def gps(self, fn, waits=None, inc=None):
        return self._op("gpsimd", fn, waits, inc)

    def v(self, s):
        return self.semv[s]

    def phase_barrier(self):
        ws = [(s, self.semv[s]) for s in ("SP", "SA", "SD", "SW", "SX", "SC")]
        for eng in ENGS:
            self._op(eng, lambda e: e.nop(), list(ws), None)

    def barrier_sst(self):
        tgt = self.semv["SST"]
        for eng in ENGS:
            self._op(eng, lambda e: e.nop(), [("SST", tgt)], None)


def build(n_heads=H, n_units=NU, debug=False):
    nc = bass.Bass("TRN2", target_bir_lowering=False)

    x_d = nc.dram_tensor("x", [S, D], F32, kind="ExternalInput").ap()
    wqkv_d = nc.dram_tensor("wqkv_t", [D, 3 * D], F32, kind="ExternalInput").ap()
    wo_d = nc.dram_tensor("wo_t", [D, D], F32, kind="ExternalInput").ap()
    wlr_d = nc.dram_tensor("wlr_t", [D, 48], F32, kind="ExternalInput").ap()
    wnat_d = nc.dram_tensor("w_nat", [3, H, 128, 128], F32, kind="ExternalInput").ap()
    wt_d = nc.dram_tensor("w_t", [2, H, 128, 128], F32, kind="ExternalInput").ap()
    onw_d = nc.dram_tensor("onw", [128, 1], F32, kind="ExternalInput").ap()
    wgate_d = nc.dram_tensor("wgate_t", [D, 32], F32, kind="ExternalInput").ap()
    wgu_d = nc.dram_tensor("wgu_t", [NU, D, 2816], F32, kind="ExternalInput").ap()
    wdn_d = nc.dram_tensor("wdn_t", [NU, EI, D], F32, kind="ExternalInput").ap()
    out_d = nc.dram_tensor("out", [S, D], F32, kind="ExternalOutput").ap()

    scr_kind = "ExternalOutput" if debug else "Internal"
    acc_d = nc.dram_tensor("acc", [S, D], F32, kind=scr_kind).ap()
    oscr_d = nc.dram_tensor("o_scr", [H, 128, S], BF16, kind="Internal").ap()
    if debug:
        xn_dbg = nc.dram_tensor("xn_dbg", [NK, 128, S], F32, kind="ExternalOutput").ap()
        oscr_dbg = nc.dram_tensor("o_scr_dbg", [H, 128, S], F32, kind="ExternalOutput").ap()
        call_dbg = nc.dram_tensor("call_dbg", [128, NT * 17], F32, kind="ExternalOutput").ap()
        lr_dbg = nc.dram_tensor("lr_dbg", [128, NT * 48], F32, kind="ExternalOutput").ap()
        qkv_dbg = nc.dram_tensor("qkv_dbg", [3, 128, S], F32, kind="ExternalOutput").ap()
        e16_dbg = nc.dram_tensor("e16_dbg", [128, NT * 16], F32, kind="ExternalOutput").ap()
        rt_dbg = nc.dram_tensor("rt_dbg", [128, NT * 17], F32, kind="ExternalOutput").ap()
        wg_dbg = nc.dram_tensor("wg_dbg", [128, NK * 32], F32, kind="ExternalOutput").ap()
        nrm_dbg = nc.dram_tensor("nrm_dbg", [128, 64], F32, kind="ExternalOutput").ap()
        qtm_dbg = nc.dram_tensor("qtm_dbg", [128, S], F32, kind="ExternalOutput").ap()
        dw_dbg = nc.dram_tensor("dw_dbg", [3, 128, 128], F32, kind="ExternalOutput").ap()
        ns_dbg = nc.dram_tensor("ns_dbg", [3, 128, 128], F32, kind="ExternalOutput").ap()
        wn_dbg = nc.dram_tensor("wn_dbg", [3, 128, 128], F32, kind="ExternalOutput").ap()
        o_dbg = nc.dram_tensor("o_dbg", [128, S], F32, kind="ExternalOutput").ap()
        h_dbg = nc.dram_tensor("h_dbg", [128, NJ * S], F32, kind="ExternalOutput").ap()

    ident = nc.inline_tensor(np.eye(128, dtype=ml_dtypes.bfloat16).view(np.uint16),
                             name="ident_bf")
    identf = nc.inline_tensor(np.eye(128, dtype=np.float32), name="ident_f32")
    onesr = nc.inline_tensor(np.ones((1, 128), dtype=ml_dtypes.bfloat16).view(np.uint16),
                             name="ones_row_bf")
    onesc = nc.inline_tensor(np.ones((128, 1), dtype=ml_dtypes.bfloat16).view(np.uint16),
                             name="ones_col_bf")
    cst_np = np.tile(np.array([[1.0 / D, 1e-5, BASE_LR_INV, 1e-7]], np.float32), (128, 1))
    cst_t = nc.inline_tensor(cst_np, name="cst_f32")

    A = nc.alloc_sbuf_tensor
    xn_fm = A("xn_fm", [128, NK * S], BF16)
    c_all = A("c_all", [128, NT * 17], F32)
    lr_tm = A("lr_tm", [128, NT * 48], F32)


    wlr_sb = A("wlr_sb", [128, NK * 48], BF16)
    onw_sb = A("onw_sb", [128, 1], F32)
    id_sb = A("id_sb", [128, 128], BF16)
    idf_sb = A("idf_sb", [128, 128], F32)
    wgf_sb = A("wgf_sb", [128, NK * 32], F32)
    onr_sb = A("onr_sb", [1, 128], BF16)
    onc_sb = A("onc_sb", [128, 1], BF16)
    cst_sb = A("cst_sb", [128, 4], F32)
    rt_sb = A("rt_sb", [128, 17], F32)
    rt_all = A("rt_all", [128, NT * 17], F32)
    e16a = A("e16a", [128, 16], F32)
    e16b = A("e16b", [128, 16], F32)
    msk = A("msk", [128, 16], F32)
    nm_sb = A("nm_sb", [128, 1], F32)
    mr_sb = A("mr_sb", [128, 1], F32)
    mr2_sb = A("mr2_sb", [128, 1], F32)
    ns_sb = A("ns_sb", [128, 1], F32)
    tng = A("tng", [128, 1], F32)
    st_s = [A(f"st_s{i}", [128, 1], F32) for i in range(2)]
    st_m = [A(f"st_m{i}", [128, 1], F32) for i in range(2)]
    st_v = [A(f"st_v{i}", [128, 1], F32) for i in range(2)]
    st_r = [A(f"st_r{i}", [128, 1], F32) for i in range(2)]
    sq_scrap = A("sq_scrap", [128, 2048], F32)

    k = Emit(nc)
    for s in ("SX", "SW", "SP", "SA", "SD", "SC", "SST"):
        k.sem(s)

    bank = [nc.alloc_psum_tensor(f"bank{i}", [128, 512], F32) for i in range(8)]
    bank_free = [("SP", 0)] * 8

    def mm(b, out_ap_fn, lhsT_fn, rhs_fn, start, stop, waits=None, inc=None):
        def f(e):
            return e.matmul(out_ap_fn(), lhsT_fn(), rhs_fn(), start=start, stop=stop)
        return k.pe(f, waits=waits, inc=inc)

    k.syn(lambda e: e.dma_start(out=id_sb[:, :], in_=ident.ap().bitcast(BF16)), inc=("SX", 16))
    k.syn(lambda e: e.dma_start(out=idf_sb[:, :], in_=identf.ap()), inc=("SX", 16))
    k.syn(lambda e: e.dma_start(
        out=wgf_sb[:, :].rearrange("p (c m) -> p c m", c=NK),
        in_=wgate_d.rearrange("(c p) m -> p c m", p=128)), inc=("SX", 16))
    k.syn(lambda e: e.dma_start(out=onr_sb[:, :], in_=onesr.ap().bitcast(BF16)), inc=("SX", 16))
    k.syn(lambda e: e.dma_start(out=onc_sb[:, :], in_=onesc.ap().bitcast(BF16)), inc=("SX", 16))
    k.syn(lambda e: e.dma_start(out=onw_sb[:, :], in_=onw_d), inc=("SX", 16))
    k.syn(lambda e: e.dma_start(out=cst_sb[:, :], in_=cst_t.ap()), inc=("SX", 16))
    k.gps(lambda e: e.dma_start(
        out=wlr_sb[:, :].rearrange("p (c m) -> p c m", c=NK),
        in_=wlr_d.rearrange("(c p) m -> p c m", p=128)), inc=("SW", 16))
    sx_const = k.v("SX")
    sw_const = k.v("SW")

    import contextlib
    ln_es = contextlib.ExitStack()
    xt = [ln_es.enter_context(nc.sbuf_tensor(f"xt{i}", [128, 2048], F32)) for i in range(2)]
    xcb = [ln_es.enter_context(nc.sbuf_tensor(f"xcb{i}", [128, 2048], F32)) for i in range(2)]
    xnt = [ln_es.enter_context(nc.sbuf_tensor(f"xnt{i}", [128, 2048], BF16)) for i in range(2)]
    lnsq = [ln_es.enter_context(nc.sbuf_tensor(f"lnsq{i}", [128, 2048], F32)) for i in range(2)]
    xnf32 = ln_es.enter_context(nc.sbuf_tensor("xnf32", [128, NK * 128], F32))

    xt_free = [0, 0]
    rtf_done = 0
    rt_ready = [0] * NT
    st_m_free = 0
    xt_free_d = [0, 0]
    xcb_free = [0, 0]
    xcb_free_a = [0, 0]
    xnt_free = [0, 0]
    xn_ready = [0] * NT

    for tt in range(NT):
        p = tt % 2
        sx = k.syn(lambda e, tt=tt, p=p: e.dma_start(out=xt[p][:, :], in_=x_d[tt * 128:(tt + 1) * 128, :]),
                   waits=[("SD", xt_free[p])], inc=("SX", 16))
        sd = k.dve(lambda e, p=p: e.tensor_reduce(st_s[p][:, :], xt[p][:, :], AX, ALU.add, negate=True),
                   waits=[("SX", sx), ("SA", st_m_free)], inc=("SD", 1))
        sa = k.act(lambda e, p=p: e.activation(st_m[p][:, :], st_s[p][:, :], AF.Identity,
                                          scale=cst_sb[:, 0:1]),
                   waits=[("SD", sd), ("SX", sx_const)], inc=("SA", 1))
        sd = k.dve(lambda e, p=p: e.tensor_scalar(xcb[p][:, :], xt[p][:, :], st_m[p][:, 0:1],
                                                  None, ALU.add),
                   waits=[("SA", sa), ("SA", xcb_free_a[p])], inc=("SD", 1))
        xt_free[p] = sd
        st_m_free = k.v("SA")
        sa = k.act(lambda e, p=p: e.activation(lnsq[p][:, :], xcb[p][:, :], AF.Square),
                   waits=[("SD", sd)], inc=("SA", 1))
        sd = k.dve(lambda e, p=p: e.tensor_reduce(st_v[p][:, :], lnsq[p][:, :], AX, ALU.add),
                   waits=[("SA", sa)], inc=("SD", 1))
        sa = k.act(lambda e, p=p: e.activation(st_v[p][:, :], st_v[p][:, :], AF.Sqrt,
                                          scale=cst_sb[:, 0:1], bias=cst_sb[:, 1:2]),
                   waits=[("SD", sd)], inc=("SA", 1))
        sd = k.dve(lambda e, p=p: e.reciprocal(st_r[p][:, :], st_v[p][:, :]), waits=[("SA", sa)], inc=("SD", 1))
        sa = k.act(lambda e, p=p: e.activation(xnt[p][:, :], xcb[p][:, :], AF.Identity,
                                               scale=st_r[p][:, 0:1]),
                   waits=[("SD", sd), ("SP", xnt_free[p])], inc=("SA", 1))
        xcb_free[p] = k.v("SD")
        xcb_free_a[p] = sa
        xt_free_d[p] = sd
        last_sp = 0
        for g in range(4):
            b = g % 2
            for j in range(4):
                kc = g * 4 + j
                sp = mm(b, lambda b=b, j=j: bank[b][:, j * 128:(j + 1) * 128],
                        lambda p=p, kc=kc: xnt[p][:, kc * 128:(kc + 1) * 128],
                        lambda: id_sb[:, :], True, True,
                        waits=[("SA", sa), (bank_free[b][0], bank_free[b][1])],
                        inc=("SP", 1) if j == 3 else None)
            k.dve(lambda e, b=b, g=g, tt=tt: e.tensor_copy(
                xn_fm[:, :].rearrange("p (c s) -> p c s", c=NK)[:, g * 4:(g + 1) * 4, tt * 128:(tt + 1) * 128],
                bank[b][:, :].rearrange("p (c s) -> p c s", c=4)),
                waits=[("SP", sp), ("SP", rtf_done)], inc=("SD", 1))
            sdv = k.dve(lambda e, b=b, g=g: e.tensor_copy(
                xnf32[:, g * 512:(g + 1) * 512], bank[b][:, :]),
                inc=("SD", 1))
            bank_free[b] = ("SD", sdv)
            last_sp = sp
        xnt_free[p] = last_sp
        xn_ready[tt] = sdv
        for kc in range(NK):
            rtf_done = mm(2, lambda: bank[2][:, 0:32],
                          lambda kc=kc: xnf32[:, kc * 128:(kc + 1) * 128],
                          lambda kc=kc: wgf_sb[:, kc * 32:(kc + 1) * 32],
                          kc == 0, kc == NK - 1,
                          waits=[("SD", sdv), ("SX", sx_const),
                                 (bank_free[2][0], bank_free[2][1])],
                          inc=("SP", 1) if kc == NK - 1 else None)
        sa_rt = k.act(lambda e, tt=tt: e.activation(rt_all[:, tt * 17:(tt + 1) * 17],
                                                    bank[2][:, 0:17], AF.Identity),
                      waits=[("SP", rtf_done)], inc=("SA", 1))
        bank_free[2] = ("SA", sa_rt)
        rt_ready[tt] = sa_rt
    ln_es.close()
    k.phase_barrier()

    if debug:
        for kc in range(NK):
            k.gps(lambda e, kc=kc: e.dma_start(out=xn_dbg[kc], in_=xn_fm[:, kc * S:(kc + 1) * S]),
                  waits=[("SD", xn_ready[NT - 1])], inc=("SST", 16))

    rt_state = {"free_sd": 0}

    def emit_router_tile(tt):
        sa = k.act(lambda e, tt=tt: e.activation(rt_sb[:, :], rt_all[:, tt * 17:(tt + 1) * 17],
                                                 AF.Identity),
                   waits=[("SA", rt_ready[tt]), ("SD", rt_state["free_sd"])], inc=("SA", 1))
        sd = k.dve(lambda e: e.tensor_reduce(nm_sb[:, :], rt_sb[:, 0:16], AX, ALU.max, negate=True),
                   waits=[("SA", sa)], inc=("SD", 1))
        sa = k.act(lambda e: e.activation(e16a[:, :], rt_sb[:, 0:16], AF.Exp, bias=nm_sb[:, 0:1]),
                   waits=[("SD", sd)], inc=("SA", 1))
        sa = k.act(lambda e: e.activation(tng[:, :], rt_sb[:, 16:17], AF.Tanh, scale=0.5),
                   inc=("SA", 1))
        csl = lambda tt=tt: c_all[:, tt * 17: tt * 17 + 16]
        k.dve(lambda e, tt=tt: e.tensor_scalar(c_all[:, tt * 17 + 16: tt * 17 + 17], tng[:, :],
                                               0.5, 0.5, ALU.mult, ALU.add),
              waits=[("SA", sa)])
        k.dve(lambda e, csl=csl: e.memset(csl(), 0.0))
        ebufs = [e16a, e16b]
        for r in range(4):
            ecur = ebufs[r % 2]
            enxt = ebufs[(r + 1) % 2]
            sd = k.dve(lambda e, ecur=ecur: e.tensor_reduce(mr_sb[:, :], ecur[:, :], AX, ALU.max),
                       inc=("SD", 1))
            sa = k.act(lambda e: e.activation(mr2_sb[:, :], mr_sb[:, :], AF.Identity),
                       waits=[("SD", sd)], inc=("SA", 1))
            k.dve(lambda e, ecur=ecur: e.tensor_scalar(msk[:, :], ecur[:, :], mr2_sb[:, 0:1],
                                                       None, ALU.is_ge),
                  waits=[("SA", sa)])
            k.dve(lambda e, csl=csl: e.scalar_tensor_tensor(csl(), msk[:, :], mr2_sb[:, 0:1], csl(),
                                                            ALU.mult, ALU.add))
            if r == 0:
                sd_ns_acc = k.dve(lambda e: e.tensor_copy(ns_sb[:, :], mr2_sb[:, :]), inc=("SD", 1))
            else:
                sd_ns_acc = k.dve(lambda e: e.tensor_add(ns_sb[:, :], ns_sb[:, :], mr2_sb[:, :]),
                                  inc=("SD", 1))
            if r < 3:
                sd = k.dve(lambda e, ecur=ecur, enxt=enxt: e.scalar_tensor_tensor(
                    enxt[:, :], msk[:, :], -1e30, ecur[:, :], ALU.mult, ALU.add),
                    inc=("SD", 1))
                sa = k.act(lambda e, enxt=enxt: e.activation(enxt[:, :], enxt[:, :], AF.Identity),
                           waits=[("SD", sd)], inc=("SA", 1))
                k.dve(lambda e: e.nop(), waits=[("SA", sa)])
        sa = k.act(lambda e: e.activation(mr2_sb[:, :], ns_sb[:, :], AF.Identity),
                   waits=[("SD", sd_ns_acc)], inc=("SA", 1))
        sd = k.dve(lambda e: e.reciprocal(mr_sb[:, :], mr2_sb[:, :]),
                   waits=[("SA", sa)], inc=("SD", 1))
        k.act(lambda e, csl=csl: e.activation(csl(), csl(), AF.Identity,
                                              scale=mr_sb[:, 0:1]),
              waits=[("SD", sd)], inc=("SA", 1))
        rt_state["free_sd"] = k.v("SD")

    import contextlib as _ctx
    lr_es = _ctx.ExitStack()
    lr_fm = lr_es.enter_context(nc.sbuf_tensor("lr_fm", [48, S], F32))
    lr_scrap_free = 0
    for st in range(4):
        sp = None
        for kc in range(NK):
            sp = mm(3, lambda: bank[3][0:48, :],
                    lambda kc=kc: wlr_sb[:, kc * 48:(kc + 1) * 48],
                    lambda kc=kc, st=st: xn_fm[:, kc * S + st * 512: kc * S + (st + 1) * 512],
                    kc == 0, kc == NK - 1,
                    waits=[("SD", xn_ready[NT - 1]), ("SW", sw_const),
                           (bank_free[3][0], bank_free[3][1])],
                    inc=("SP", 1) if kc == NK - 1 else None)
        sa = k.act(lambda e: e.activation(sq_scrap[0:48, 0:512], bank[3][0:48, :], AF.Abs,
                                          bias=cst_sb[0:48, 2:3]),
                   waits=[("SP", sp), ("SD", lr_scrap_free if st > 0 else 0)], inc=("SA", 1))
        sa = k.act(lambda e: e.activation(sq_scrap[0:48, 512:1024], sq_scrap[0:48, 0:512], AF.Exp,
                                          scale=-1.0), waits=[("SA", sa)], inc=("SA", 1))
        sa = k.act(lambda e: e.activation(sq_scrap[0:48, 512:1024], sq_scrap[0:48, 512:1024], AF.Ln,
                                          bias=1.0), waits=[("SA", sa)], inc=("SA", 1))
        sd = k.dve(lambda e: e.tensor_scalar(sq_scrap[0:48, 1024:1536], bank[3][0:48, :],
                                             BASE_LR_INV, 0.0, ALU.add, ALU.max),
                   waits=[("SA", sa)], inc=("SD", 1))
        bank_free[3] = ("SD", sd)
        sd = k.dve(lambda e, st=st: e.tensor_add(lr_fm[:, st * 512:(st + 1) * 512],
                                                 sq_scrap[0:48, 1024:1536], sq_scrap[0:48, 512:1024]),
                   inc=("SD", 1))
        lr_scrap_free = sd
    lr_fm_ready = sd
    for g in range(2):
        b = 4 + g
        for j in range(8):
            tt = g * 8 + j
            sp = mm(b, lambda b=b, j=j: bank[b][:, j * 48:(j + 1) * 48],
                    lambda tt=tt: lr_fm[:, tt * 128:(tt + 1) * 128],
                    lambda: idf_sb[0:48, 0:48], True, True,
                    waits=[("SD", lr_fm_ready), ("SX", sx_const),
                           (bank_free[b][0], bank_free[b][1])],
                    inc=("SP", 1) if j == 7 else None)
        sd = k.dve(lambda e, b=b, g=g: e.tensor_copy(lr_tm[:, g * 8 * 48:(g + 1) * 8 * 48],
                                                     bank[b][:, 0:384]),
                   waits=[("SP", sp)], inc=("SD", 1))
        bank_free[b] = ("SD", sd)
    lr_ready = k.v("SD")
    lr_es.close()
    k.phase_barrier()
    if debug:
        k.syn(lambda e: e.dma_start(out=lr_dbg, in_=lr_tm[:, :]),
              waits=[("SD", lr_ready)], inc=("SST", 16))

    at_es = contextlib.ExitStack()
    C = at_es.enter_context
    qkvw = [C(nc.sbuf_tensor(f"qkvw{i}", [128, 3 * NK * 128], BF16)) for i in range(2)]
    wnat = [C(nc.sbuf_tensor(f"wnat{i}", [128, 3 * 128], F32)) for i in range(2)]
    wT = [C(nc.sbuf_tensor(f"wT{i}", [128, 2 * 128], BF16)) for i in range(2)]
    w1bf = [C(nc.sbuf_tensor(f"w1bf{i}", [128, 128], BF16)) for i in range(2)]
    q_fm2 = [C(nc.sbuf_tensor(f"q_fm{i}", [128, S], BF16)) for i in range(2)]
    k_fm = C(nc.sbuf_tensor("k_fm", [128, S], BF16))
    v_fm = C(nc.sbuf_tensor("v_fm", [128, S], BF16))
    q_tm = C(nc.sbuf_tensor("q_tm", [128, S], BF16))
    k_tm = C(nc.sbuf_tensor("k_tm", [128, S], BF16))
    v_tm = C(nc.sbuf_tensor("v_tm", [128, S], BF16))
    nrm32 = C(nc.sbuf_tensor("nrm32", [128, 32], F32))
    rr32 = C(nc.sbuf_tensor("rr32", [128, 32], F32))
    rr32b = C(nc.sbuf_tensor("rr32b", [128, 32], F32))
    tanb = C(nc.sbuf_tensor("tanb", [128, 512], F32))
    sigb = C(nc.sbuf_tensor("sigb", [128, 512], F32))
    ub = C(nc.sbuf_tensor("ub", [128, 512], F32))
    dgb = C(nc.sbuf_tensor("dgb", [128, 512], F32))
    sgb2 = [C(nc.sbuf_tensor(f"sgb{i}", [128, 512], F32)) for i in range(2)]
    hbb = C(nc.sbuf_tensor("hbb", [128, 512], F32))
    dhb = C(nc.sbuf_tensor("dhb", [128, 512], F32))
    hidb2 = [C(nc.sbuf_tensor(f"hidb{i}", [128, 512], BF16)) for i in range(2)]
    dbmb = C(nc.sbuf_tensor("dbmb", [128, 512], BF16))
    dgab = C(nc.sbuf_tensor("dgab", [128, 512], BF16))
    hl1t = C(nc.sbuf_tensor("hl1t", [128, 512], BF16))
    dgat = C(nc.sbuf_tensor("dgat", [128, 512], BF16))
    dbmt = C(nc.sbuf_tensor("dbmt", [128, 512], BF16))
    kl0t = C(nc.sbuf_tensor("kl0t", [128, 512], BF16))
    kl2t = C(nc.sbuf_tensor("kl2t", [128, 512], BF16))
    Xb = [C(nc.sbuf_tensor(f"Xb{i}", [128, 128], BF16)) for i in range(3)]
    XTb = [C(nc.sbuf_tensor(f"XTb{i}", [128, 128], BF16)) for i in range(3)]
    Ab = [C(nc.sbuf_tensor(f"Ab{i}", [128, 128], BF16)) for i in range(3)]
    Abf = [C(nc.sbuf_tensor(f"Abf{i}", [128, 128], F32)) for i in range(3)]
    Bb = [C(nc.sbuf_tensor(f"Bb{i}", [128, 128], BF16)) for i in range(3)]
    dwssb = [C(nc.sbuf_tensor(f"dwssb{i}", [128, 1], BF16)) for i in range(3)]
    facb = [C(nc.sbuf_tensor(f"facb{i}", [128, 1], F32)) for i in range(3)]
    smll = [C(nc.sbuf_tensor(f"smll{i}", [1, 8], F32)) for i in range(3)]
    smlb = [C(nc.sbuf_tensor(f"smlb{i}", [1, 1], BF16)) for i in range(3)]
    sqG = C(nc.sbuf_tensor("sqG", [128, 384], F32))
    wn_h2 = [C(nc.sbuf_tensor(f"wn_h{i}", [128, 3], F32)) for i in range(2)]
    wnsq = C(nc.sbuf_tensor("wnsq", [128, 384], F32))
    rn3 = C(nc.sbuf_tensor("rn3", [128, 3], F32))
    Wf32 = C(nc.sbuf_tensor("Wf32", [128, 128], F32))
    Wnb = [C(nc.sbuf_tensor(f"Wnb{i}", [128, 128], BF16)) for i in range(3)]
    WnT = [C(nc.sbuf_tensor(f"WnT{i}", [128, 128], BF16)) for i in range(3)]
    o_bf = C(nc.sbuf_tensor("o_bf", [128, S], BF16))
    o2_bf = C(nc.sbuf_tensor("o2_bf", [128, S], BF16))
    o_tm = C(nc.sbuf_tensor("o_tm", [128, S], BF16))
    dsum = C(nc.sbuf_tensor("dsum", [128, 16], F32))
    rmfac = C(nc.sbuf_tensor("rmfac", [128, 16], F32))
    rmfacb = C(nc.sbuf_tensor("rmfacb", [128, 16], F32))

    hw_free = [(0, 0, 0), (0, 0, 0)]
    wnh_free = [0, 0]
    oscr_done = []

    def head_dmas(h):
        p = h % 2
        w = [("SP", hw_free[p][0]), ("SA", hw_free[p][1]), ("SD", hw_free[p][2])]
        for c in range(3):
            k.gps(lambda e, c=c, h=h, p=p: e.dma_start(
                out=qkvw[p][:, c * 2048:(c + 1) * 2048].rearrange("p (c d) -> p c d", d=128),
                in_=wqkv_d[:, c * 2048 + h * 128: c * 2048 + (h + 1) * 128]
                .rearrange("(c p) d -> p c d", p=128)), waits=w, inc=("SW", 16))
        k.gps(lambda e, h=h, p=p: e.dma_start(
            out=wT[p][:, :].rearrange("p (i d) -> p i d", i=2),
            in_=wt_d[:, h].rearrange("i p d -> p i d")), waits=w, inc=("SW", 16))
        k.gps(lambda e, h=h, p=p: e.dma_start(out=w1bf[p][:, :], in_=wnat_d[1, h]),
              waits=w, inc=("SW", 16))
        k.syn(lambda e, h=h, p=p: e.dma_start(
            out=wnat[p][:, :].rearrange("p (i d) -> p i d", i=3),
            in_=wnat_d[:, h].rearrange("i p d -> p i d")), waits=w, inc=("SX", 16))
        return k.v("SW"), k.v("SX")

    def transpose_batch(src_fn, dst_fn, n_tiles, b0, b1, src_ready, fuse_scale=None,
                        dst_dtype_bf=True):
        nb = (n_tiles + 3) // 4
        last_sa = None
        for g in range(nb):
            b = (b0, b1)[g % 2]
            cnt = min(4, n_tiles - g * 4)
            for j in range(cnt):
                t = g * 4 + j
                sp = mm(b, lambda b=b, j=j: bank[b][:, j * 128:(j + 1) * 128],
                        lambda t=t: src_fn(t), lambda: id_sb[:, :], True, True,
                        waits=[("SD", src_ready[0] if isinstance(src_ready, tuple) else src_ready),
                               (bank_free[b][0], bank_free[b][1])] +
                              ([("SA", src_ready[1])] if isinstance(src_ready, tuple) else []),
                        inc=("SP", 1) if j == cnt - 1 else None)
            use_act = (nb >= 2 and g % 2 == 1)
            if fuse_scale is None:
                if use_act:
                    sa = k.act(lambda e, b=b, g=g, cnt=cnt: e.activation(
                        dst_fn(g * 4, cnt), bank[b][:, 0:cnt * 128], AF.Identity),
                        waits=[("SP", sp)], inc=("SA", 1))
                else:
                    sd = k.dve(lambda e, b=b, g=g, cnt=cnt: e.tensor_copy(
                        dst_fn(g * 4, cnt), bank[b][:, 0:cnt * 128]),
                        waits=[("SP", sp)], inc=("SD", 1))
            else:
                for j in range(cnt):
                    t = g * 4 + j
                    if use_act:
                        sa = k.act(lambda e, b=b, j=j, t=t: e.activation(
                            dst_fn(t, None), bank[b][:, j * 128:(j + 1) * 128],
                            AF.Identity, scale=fuse_scale(t)),
                            waits=[("SP", sp)], inc=("SA", 1))
                    else:
                        sd = k.dve(lambda e, b=b, j=j, t=t: e.tensor_scalar(
                            dst_fn(t, None), bank[b][:, j * 128:(j + 1) * 128],
                            fuse_scale(t), None, ALU.mult),
                            waits=[("SP", sp)], inc=("SD", 1))
            if use_act:
                bank_free[b] = ("SA", sa)
                last_sa = sa
            else:
                bank_free[b] = ("SD", sd)
        if last_sa is not None:
            k.dve(lambda e: e.nop(), waits=[("SA", last_sa)], inc=("SD", 1))
        return k.v("SD")

    sw0, sx0 = head_dmas(0)
    head_w_ready = {0: (sw0, sx0)}

    qfm_free = [0, 0]

    def emit_front(h, ctx):
        p = h % 2
        q_fm = q_fm2[p]
        sgb = sgb2[p]
        hidb = hidb2[p]
        wn_h = wn_h2[p]
        sw_h, sx_h = head_w_ready[h]

        dst = (q_fm, k_fm, v_fm)
        qkv_ready_a = None
        for st in range(4):
            for c in range(3):
                b = c
                for kc in range(NK):
                    sp = mm(b, lambda b=b: bank[b][:, :],
                            lambda p=p, c=c, kc=kc: qkvw[p][:, c * 2048 + kc * 128: c * 2048 + (kc + 1) * 128],
                            lambda kc=kc, st=st: xn_fm[:, kc * S + st * 512: kc * S + (st + 1) * 512],
                            kc == 0, kc == NK - 1,
                            waits=[("SW", sw_h), ("SD", xn_ready[st * 4 + 3]),
                                   (bank_free[b][0], bank_free[b][1])],
                            inc=("SP", 1) if kc == NK - 1 else None)
                sa = k.act(lambda e, c=c, st=st, b=b, dst=dst: e.activation(
                    dst[c][:, st * 512:(st + 1) * 512], bank[b][:, :], AF.Silu),
                    waits=[("SP", sp), ("SP", qfm_free[p])], inc=("SA", 1))
                bank_free[b] = ("SA", sa)
                yield
        qkv_ready_a = k.v("SA")

        for src, dstt in ((q_fm, q_tm), (k_fm, k_tm), (v_fm, v_tm)):
            tm_ready = transpose_batch(
                lambda t, src=src: src[:, t * 128:(t + 1) * 128],
                lambda t0, cnt, dstt=dstt: dstt[:, t0 * 128:(t0 + cnt) * 128],
                16, 3, 7, (0, qkv_ready_a))
            yield

        sdr = None
        for src_t, base in ((q_tm, 0), (k_tm, 16)):
            sa = k.act(lambda e, src_t=src_t: e.activation(sq_scrap[:, :], src_t[:, :], AF.Square),
                       waits=[("SD", tm_ready), ("SD", sdr or 0)], inc=("SA", 1))
            for t in range(16):
                sdr = k.dve(lambda e, t=t, base=base: e.tensor_reduce(
                    nrm32[:, base + t: base + t + 1], sq_scrap[:, t * 128:(t + 1) * 128],
                    AX, ALU.add),
                    waits=[("SA", sa)], inc=("SD", 1) if t == 15 else None)
        if debug and h == 0:
            k.gps(lambda e: e.dma_start(out=nrm_dbg[:, 0:32], in_=nrm32[:, :]),
                  waits=[("SD", sdr)], inc=("SST", 16))
            k.barrier_sst()
        sa = k.act(lambda e: e.activation(nrm32[:, :], nrm32[:, :], AF.Sqrt),
                   waits=[("SD", sdr)], inc=("SA", 1))
        yield
        sa = k.act(lambda e: e.activation(nrm32[:, :], nrm32[:, :], AF.Identity,
                                          bias=cst_sb[:, 1:2]),
                   waits=[("SA", sa)], inc=("SA", 1))
        sd = k.dve(lambda e: e.reciprocal(rr32b[:, :], nrm32[:, :]),
                   waits=[("SA", sa)], inc=("SD", 1))
        sa = k.act(lambda e: e.activation(rr32[:, :], rr32b[:, :], AF.Identity),
                   waits=[("SD", sd)], inc=("SA", 1))
        for t in range(16):
            k.dve(lambda e, t=t: e.tensor_scalar(q_tm[:, t * 128:(t + 1) * 128],
                                                 q_tm[:, t * 128:(t + 1) * 128],
                                                 rr32[:, t:t + 1], None, ALU.mult),
                  waits=[("SA", sa)])
            sd = k.dve(lambda e, t=t: e.tensor_scalar(k_tm[:, t * 128:(t + 1) * 128],
                                                      k_tm[:, t * 128:(t + 1) * 128],
                                                      rr32[:, 16 + t:17 + t], None, ALU.mult),
                       inc=("SD", 1))
        if debug and h == 0:
            k.gps(lambda e: e.dma_start(out=qtm_dbg, in_=q_tm[:, :]),
                  waits=[("SD", sd)], inc=("SST", 16))
            k.gps(lambda e: e.dma_start(out=nrm_dbg[:, 32:64], in_=rr32[:, :]),
                  waits=[("SA", sa)], inc=("SST", 16))
            k.barrier_sst()
        yield
        if h < NT:
            emit_router_tile(h)
        for src, dstf in ((q_tm, q_fm), (k_tm, k_fm)):
            fm_ready = transpose_batch(
                lambda t, src=src: src[:, t * 128:(t + 1) * 128],
                lambda t0, cnt, dstf=dstf: dstf[:, t0 * 128:(t0 + cnt) * 128],
                16, 3, 7, sd)
            yield

        if debug and h == 0:
            k.gps(lambda e: e.dma_start(out=qkv_dbg[0], in_=q_fm[:, :]),
                  waits=[("SD", fm_ready)], inc=("SST", 16))
            k.gps(lambda e: e.dma_start(out=qkv_dbg[1], in_=k_fm[:, :]),
                  waits=[("SD", fm_ready)], inc=("SST", 16))
            k.gps(lambda e: e.dma_start(out=qkv_dbg[2], in_=v_fm[:, :]),
                  waits=[("SD", fm_ready)], inc=("SST", 16))
            k.barrier_sst()

        sa_wn = k.act(lambda e, p=p: e.activation(wnsq[:, :], wnat[p][:, :], AF.Square),
                      waits=[("SX", sx_h), ("SD", wnh_free[p])], inc=("SA", 1))
        sd_wn = None
        for i in range(3):
            sd_wn = k.dve(lambda e, i=i, wn_h=wn_h: e.tensor_reduce(wn_h[:, i:i + 1],
                                                         wnsq[:, i * 128:(i + 1) * 128],
                                                         AX, ALU.add),
                          waits=[("SA", sa_wn)], inc=("SD", 1) if i == 2 else None)
        sa_wn = k.act(lambda e, wn_h=wn_h: e.activation(wn_h[:, :], wn_h[:, :], AF.Sqrt),
                      waits=[("SD", sd_wn)], inc=("SA", 1))
        yield

        yield "drain"
        if h + 1 < n_heads:
            head_w_ready[h + 1] = head_dmas(h + 1)
        dw_prev = 0
        sgrp_free = 0
        for sc in range(4):
            cs = slice(sc * 512, (sc + 1) * 512)
            spg = mm(0, lambda: bank[0][:, :], lambda p=p: wT[p][:, 0:128],
                     lambda cs=cs: k_fm[:, cs], True, True,
                     waits=[("SD", fm_ready), ("SW", sw_h),
                            (bank_free[0][0], bank_free[0][1])], inc=("SP", 1))
            sph = mm(1, lambda: bank[1][:, :], lambda p=p: wT[p][:, 128:256],
                     lambda cs=cs: k_fm[:, cs], True, True,
                     waits=[(bank_free[1][0], bank_free[1][1])], inc=("SP", 1))
            spd = mm(2, lambda: bank[2][:, :], lambda p=p: w1bf[p][:, :],
                     lambda cs=cs: v_fm[:, cs], True, True,
                     waits=[(bank_free[2][0], bank_free[2][1])], inc=("SP", 1))
            sa1 = k.act(lambda e: e.activation(sgb[:, :], bank[0][:, :], AF.Silu),
                        waits=[("SP", spg), ("SD", sgrp_free)], inc=("SA", 1))
            sa2 = k.act(lambda e: e.activation(tanb[:, :], bank[0][:, :], AF.Tanh, scale=0.5),
                        inc=("SA", 1))
            sa3 = k.act(lambda e: e.activation(hbb[:, :], bank[1][:, :], AF.Copy),
                        waits=[("SP", sph)], inc=("SA", 1))
            sa4 = k.act(lambda e: e.activation(dhb[:, :], bank[2][:, :], AF.Copy),
                        waits=[("SP", spd)], inc=("SA", 1))
            bank_free[1] = ("SA", sa3)
            bank_free[2] = ("SA", sa4)
            k.dve(lambda e: e.tensor_mul(hidb[:, :], sgb[:, :], hbb[:, :]),
                  waits=[("SA", sa3), ("SP", dw_prev)])
            k.dve(lambda e: e.tensor_tensor(dgb[:, :], dhb[:, :], hbb[:, :], ALU.mult),
                  waits=[("SA", sa4)])
            k.dve(lambda e: e.tensor_mul(dbmb[:, :], dhb[:, :], sgb[:, :]))
            k.dve(lambda e: e.tensor_scalar(sigb[:, :], tanb[:, :], 0.5, 0.5, ALU.mult, ALU.add))
            k.dve(lambda e: e.tensor_scalar(tanb[:, :], tanb[:, :], -1.0, 1.0, ALU.mult, ALU.add))
            sdu = k.dve(lambda e: e.scalar_tensor_tensor(ub[:, :], bank[0][:, :], 0.5, tanb[:, :],
                                                         ALU.mult, ALU.mult), inc=("SD", 1))
            bank_free[0] = ("SD", sdu)
            k.dve(lambda e: e.scalar_tensor_tensor(sigb[:, :], ub[:, :], 1.0, sigb[:, :],
                                                   ALU.add, ALU.mult))
            sd_dga = k.dve(lambda e: e.tensor_tensor(dgab[:, :], dgb[:, :], sigb[:, :], ALU.mult),
                           inc=("SD", 1))
            sd_hid = sd_dga
            sgrp_free = sd_dga
            for j in range(4):
                t = sc * 4 + j
                k.dve(lambda e, j=j, t=t, h=h: e.tensor_scalar(
                    kl0t[:, j * 128:(j + 1) * 128], k_tm[:, t * 128:(t + 1) * 128],
                    lr_tm[:, t * 48 + h * 3: t * 48 + h * 3 + 1], None, ALU.mult),
                    waits=[("SD", lr_ready), ("SP", dw_prev)])
                sd_kl = k.dve(lambda e, j=j, t=t, h=h: e.tensor_scalar(
                    kl2t[:, j * 128:(j + 1) * 128], k_tm[:, t * 128:(t + 1) * 128],
                    lr_tm[:, t * 48 + h * 3 + 2: t * 48 + h * 3 + 3], None, ALU.mult),
                    inc=("SD", 1))
            transpose_batch(lambda t: hidb[:, (t % 4) * 128:(t % 4 + 1) * 128],
                            lambda t, _ : hl1t[:, (t % 4) * 128:(t % 4 + 1) * 128],
                            4, 3, 7, sd_hid,
                            fuse_scale=lambda t, sc=sc, h=h: lr_tm[:, (sc * 4 + t) * 48 + h * 3 + 1:
                                                                   (sc * 4 + t) * 48 + h * 3 + 2])
            transpose_batch(lambda t: dgab[:, (t % 4) * 128:(t % 4 + 1) * 128],
                            lambda t0, cnt: dgat[:, t0 * 128:(t0 + cnt) * 128],
                            4, 3, 7, sd_dga)
            sd_t = transpose_batch(lambda t: dbmb[:, (t % 4) * 128:(t % 4 + 1) * 128],
                                   lambda t0, cnt: dbmt[:, t0 * 128:(t0 + cnt) * 128],
                                   4, 3, 7, sd_dga)
            for j in range(4):
                st_flag = (sc == 0 and j == 0)
                sp_flag = (sc == 3 and j == 3)
                w0w = [("SD", max(sd_t, sd_kl)), (bank_free[4][0], bank_free[4][1])]
                mm(4, lambda: bank[4][:, 0:128],
                   lambda j=j: dgat[:, j * 128:(j + 1) * 128],
                   lambda j=j: kl0t[:, j * 128:(j + 1) * 128], st_flag, sp_flag,
                   waits=w0w, inc=("SP", 1) if sp_flag else None)
                mm(5, lambda: bank[5][:, 0:128],
                   lambda sc=sc, j=j: v_tm[:, (sc * 4 + j) * 128:(sc * 4 + j + 1) * 128],
                   lambda j=j: hl1t[:, j * 128:(j + 1) * 128], st_flag, sp_flag,
                   waits=[(bank_free[5][0], bank_free[5][1])],
                   inc=("SP", 1) if sp_flag else None)
                last = (j == 3)
                sp_dw = mm(6, lambda: bank[6][:, 0:128],
                           lambda j=j: dbmt[:, j * 128:(j + 1) * 128],
                           lambda j=j: kl2t[:, j * 128:(j + 1) * 128], st_flag, sp_flag,
                           waits=[(bank_free[6][0], bank_free[6][1])],
                           inc=("SP", 1) if last else None)
                if last:
                    dw_prev = sp_dw
            yield
        ctx.update(p=p, q_fm=q_fm, sgb=sgb, hidb=hidb, wn_h=wn_h, sp_dw=sp_dw,
                   sa_wn=sa_wn, h=h)

    def emit_back(h, ctx):
        p = ctx["p"]
        q_fm = ctx["q_fm"]
        sgb = ctx["sgb"]
        hidb = ctx["hidb"]
        wn_h = ctx["wn_h"]
        sp_dw = ctx["sp_dw"]
        sa_wn = ctx["sa_wn"]
        nsb = [(3, 4), (5, 6), (7, 3)]
        sd_x = None
        for i in range(3):
            sd_x = k.dve(lambda e, i=i: e.tensor_copy(Xb[i][:, :], bank[4 + i][:, 0:128]),
                         waits=[("SP", sp_dw)], inc=("SD", 1))
        for i in range(3):
            bank_free[4 + i] = ("SD", sd_x)
        if debug and h == 0:
            for i in range(3):
                k.gps(lambda e, i=i: e.dma_start(out=dw_dbg[i], in_=Xb[i][:, :]),
                      waits=[("SD", sd_x)], inc=("SST", 16))
                k.barrier_sst()
        sa_scale = [0, 0, 0]
        for i in range(3):
            ba = nsb[i][0]
            sa = k.act(lambda e, i=i: e.activation(sqG[:, i * 128:(i + 1) * 128], Xb[i][:, :],
                                                   AF.Square),
                       waits=[("SD", sd_x)], inc=("SA", 1))
            sd = k.dve(lambda e, i=i: e.tensor_reduce(facb[i][:, :], sqG[:, i * 128:(i + 1) * 128],
                                                      AX, ALU.add),
                       waits=[("SA", sa)], inc=("SD", 1))
            sd = k.dve(lambda e, i=i: e.tensor_copy(dwssb[i][:, :], facb[i][:, :]),
                       waits=[("SD", sd)], inc=("SD", 1))
            sp = mm(ba, lambda ba=ba: bank[ba][0:1, 0:1], lambda i=i: dwssb[i][:, :],
                    lambda: onc_sb[:, :], True, True,
                    waits=[("SD", sd), (bank_free[ba][0], bank_free[ba][1])], inc=("SP", 1))
            sa = k.act(lambda e, i=i, ba=ba: e.activation(smll[i][0:1, 0:1], bank[ba][0:1, 0:1],
                                                          AF.Sqrt),
                       waits=[("SP", sp)], inc=("SA", 1))
            sa = k.act(lambda e, i=i: e.activation(smll[i][0:1, 0:1], smll[i][0:1, 0:1],
                                                   AF.Identity, bias=cst_sb[0:1, 3:4]),
                       waits=[("SA", sa)], inc=("SA", 1))
            sd = k.dve(lambda e, i=i: e.reciprocal(smll[i][0:1, 1:2], smll[i][0:1, 0:1]),
                       waits=[("SA", sa)], inc=("SD", 1))
            sd = k.dve(lambda e, i=i: e.tensor_copy(smlb[i][0:1, 0:1], smll[i][0:1, 1:2]),
                       waits=[("SD", sd)], inc=("SD", 1))
            sp = mm(ba, lambda ba=ba: bank[ba][:, 128:129], lambda: onr_sb[:, :],
                    lambda i=i: smlb[i][0:1, 0:1], True, True,
                    waits=[("SD", sd)], inc=("SP", 1))
            sd = k.dve(lambda e, i=i, ba=ba: e.tensor_copy(facb[i][:, :], bank[ba][:, 128:129]),
                       waits=[("SP", sp)], inc=("SD", 1))
            bank_free[ba] = ("SD", sd)
            sa_scale[i] = k.act(lambda e, i=i: e.activation(Xb[i][:, :], Xb[i][:, :], AF.Identity,
                                                            scale=facb[i][:, 0:1]),
                                waits=[("SD", sd)], inc=("SA", 1))
            yield
        chain = [("SA", sa_scale[i]) for i in range(3)]
        for it, (ca, cb, cc) in enumerate(NS_COEFS):
            for i in range(3):
                ba, bb2 = nsb[i]
                sp = mm(ba, lambda ba=ba: bank[ba][:, 0:128], lambda i=i: Xb[i][:, :],
                        lambda: id_sb[:, :], True, True,
                        waits=[chain[i], (bank_free[ba][0], bank_free[ba][1])], inc=("SP", 1))
                sd = k.dve(lambda e, i=i, ba=ba: e.tensor_copy(XTb[i][:, :], bank[ba][:, 0:128]),
                           waits=[("SP", sp)], inc=("SD", 1))
                bank_free[ba] = ("SD", sd)
                yield
                sp = mm(bb2, lambda bb2=bb2: bank[bb2][:, 0:128], lambda i=i: XTb[i][:, :],
                        lambda i=i: XTb[i][:, :], True, True,
                        waits=[("SD", sd), (bank_free[bb2][0], bank_free[bb2][1])], inc=("SP", 1))
                sd = k.dve(lambda e, i=i, bb2=bb2: e.tensor_copy(Ab[i][:, :], bank[bb2][:, 0:128]),
                           waits=[("SP", sp)], inc=("SD", 1))
                bank_free[bb2] = ("SD", sd)
                yield
                sp = mm(ba, lambda ba=ba: bank[ba][:, 0:128], lambda i=i: Ab[i][:, :],
                        lambda i=i: Ab[i][:, :], True, True,
                        waits=[("SD", sd), (bank_free[ba][0], bank_free[ba][1])], inc=("SP", 1))
                k.dve(lambda e, i=i, cb=cb: e.tensor_scalar(Abf[i][:, :], Ab[i][:, :], cb, None,
                                                            ALU.mult))
                sd = k.dve(lambda e, i=i, cc=cc, ba=ba: e.scalar_tensor_tensor(
                    Bb[i][:, :], bank[ba][:, 0:128], cc, Abf[i][:, :], ALU.mult, ALU.add),
                    waits=[("SP", sp)], inc=("SD", 1))
                bank_free[ba] = ("SD", sd)
                yield
                sp = mm(bb2, lambda bb2=bb2: bank[bb2][:, 0:128], lambda i=i: Bb[i][:, :],
                        lambda i=i: Xb[i][:, :], True, True,
                        waits=[("SD", sd), (bank_free[bb2][0], bank_free[bb2][1])], inc=("SP", 1))
                sd = k.dve(lambda e, i=i, ca=ca, bb2=bb2: e.scalar_tensor_tensor(
                    Xb[i][:, :], Xb[i][:, :], ca, bank[bb2][:, 0:128], ALU.mult, ALU.add),
                    waits=[("SP", sp)], inc=("SD", 1))
                bank_free[bb2] = ("SD", sd)
                chain[i] = ("SD", sd)
                yield

        for i in range(3):
            sd = k.dve(lambda e, i=i, p=p: e.tensor_tensor(
                Wf32[:, :], wnat[p][:, i * 128:(i + 1) * 128], Xb[i][:, :], ALU.add),
                inc=("SD", 1))
            sa = k.act(lambda e, i=i: e.activation(sqG[:, i * 128:(i + 1) * 128], Wf32[:, :],
                                                   AF.Square),
                       waits=[("SD", sd)], inc=("SA", 1))
            sd = k.dve(lambda e, i=i: e.tensor_reduce(rn3[:, 0:1], sqG[:, i * 128:(i + 1) * 128],
                                                      AX, ALU.add),
                       waits=[("SA", sa)], inc=("SD", 1))
            sa = k.act(lambda e: e.activation(rn3[:, 1:2], rn3[:, 0:1], AF.Sqrt),
                       waits=[("SD", sd)], inc=("SA", 1))
            sa = k.act(lambda e: e.activation(rn3[:, 1:2], rn3[:, 1:2], AF.Identity,
                                              bias=cst_sb[:, 1:2]),
                       waits=[("SA", sa)], inc=("SA", 1))
            sd = k.dve(lambda e: e.reciprocal(rn3[:, 2:3], rn3[:, 1:2]),
                       waits=[("SA", sa)], inc=("SD", 1))
            sd = k.dve(lambda e, i=i: e.tensor_tensor(rn3[:, 2:3], rn3[:, 2:3], wn_h[:, i:i + 1],
                                                      ALU.mult),
                       waits=[("SA", sa_wn), ("SD", sd)], inc=("SD", 1))
            wnh_free[p] = sd
            sa = k.act(lambda e, i=i: e.activation(Wnb[i][:, :], Wf32[:, :], AF.Identity,
                                                   scale=rn3[:, 2:3]),
                       waits=[("SD", sd)], inc=("SA", 1))
            bh = (3, 5, 7)[i]
            sp = mm(bh, lambda bh=bh: bank[bh][:, 0:128], lambda i=i: Wnb[i][:, :],
                    lambda: id_sb[:, :],
                    True, True, waits=[("SA", sa), (bank_free[bh][0], bank_free[bh][1])],
                    inc=("SP", 1))
            sd = k.dve(lambda e, i=i, bh=bh: e.tensor_copy(WnT[i][:, :], bank[bh][:, 0:128]),
                       waits=[("SP", sp)], inc=("SD", 1))
            bank_free[bh] = ("SD", sd)
            yield
        wn_ready = k.v("SD")
        if debug and h == 0:
            for i in range(3):
                k.gps(lambda e, i=i: e.dma_start(out=wn_dbg[i], in_=Wnb[i][:, :]),
                      waits=[("SA", k.v("SA"))], inc=("SST", 16))
            k.barrier_sst()

        for st in range(4):
            cs = slice(st * 512, (st + 1) * 512)
            sp0 = mm(4, lambda: bank[4][:, :], lambda: WnT[0][:, :], lambda cs=cs, q_fm=q_fm: q_fm[:, cs],
                     True, True, waits=[("SD", wn_ready), (bank_free[4][0], bank_free[4][1])],
                     inc=("SP", 1))
            sp2 = mm(5, lambda: bank[5][:, :], lambda: WnT[2][:, :], lambda cs=cs, q_fm=q_fm: q_fm[:, cs],
                     True, True, waits=[(bank_free[5][0], bank_free[5][1])], inc=("SP", 1))
            sa = k.act(lambda e: e.activation(sgb[:, :], bank[4][:, :], AF.Silu),
                       waits=[("SP", sp0), ("SD", wn_ready)], inc=("SA", 1))
            bank_free[4] = ("SA", sa)
            sd = k.dve(lambda e: e.tensor_tensor(hidb[:, :], sgb[:, :], bank[5][:, :], ALU.mult),
                       waits=[("SA", sa), ("SP", sp2)], inc=("SD", 1))
            bank_free[5] = ("SD", sd)
            sp = mm(6, lambda: bank[6][:, :], lambda: WnT[1][:, :], lambda: hidb[:, :],
                    True, True, waits=[("SD", sd), (bank_free[6][0], bank_free[6][1])],
                    inc=("SP", 1))
            sa = k.act(lambda e, cs=cs: e.activation(o2_bf[:, cs], bank[6][:, :], AF.Square),
                       waits=[("SP", sp)], inc=("SA", 1))
            sd = k.dve(lambda e, cs=cs: e.tensor_copy(o_bf[:, cs], bank[6][:, :]),
                       waits=[("SA", sa)], inc=("SD", 1))
            bank_free[6] = ("SD", sd)
            yield
        qfm_free[p] = k.v("SP")
        o_ready = (k.v("SD"), k.v("SA"))
        if debug and h == 0:
            k.gps(lambda e: e.dma_start(out=o_dbg, in_=o_bf[:, :]),
                  waits=[("SD", o_ready[0])], inc=("SST", 16))
            k.barrier_sst()
        for t in range(16):
            sp = mm(7, lambda t=t: bank[7][:, t:t + 1],
                    lambda t=t: o2_bf[:, t * 128:(t + 1) * 128], lambda: onc_sb[:, :],
                    True, True,
                    waits=[("SA", o_ready[1]), (bank_free[7][0], bank_free[7][1])],
                    inc=("SP", 1) if t == 15 else None)
        sd = k.dve(lambda e: e.tensor_scalar(dsum[:, :], bank[7][:, 0:16], 1.0 / 128, 1e-5,
                                             ALU.mult, ALU.add),
                   waits=[("SP", sp)], inc=("SD", 1))
        bank_free[7] = ("SD", sd)
        sa = k.act(lambda e: e.activation(dsum[:, :], dsum[:, :], AF.Sqrt),
                   waits=[("SD", sd)], inc=("SA", 1))
        sd = k.dve(lambda e: e.reciprocal(rmfac[:, :], dsum[:, :]), waits=[("SA", sa)],
                   inc=("SD", 1))
        sa = k.act(lambda e: e.activation(rmfacb[:, :], rmfac[:, :], AF.Identity),
                   waits=[("SD", sd)], inc=("SA", 1))
        rm_ready = sa
        yield
        transpose_batch(lambda t: o_bf[:, t * 128:(t + 1) * 128],
                        lambda t, _: o_tm[:, t * 128:(t + 1) * 128],
                        16, 3, 7, (o_ready[0], rm_ready),
                        fuse_scale=lambda t: rmfacb[:, t:t + 1])
        yield
        sd = transpose_batch(lambda t: o_tm[:, t * 128:(t + 1) * 128],
                             lambda t0, cnt: o2_bf[:, t0 * 128:(t0 + cnt) * 128],
                             16, 3, 7, k.v("SD"))
        yield
        sx = k.syn(lambda e, h=h: e.dma_start(out=oscr_d[h], in_=o2_bf[:, :]),
                   waits=[("SD", sd)], inc=("SX", 16))
        oscr_done.append(sx)
        if debug:
            k.gps(lambda e, h=h: e.dma_start(out=oscr_dbg[h], in_=o2_bf[:, :]),
                  waits=[("SD", sd)], inc=("SST", 16))
        hw_free[p] = (k.v("SP"), k.v("SA"), k.v("SD"))

    back_it = None
    for h in range(n_heads):
        ctx = {}
        for tok in emit_front(h, ctx):
            if tok == "drain" and back_it is not None:
                for _ in back_it:
                    pass
                back_it = None
            elif back_it is not None:
                for _ in range(8):
                    if next(back_it, "_end") == "_end":
                        back_it = None
                        break
        if back_it is not None:
            for _ in back_it:
                pass
            back_it = None
        back_it = emit_back(h, ctx)
    if back_it is not None:
        for _ in back_it:
            pass
    for tt in range(n_heads, NT):
        emit_router_tile(tt)
    c_ready = k.v("SA")

    at_es.close()
    k.phase_barrier()

    op_es = contextlib.ExitStack()
    C = op_es.enter_context
    wob = [C(nc.sbuf_tensor(f"wob{i}", [128, NK * 512], BF16)) for i in range(2)]
    ostr = [C(nc.sbuf_tensor(f"ostr{i}", [128, H * 128], BF16)) for i in range(2)]
    oyb = [C(nc.sbuf_tensor(f"oyb{i}", [128, 512], F32)) for i in range(2)]
    wob_free = [0, 0]
    ostr_free = [0, 0]
    oyb_free = [0, 0]
    oyb_free_sem = [("SC", 0), ("SC", 0)]
    oscr_all = oscr_done[-1] if oscr_done else 0

    for m2t in range(4):
        pw = m2t % 2
        sw = k.gps(lambda e, m2t=m2t, pw=pw: e.dma_start(
            out=wob[pw][:, :].rearrange("p (c m) -> p c m", c=NK),
            in_=wo_d[:, m2t * 512:(m2t + 1) * 512].rearrange("(c p) m -> p c m", p=128)),
            waits=[("SP", wob_free[pw])], inc=("SW", 16))
        sd_w = k.dve(lambda e, pw=pw: e.tensor_scalar(wob[pw][:, :], wob[pw][:, :],
                                                      onw_sb[:, 0:1], None, ALU.mult),
                     waits=[("SW", sw), ("SX", sx_const)], inc=("SD", 1))
        for tt in range(NT):
            po = tt % 2
            sxo = k.syn(lambda e, tt=tt, po=po: e.dma_start(
                out=ostr[po][:, :].rearrange("p (hh t) -> p hh t", hh=H),
                in_=oscr_d[:, :, tt * 128:(tt + 1) * 128].rearrange("hh d t -> d hh t")),
                waits=[("SX", oscr_all), ("SP", ostr_free[po])], inc=("SX", 16))
            b = tt % 2
            for kc in range(H):
                sp = mm(b, lambda b=b: bank[b][:, :],
                        lambda po=po, kc=kc: ostr[po][:, kc * 128:(kc + 1) * 128],
                        lambda pw=pw, kc=kc: wob[pw][:, kc * 512:(kc + 1) * 512],
                        kc == 0, kc == H - 1,
                        waits=[("SX", sxo), ("SD", sd_w), (bank_free[b][0], bank_free[b][1])],
                        inc=("SP", 1) if kc == H - 1 else None)
            ostr_free[po] = sp
            wob_free[pw] = sp
            sd = k.dve(lambda e, b=b, po=po: e.tensor_copy(oyb[po][:, :], bank[b][:, :]),
                       waits=[("SP", sp), oyb_free_sem[po]], inc=("SD", 1))
            bank_free[b] = ("SD", sd)
            sc = k.syn(lambda e, tt=tt, m2t=m2t, po=po: e.dma_start(
                out=acc_d[tt * 128:(tt + 1) * 128, m2t * 512:(m2t + 1) * 512], in_=oyb[po][:, :]),
                waits=[("SD", sd)], inc=("SC", 16))
            oyb_free_sem[po] = ("SC", sc)
    op_es.close()
    k.phase_barrier()
    attn_acc_done = k.v("SC")

    ff_es = contextlib.ExitStack()
    C = ff_es.enter_context
    h_sb = C(nc.sbuf_tensor("h_sb", [128, NJ * S], BF16))
    wdnq = [C(nc.sbuf_tensor(f"wdnq{i}", [128, NJ * 512], BF16)) for i in range(2)]
    gpair = [C(nc.sbuf_tensor(f"gpair{i}", [128, NK * 256], BF16)) for i in range(2)]
    fsg = [C(nc.sbuf_tensor(f"fsg{i}", [128, 512], F32)) for i in range(2)]
    fsg_free = [0, 0]
    facc = [C(nc.sbuf_tensor(f"facc{i}", [128, 512], BF16)) for i in range(4)]
    gpair_free = [0, 0]
    wdnq_free = [0, 0]
    facc_free = [("SC", 0)] * 4
    h_free = 0

    def gu_dma(u, j):
        pg = j % 2
        k.gps(lambda e, u=u, j=j, pg=pg: e.dma_start(
            out=gpair[pg][:, :].rearrange("p (c m) -> p c m", c=NK),
            in_=wgu_d[u, :, j * 256:(j + 1) * 256].rearrange("(c p) m -> p c m", p=128)),
            waits=[("SP", gpair_free[pg])], inc=("SW", 16))
        return k.v("SW")

    for u in range(n_units):
        sw_g = {0: gu_dma(u, 0)}
        for j in range(NJ):
            pg = j % 2
            if j + 1 < NJ:
                sw_g[j + 1] = gu_dma(u, j + 1)
            for st in range(4):
                bg, bu = st % 2, 2 + st % 2
                for kc in range(NK):
                    spg = mm(bg, lambda bg=bg: bank[bg][:, :],
                             lambda pg=pg, kc=kc: gpair[pg][:, kc * 256: kc * 256 + 128],
                             lambda kc=kc, st=st: xn_fm[:, kc * S + st * 512: kc * S + (st + 1) * 512],
                             kc == 0, kc == NK - 1,
                             waits=[("SW", sw_g[j]), (bank_free[bg][0], bank_free[bg][1])],
                             inc=("SP", 1) if kc == NK - 1 else None)
                for kc in range(NK):
                    spu = mm(bu, lambda bu=bu: bank[bu][:, :],
                             lambda pg=pg, kc=kc: gpair[pg][:, kc * 256 + 128: kc * 256 + 256],
                             lambda kc=kc, st=st: xn_fm[:, kc * S + st * 512: kc * S + (st + 1) * 512],
                             kc == 0, kc == NK - 1,
                             waits=[(bank_free[bu][0], bank_free[bu][1])],
                             inc=("SP", 1) if kc == NK - 1 else None)
                ps = st % 2
                sa = k.act(lambda e, ps=ps, bg=bg: e.activation(fsg[ps][:, :], bank[bg][:, :], AF.Silu),
                           waits=[("SP", spg), ("SD", fsg_free[ps])], inc=("SA", 1))
                bank_free[bg] = ("SA", sa)
                sd = k.dve(lambda e, ps=ps, bu=bu, j=j, st=st: e.tensor_tensor(
                    h_sb[:, j * S + st * 512: j * S + (st + 1) * 512], fsg[ps][:, :],
                    bank[bu][:, :], ALU.mult),
                    waits=[("SA", sa), ("SP", spu), ("SP", h_free)], inc=("SD", 1))
                bank_free[bu] = ("SD", sd)
                fsg_free[ps] = sd
            gpair_free[pg] = k.v("SP")
        h_ready = k.v("SD")
        if debug and u == 0:
            k.gps(lambda e: e.dma_start(out=h_dbg, in_=h_sb[:, :]),
                  waits=[("SD", h_ready)], inc=("SST", 16))
            k.barrier_sst()
        for m2t in range(4):
            pw = (u * 4 + m2t) % 2
            sw_d = k.gps(lambda e, u=u, m2t=m2t, pw=pw: e.dma_start(
                out=wdnq[pw][:, :].rearrange("p (c m) -> p c m", c=NJ),
                in_=wdn_d[u, :, m2t * 512:(m2t + 1) * 512].rearrange("(c p) m -> p c m", p=128)),
                waits=[("SP", wdnq_free[pw])], inc=("SW", 16))
            for tt in range(NT):
                b = 4 + tt % 2
                for j in range(NJ):
                    sp = mm(b, lambda b=b: bank[b][:, :],
                            lambda j=j, tt=tt: h_sb[:, j * S + tt * 128: j * S + (tt + 1) * 128],
                            lambda j=j, pw=pw: wdnq[pw][:, j * 512:(j + 1) * 512],
                            j == 0, j == NJ - 1,
                            waits=[("SW", sw_d), ("SD", h_ready),
                                   (bank_free[b][0], bank_free[b][1])],
                            inc=("SP", 1) if j == NJ - 1 else None)
                wdnq_free[pw] = sp
                py = tt % 4
                ccol = (lambda tt=tt, u=u: c_all[:, tt * 17 + u: tt * 17 + u + 1]) if u < 16 \
                    else (lambda tt=tt: c_all[:, tt * 17 + 16: tt * 17 + 17])
                sd = k.dve(lambda e, b=b, py=py, ccol=ccol: e.tensor_scalar(
                    facc[py][:, :], bank[b][:, :], ccol(), None, ALU.mult),
                    waits=[("SP", sp), ("SA", c_ready), facc_free[py]], inc=("SD", 1))
                bank_free[b] = ("SD", sd)
                sc = k.gps(lambda e, tt=tt, m2t=m2t, py=py: e.dma_start(
                    out=acc_d[tt * 128:(tt + 1) * 128, m2t * 512:(m2t + 1) * 512],
                    in_=facc[py][:, :], accum_op=ALU.add),
                    waits=[("SD", sd)], inc=("SC", 16))
                facc_free[py] = ("SC", sc)
        h_free = k.v("SP")
    ff_es.close()
    k.phase_barrier()
    acc_all = k.v("SC")

    fn_es = contextlib.ExitStack()
    C = fn_es.enter_context
    fx = [C(nc.sbuf_tensor(f"fx{i}", [128, 2048], F32)) for i in range(2)]
    fa = [C(nc.sbuf_tensor(f"fa{i}", [128, 2048], F32)) for i in range(2)]
    f_free = [0, 0]
    fa_free = [0, 0]
    for tt in range(NT):
        p = tt % 2
        sx1 = k.syn(lambda e, tt=tt, p=p: e.dma_start(out=fx[p][:, :],
                                                      in_=x_d[tt * 128:(tt + 1) * 128, :]),
                    waits=[("SST", f_free[p])], inc=("SX", 16))
        sx2 = k.syn(lambda e, tt=tt, p=p: e.dma_start(out=fa[p][:, :],
                                                      in_=acc_d[tt * 128:(tt + 1) * 128, :]),
                    waits=[("SC", acc_all), ("SD", fa_free[p])], inc=("SX", 16))
        sd = k.dve(lambda e, p=p: e.tensor_add(fx[p][:, :], fx[p][:, :], fa[p][:, :]),
                   waits=[("SX", sx2)], inc=("SD", 1))
        fa_free[p] = sd
        st = k.syn(lambda e, tt=tt, p=p: e.dma_start(out=out_d[tt * 128:(tt + 1) * 128, :],
                                                     in_=fx[p][:, :]),
                   waits=[("SD", sd)], inc=("SST", 16))
        f_free[p] = st
    fn_es.close()

    sst_final = k.v("SST")
    with nc.Block() as blk:
        blk.tensor(lambda e: [f(e) for f in k.ops["tensor"]])
        blk.scalar(lambda e: [f(e) for f in k.ops["scalar"]])
        blk.vector(lambda e: [f(e) for f in k.ops["vector"]])
        blk.gpsimd(lambda e: [f(e) for f in k.ops["gpsimd"]])

        def syncf(e):
            for f in k.ops["sync"]:
                f(e)
            e.wait_ge(k.sems["SST"], sst_final)
        blk.sync(syncf)
    return nc


def prep_inputs(L0_attn_x, L0_attn_to_qkv_w, L0_attn_o_proj_w, L0_attn_lr_proj_w,
                L0_attn_w0, L0_attn_w1, L0_attn_w2, L0_attn_o_norm_w,
                L0_ffn_gate_w, L0_ffn_expert_gate_up, L0_ffn_expert_down,
                L0_ffn_shared_gate_up, L0_ffn_shared_down, L0_ffn_shared_gate_vec):
    f = np.float32
    ac = np.ascontiguousarray
    wqkv_t = ac(np.asarray(L0_attn_to_qkv_w, f).T)
    wo_t = ac(np.asarray(L0_attn_o_proj_w, f).T)
    wlr_t = ac(np.asarray(L0_attn_lr_proj_w, f).T)
    w0 = np.asarray(L0_attn_w0, f)
    w1 = np.asarray(L0_attn_w1, f)
    w2 = np.asarray(L0_attn_w2, f)
    w_nat = ac(np.stack([w0, w1, w2], 0))
    w_t = ac(np.stack([w0.transpose(0, 2, 1), w2.transpose(0, 2, 1)], 0))
    onw = ac(np.asarray(L0_attn_o_norm_w, f).reshape(128, 1))
    wgate_t = np.zeros((D, 32), f)
    wgate_t[:, 0:16] = np.asarray(L0_ffn_gate_w, f).T
    wgate_t[:, 16:17] = np.asarray(L0_ffn_shared_gate_vec, f).T
    egu = np.asarray(L0_ffn_expert_gate_up, f)
    sgu = np.asarray(L0_ffn_shared_gate_up, f)
    edn = np.asarray(L0_ffn_expert_down, f)
    sdn = np.asarray(L0_ffn_shared_down, f)
    perm = np.empty(2816, np.int64)
    for j in range(NJ):
        perm[j * 256: j * 256 + 128] = np.arange(j * 128, (j + 1) * 128)
        perm[j * 256 + 128: (j + 1) * 256] = np.arange(EI + j * 128, EI + (j + 1) * 128)
    units_gu = np.empty((NU, D, 2816), f)
    units_dn = np.empty((NU, EI, D), f)
    for e in range(16):
        units_gu[e] = egu[e][perm].T
        units_dn[e] = edn[e].T
    for j in range(4):
        blk_g = sgu[j * EI:(j + 1) * EI]
        blk_u = sgu[5632 + j * EI: 5632 + (j + 1) * EI]
        gu = np.concatenate([blk_g, blk_u], 0)
        units_gu[16 + j] = gu[perm].T
        units_dn[16 + j] = sdn[:, j * EI:(j + 1) * EI].T
    shared = {
        "wqkv_t": wqkv_t, "wo_t": wo_t, "wlr_t": wlr_t, "w_nat": w_nat,
        "w_t": w_t, "onw": onw, "wgate_t": wgate_t,
        "wgu_t": ac(units_gu), "wdn_t": ac(units_dn),
    }
    xs = np.asarray(L0_attn_x, f)
    return xs, shared


def kernel(**inputs):
    xs, shared = prep_inputs(**inputs)
    nc = build()
    in_maps = [dict(shared, x=np.ascontiguousarray(xs[b])) for b in range(8)]
    res = run_bass_kernel_spmd(nc, in_maps, core_ids=list(range(8)))
    return np.stack([r["out"] for r in res.results], axis=0)


# version 13
# speedup vs baseline: 1.2113x; 1.0975x over previous
import math
import numpy as np
import ml_dtypes
import concourse.bass as bass
import concourse.mybir as mybir
from concourse.bass_utils import run_bass_kernel_spmd

F32 = mybir.dt.float32
BF16 = mybir.dt.bfloat16
AF = mybir.ActivationFunctionType
ALU = mybir.AluOpType
AX = mybir.AxisListType.X

S = 2048
D = 2048
H = 16
HD = 128
NT = S // 128
NK = D // 128
NU = 20
EI = 1408
NJ = EI // 128
BASE_LR_INV = float(0.01 + math.log(-math.expm1(-0.01)))
NS_COEFS = [(4.0848, -6.8946, 2.927), (3.9505, -6.3029, 2.6377),
            (3.7418, -5.5913, 2.3037), (2.8769, -3.1427, 1.2046),
            (2.8366, -3.0525, 1.2012)]

ENGS = ("tensor", "scalar", "vector", "sync", "gpsimd")


class Emit:

    def __init__(self, nc):
        self.nc = nc
        self.ops = {e: [] for e in ENGS}
        self.semv = {}
        self.sems = {}
        self.hw = {e: {} for e in ENGS}

    def sem(self, name):
        self.sems[name] = self.nc.alloc_semaphore(name)
        self.semv[name] = 0

    def _op(self, eng, fn, waits, inc):
        tgt = None
        if inc is not None:
            s, n = inc
            self.semv[s] += n
            tgt = self.semv[s]
        ws = []
        for s, v in waits or ():
            if v is None or v <= 0:
                continue
            if self.hw[eng].get(s, -1) >= v:
                continue
            self.hw[eng][s] = v
            ws.append((self.sems[s], v))
        inch = (self.sems[inc[0]], inc[1]) if inc is not None else None

        def run(e, ws=ws, fn=fn, inch=inch):
            for sh, v in ws:
                e.wait_ge(sh, v)
            ins = fn(e)
            if inch is not None:
                ins.then_inc(inch[0], inch[1])
        self.ops[eng].append(run)
        return tgt

    def pe(self, fn, waits=None, inc=None):
        return self._op("tensor", fn, waits, inc)

    def act(self, fn, waits=None, inc=None):
        return self._op("scalar", fn, waits, inc)

    def dve(self, fn, waits=None, inc=None):
        return self._op("vector", fn, waits, inc)

    def syn(self, fn, waits=None, inc=None):
        return self._op("sync", fn, waits, inc)

    def gps(self, fn, waits=None, inc=None):
        return self._op("gpsimd", fn, waits, inc)

    def v(self, s):
        return self.semv[s]

    def phase_barrier(self):
        ws = [(s, self.semv[s]) for s in ("SP", "SA", "SD", "SW", "SX", "SC")]
        for eng in ENGS:
            self._op(eng, lambda e: e.nop(), list(ws), None)

    def barrier_sst(self):
        tgt = self.semv["SST"]
        for eng in ENGS:
            self._op(eng, lambda e: e.nop(), [("SST", tgt)], None)


def build(n_heads=H, n_units=NU, debug=False):
    nc = bass.Bass("TRN2", target_bir_lowering=False)

    x_d = nc.dram_tensor("x", [S, D], F32, kind="ExternalInput").ap()
    wqkv_d = nc.dram_tensor("wqkv_t", [D, 3 * D], F32, kind="ExternalInput").ap()
    wo_d = nc.dram_tensor("wo_t", [D, D], F32, kind="ExternalInput").ap()
    wlr_d = nc.dram_tensor("wlr_t", [D, 48], F32, kind="ExternalInput").ap()
    wnat_d = nc.dram_tensor("w_nat", [3, H, 128, 128], F32, kind="ExternalInput").ap()
    wt_d = nc.dram_tensor("w_t", [2, H, 128, 128], F32, kind="ExternalInput").ap()
    onw_d = nc.dram_tensor("onw", [128, 1], F32, kind="ExternalInput").ap()
    wgate_d = nc.dram_tensor("wgate_t", [D, 32], F32, kind="ExternalInput").ap()
    wgu_d = nc.dram_tensor("wgu_t", [NU, D, 2816], F32, kind="ExternalInput").ap()
    wdn_d = nc.dram_tensor("wdn_t", [NU, EI, D], F32, kind="ExternalInput").ap()
    out_d = nc.dram_tensor("out", [S, D], F32, kind="ExternalOutput").ap()

    scr_kind = "ExternalOutput" if debug else "Internal"
    acc_d = nc.dram_tensor("acc", [S, D], F32, kind=scr_kind).ap()
    yscr_d = nc.dram_tensor("y_scr", [NU, S, D], BF16, kind="Internal").ap()
    oscr_d = nc.dram_tensor("o_scr", [H, 128, S], BF16, kind="Internal").ap()
    if debug:
        xn_dbg = nc.dram_tensor("xn_dbg", [NK, 128, S], F32, kind="ExternalOutput").ap()
        oscr_dbg = nc.dram_tensor("o_scr_dbg", [H, 128, S], F32, kind="ExternalOutput").ap()
        call_dbg = nc.dram_tensor("call_dbg", [128, NT * 17], F32, kind="ExternalOutput").ap()
        lr_dbg = nc.dram_tensor("lr_dbg", [128, NT * 48], F32, kind="ExternalOutput").ap()
        qkv_dbg = nc.dram_tensor("qkv_dbg", [3, 128, S], F32, kind="ExternalOutput").ap()
        e16_dbg = nc.dram_tensor("e16_dbg", [128, NT * 16], F32, kind="ExternalOutput").ap()
        rt_dbg = nc.dram_tensor("rt_dbg", [128, NT * 17], F32, kind="ExternalOutput").ap()
        wg_dbg = nc.dram_tensor("wg_dbg", [128, NK * 32], F32, kind="ExternalOutput").ap()
        nrm_dbg = nc.dram_tensor("nrm_dbg", [128, 64], F32, kind="ExternalOutput").ap()
        qtm_dbg = nc.dram_tensor("qtm_dbg", [128, S], F32, kind="ExternalOutput").ap()
        dw_dbg = nc.dram_tensor("dw_dbg", [3, 128, 128], F32, kind="ExternalOutput").ap()
        ns_dbg = nc.dram_tensor("ns_dbg", [3, 128, 128], F32, kind="ExternalOutput").ap()
        wn_dbg = nc.dram_tensor("wn_dbg", [3, 128, 128], F32, kind="ExternalOutput").ap()
        o_dbg = nc.dram_tensor("o_dbg", [128, S], F32, kind="ExternalOutput").ap()
        h_dbg = nc.dram_tensor("h_dbg", [128, NJ * S], F32, kind="ExternalOutput").ap()

    ident = nc.inline_tensor(np.eye(128, dtype=ml_dtypes.bfloat16).view(np.uint16),
                             name="ident_bf")
    identf = nc.inline_tensor(np.eye(128, dtype=np.float32), name="ident_f32")
    onesr = nc.inline_tensor(np.ones((1, 128), dtype=ml_dtypes.bfloat16).view(np.uint16),
                             name="ones_row_bf")
    onesc = nc.inline_tensor(np.ones((128, 1), dtype=ml_dtypes.bfloat16).view(np.uint16),
                             name="ones_col_bf")
    cst_np = np.tile(np.array([[1.0 / D, 1e-5, BASE_LR_INV, 1e-7]], np.float32), (128, 1))
    cst_t = nc.inline_tensor(cst_np, name="cst_f32")

    A = nc.alloc_sbuf_tensor
    xn_fm = A("xn_fm", [128, NK * S], BF16)
    c_all = A("c_all", [128, NT * 17], F32)
    lr_tm = A("lr_tm", [128, NT * 48], F32)


    wlr_sb = A("wlr_sb", [128, NK * 48], BF16)
    onw_sb = A("onw_sb", [128, 1], F32)
    id_sb = A("id_sb", [128, 128], BF16)
    idf_sb = A("idf_sb", [128, 128], F32)
    wgf_sb = A("wgf_sb", [128, NK * 32], F32)
    onr_sb = A("onr_sb", [1, 128], BF16)
    onc_sb = A("onc_sb", [128, 1], BF16)
    cst_sb = A("cst_sb", [128, 4], F32)
    rt_sb = A("rt_sb", [128, 17], F32)
    rt_all = A("rt_all", [128, NT * 17], F32)
    e16a = A("e16a", [128, 16], F32)
    e16b = A("e16b", [128, 16], F32)
    msk = A("msk", [128, 16], F32)
    nm_sb = A("nm_sb", [128, 1], F32)
    mr_sb = A("mr_sb", [128, 1], F32)
    mr2_sb = A("mr2_sb", [128, 1], F32)
    ns_sb = A("ns_sb", [128, 1], F32)
    tng = A("tng", [128, 1], F32)
    st_s = [A(f"st_s{i}", [128, 1], F32) for i in range(2)]
    st_m = [A(f"st_m{i}", [128, 1], F32) for i in range(2)]
    st_v = [A(f"st_v{i}", [128, 1], F32) for i in range(2)]
    st_r = [A(f"st_r{i}", [128, 1], F32) for i in range(2)]
    sq_scrap = A("sq_scrap", [128, 2048], F32)

    k = Emit(nc)
    for s in ("SX", "SW", "SP", "SA", "SD", "SC", "SST"):
        k.sem(s)

    bank = [nc.alloc_psum_tensor(f"bank{i}", [128, 512], F32) for i in range(8)]
    bank_free = [("SP", 0)] * 8

    def mm(b, out_ap_fn, lhsT_fn, rhs_fn, start, stop, waits=None, inc=None):
        def f(e):
            return e.matmul(out_ap_fn(), lhsT_fn(), rhs_fn(), start=start, stop=stop)
        return k.pe(f, waits=waits, inc=inc)

    k.syn(lambda e: e.dma_start(out=id_sb[:, :], in_=ident.ap().bitcast(BF16)), inc=("SX", 16))
    k.syn(lambda e: e.dma_start(out=idf_sb[:, :], in_=identf.ap()), inc=("SX", 16))
    k.syn(lambda e: e.dma_start(
        out=wgf_sb[:, :].rearrange("p (c m) -> p c m", c=NK),
        in_=wgate_d.rearrange("(c p) m -> p c m", p=128)), inc=("SX", 16))
    k.syn(lambda e: e.dma_start(out=onr_sb[:, :], in_=onesr.ap().bitcast(BF16)), inc=("SX", 16))
    k.syn(lambda e: e.dma_start(out=onc_sb[:, :], in_=onesc.ap().bitcast(BF16)), inc=("SX", 16))
    k.syn(lambda e: e.dma_start(out=onw_sb[:, :], in_=onw_d), inc=("SX", 16))
    k.syn(lambda e: e.dma_start(out=cst_sb[:, :], in_=cst_t.ap()), inc=("SX", 16))
    k.gps(lambda e: e.dma_start(
        out=wlr_sb[:, :].rearrange("p (c m) -> p c m", c=NK),
        in_=wlr_d.rearrange("(c p) m -> p c m", p=128)), inc=("SW", 16))
    sx_const = k.v("SX")
    sw_const = k.v("SW")

    import contextlib
    ln_es = contextlib.ExitStack()
    xt = [ln_es.enter_context(nc.sbuf_tensor(f"xt{i}", [128, 2048], F32)) for i in range(2)]
    xcb = [ln_es.enter_context(nc.sbuf_tensor(f"xcb{i}", [128, 2048], F32)) for i in range(2)]
    xnt = [ln_es.enter_context(nc.sbuf_tensor(f"xnt{i}", [128, 2048], BF16)) for i in range(2)]
    lnsq = [ln_es.enter_context(nc.sbuf_tensor(f"lnsq{i}", [128, 2048], F32)) for i in range(2)]
    xnf32 = ln_es.enter_context(nc.sbuf_tensor("xnf32", [128, NK * 128], F32))

    xt_free = [0, 0]
    rtf_done = 0
    rt_ready = [0] * NT
    st_m_free = 0
    xt_free_d = [0, 0]
    xcb_free = [0, 0]
    xcb_free_a = [0, 0]
    xnt_free = [0, 0]
    xn_ready = [0] * NT

    for tt in range(NT):
        p = tt % 2
        sx = k.syn(lambda e, tt=tt, p=p: e.dma_start(out=xt[p][:, :], in_=x_d[tt * 128:(tt + 1) * 128, :]),
                   waits=[("SD", xt_free[p])], inc=("SX", 16))
        sd = k.dve(lambda e, p=p: e.tensor_reduce(st_s[p][:, :], xt[p][:, :], AX, ALU.add, negate=True),
                   waits=[("SX", sx), ("SA", st_m_free)], inc=("SD", 1))
        sa = k.act(lambda e, p=p: e.activation(st_m[p][:, :], st_s[p][:, :], AF.Identity,
                                          scale=cst_sb[:, 0:1]),
                   waits=[("SD", sd), ("SX", sx_const)], inc=("SA", 1))
        sd = k.dve(lambda e, p=p: e.tensor_scalar(xcb[p][:, :], xt[p][:, :], st_m[p][:, 0:1],
                                                  None, ALU.add),
                   waits=[("SA", sa), ("SA", xcb_free_a[p])], inc=("SD", 1))
        xt_free[p] = sd
        st_m_free = k.v("SA")
        sa = k.act(lambda e, p=p: e.activation(lnsq[p][:, :], xcb[p][:, :], AF.Square),
                   waits=[("SD", sd)], inc=("SA", 1))
        sd = k.dve(lambda e, p=p: e.tensor_reduce(st_v[p][:, :], lnsq[p][:, :], AX, ALU.add),
                   waits=[("SA", sa)], inc=("SD", 1))
        sa = k.act(lambda e, p=p: e.activation(st_v[p][:, :], st_v[p][:, :], AF.Sqrt,
                                          scale=cst_sb[:, 0:1], bias=cst_sb[:, 1:2]),
                   waits=[("SD", sd)], inc=("SA", 1))
        sd = k.dve(lambda e, p=p: e.reciprocal(st_r[p][:, :], st_v[p][:, :]), waits=[("SA", sa)], inc=("SD", 1))
        sa = k.act(lambda e, p=p: e.activation(xnt[p][:, :], xcb[p][:, :], AF.Identity,
                                               scale=st_r[p][:, 0:1]),
                   waits=[("SD", sd), ("SP", xnt_free[p])], inc=("SA", 1))
        xcb_free[p] = k.v("SD")
        xcb_free_a[p] = sa
        xt_free_d[p] = sd
        last_sp = 0
        for g in range(4):
            b = g % 2
            for j in range(4):
                kc = g * 4 + j
                sp = mm(b, lambda b=b, j=j: bank[b][:, j * 128:(j + 1) * 128],
                        lambda p=p, kc=kc: xnt[p][:, kc * 128:(kc + 1) * 128],
                        lambda: id_sb[:, :], True, True,
                        waits=[("SA", sa), (bank_free[b][0], bank_free[b][1])],
                        inc=("SP", 1) if j == 3 else None)
            k.dve(lambda e, b=b, g=g, tt=tt: e.tensor_copy(
                xn_fm[:, :].rearrange("p (c s) -> p c s", c=NK)[:, g * 4:(g + 1) * 4, tt * 128:(tt + 1) * 128],
                bank[b][:, :].rearrange("p (c s) -> p c s", c=4)),
                waits=[("SP", sp), ("SP", rtf_done)], inc=("SD", 1))
            sdv = k.dve(lambda e, b=b, g=g: e.tensor_copy(
                xnf32[:, g * 512:(g + 1) * 512], bank[b][:, :]),
                inc=("SD", 1))
            bank_free[b] = ("SD", sdv)
            last_sp = sp
        xnt_free[p] = last_sp
        xn_ready[tt] = sdv
        for kc in range(NK):
            rtf_done = mm(2, lambda: bank[2][:, 0:32],
                          lambda kc=kc: xnf32[:, kc * 128:(kc + 1) * 128],
                          lambda kc=kc: wgf_sb[:, kc * 32:(kc + 1) * 32],
                          kc == 0, kc == NK - 1,
                          waits=[("SD", sdv), ("SX", sx_const),
                                 (bank_free[2][0], bank_free[2][1])],
                          inc=("SP", 1) if kc == NK - 1 else None)
        sa_rt = k.act(lambda e, tt=tt: e.activation(rt_all[:, tt * 17:(tt + 1) * 17],
                                                    bank[2][:, 0:17], AF.Identity),
                      waits=[("SP", rtf_done)], inc=("SA", 1))
        bank_free[2] = ("SA", sa_rt)
        rt_ready[tt] = sa_rt
    ln_es.close()
    k.phase_barrier()

    if debug:
        for kc in range(NK):
            k.gps(lambda e, kc=kc: e.dma_start(out=xn_dbg[kc], in_=xn_fm[:, kc * S:(kc + 1) * S]),
                  waits=[("SD", xn_ready[NT - 1])], inc=("SST", 16))

    rt_state = {"free_sd": 0}

    def emit_router_tile(tt):
        sa = k.act(lambda e, tt=tt: e.activation(rt_sb[:, :], rt_all[:, tt * 17:(tt + 1) * 17],
                                                 AF.Identity),
                   waits=[("SA", rt_ready[tt]), ("SD", rt_state["free_sd"])], inc=("SA", 1))
        sd = k.dve(lambda e: e.tensor_reduce(nm_sb[:, :], rt_sb[:, 0:16], AX, ALU.max, negate=True),
                   waits=[("SA", sa)], inc=("SD", 1))
        sa = k.act(lambda e: e.activation(e16a[:, :], rt_sb[:, 0:16], AF.Exp, bias=nm_sb[:, 0:1]),
                   waits=[("SD", sd)], inc=("SA", 1))
        sa = k.act(lambda e: e.activation(tng[:, :], rt_sb[:, 16:17], AF.Tanh, scale=0.5),
                   inc=("SA", 1))
        csl = lambda tt=tt: c_all[:, tt * 17: tt * 17 + 16]
        k.dve(lambda e, tt=tt: e.tensor_scalar(c_all[:, tt * 17 + 16: tt * 17 + 17], tng[:, :],
                                               0.5, 0.5, ALU.mult, ALU.add),
              waits=[("SA", sa)])
        k.dve(lambda e, csl=csl: e.memset(csl(), 0.0))
        ebufs = [e16a, e16b]
        for r in range(4):
            ecur = ebufs[r % 2]
            enxt = ebufs[(r + 1) % 2]
            sd = k.dve(lambda e, ecur=ecur: e.tensor_reduce(mr_sb[:, :], ecur[:, :], AX, ALU.max),
                       inc=("SD", 1))
            sa = k.act(lambda e: e.activation(mr2_sb[:, :], mr_sb[:, :], AF.Identity),
                       waits=[("SD", sd)], inc=("SA", 1))
            k.dve(lambda e, ecur=ecur: e.tensor_scalar(msk[:, :], ecur[:, :], mr2_sb[:, 0:1],
                                                       None, ALU.is_ge),
                  waits=[("SA", sa)])
            k.dve(lambda e, csl=csl: e.scalar_tensor_tensor(csl(), msk[:, :], mr2_sb[:, 0:1], csl(),
                                                            ALU.mult, ALU.add))
            if r == 0:
                sd_ns_acc = k.dve(lambda e: e.tensor_copy(ns_sb[:, :], mr2_sb[:, :]), inc=("SD", 1))
            else:
                sd_ns_acc = k.dve(lambda e: e.tensor_add(ns_sb[:, :], ns_sb[:, :], mr2_sb[:, :]),
                                  inc=("SD", 1))
            if r < 3:
                sd = k.dve(lambda e, ecur=ecur, enxt=enxt: e.scalar_tensor_tensor(
                    enxt[:, :], msk[:, :], -1e30, ecur[:, :], ALU.mult, ALU.add),
                    inc=("SD", 1))
                sa = k.act(lambda e, enxt=enxt: e.activation(enxt[:, :], enxt[:, :], AF.Identity),
                           waits=[("SD", sd)], inc=("SA", 1))
                k.dve(lambda e: e.nop(), waits=[("SA", sa)])
        sa = k.act(lambda e: e.activation(mr2_sb[:, :], ns_sb[:, :], AF.Identity),
                   waits=[("SD", sd_ns_acc)], inc=("SA", 1))
        sd = k.dve(lambda e: e.reciprocal(mr_sb[:, :], mr2_sb[:, :]),
                   waits=[("SA", sa)], inc=("SD", 1))
        k.act(lambda e, csl=csl: e.activation(csl(), csl(), AF.Identity,
                                              scale=mr_sb[:, 0:1]),
              waits=[("SD", sd)], inc=("SA", 1))
        rt_state["free_sd"] = k.v("SD")

    import contextlib as _ctx
    lr_es = _ctx.ExitStack()
    lr_fm = lr_es.enter_context(nc.sbuf_tensor("lr_fm", [48, S], F32))
    lr_scrap_free = 0
    for st in range(4):
        sp = None
        for kc in range(NK):
            sp = mm(3, lambda: bank[3][0:48, :],
                    lambda kc=kc: wlr_sb[:, kc * 48:(kc + 1) * 48],
                    lambda kc=kc, st=st: xn_fm[:, kc * S + st * 512: kc * S + (st + 1) * 512],
                    kc == 0, kc == NK - 1,
                    waits=[("SD", xn_ready[NT - 1]), ("SW", sw_const),
                           (bank_free[3][0], bank_free[3][1])],
                    inc=("SP", 1) if kc == NK - 1 else None)
        sa = k.act(lambda e: e.activation(sq_scrap[0:48, 0:512], bank[3][0:48, :], AF.Abs,
                                          bias=cst_sb[0:48, 2:3]),
                   waits=[("SP", sp), ("SD", lr_scrap_free if st > 0 else 0)], inc=("SA", 1))
        sa = k.act(lambda e: e.activation(sq_scrap[0:48, 512:1024], sq_scrap[0:48, 0:512], AF.Exp,
                                          scale=-1.0), waits=[("SA", sa)], inc=("SA", 1))
        sa = k.act(lambda e: e.activation(sq_scrap[0:48, 512:1024], sq_scrap[0:48, 512:1024], AF.Ln,
                                          bias=1.0), waits=[("SA", sa)], inc=("SA", 1))
        sd = k.dve(lambda e: e.tensor_scalar(sq_scrap[0:48, 1024:1536], bank[3][0:48, :],
                                             BASE_LR_INV, 0.0, ALU.add, ALU.max),
                   waits=[("SA", sa)], inc=("SD", 1))
        bank_free[3] = ("SD", sd)
        sd = k.dve(lambda e, st=st: e.tensor_add(lr_fm[:, st * 512:(st + 1) * 512],
                                                 sq_scrap[0:48, 1024:1536], sq_scrap[0:48, 512:1024]),
                   inc=("SD", 1))
        lr_scrap_free = sd
    lr_fm_ready = sd
    for g in range(2):
        b = 4 + g
        for j in range(8):
            tt = g * 8 + j
            sp = mm(b, lambda b=b, j=j: bank[b][:, j * 48:(j + 1) * 48],
                    lambda tt=tt: lr_fm[:, tt * 128:(tt + 1) * 128],
                    lambda: idf_sb[0:48, 0:48], True, True,
                    waits=[("SD", lr_fm_ready), ("SX", sx_const),
                           (bank_free[b][0], bank_free[b][1])],
                    inc=("SP", 1) if j == 7 else None)
        sd = k.dve(lambda e, b=b, g=g: e.tensor_copy(lr_tm[:, g * 8 * 48:(g + 1) * 8 * 48],
                                                     bank[b][:, 0:384]),
                   waits=[("SP", sp)], inc=("SD", 1))
        bank_free[b] = ("SD", sd)
    lr_ready = k.v("SD")
    lr_es.close()
    k.phase_barrier()
    if debug:
        k.syn(lambda e: e.dma_start(out=lr_dbg, in_=lr_tm[:, :]),
              waits=[("SD", lr_ready)], inc=("SST", 16))

    at_es = contextlib.ExitStack()
    C = at_es.enter_context
    qkvw = [C(nc.sbuf_tensor(f"qkvw{i}", [128, 3 * NK * 128], BF16)) for i in range(2)]
    wnat = [C(nc.sbuf_tensor(f"wnat{i}", [128, 3 * 128], F32)) for i in range(2)]
    wT = [C(nc.sbuf_tensor(f"wT{i}", [128, 2 * 128], BF16)) for i in range(2)]
    w1bf = [C(nc.sbuf_tensor(f"w1bf{i}", [128, 128], BF16)) for i in range(2)]
    q_fm2 = [C(nc.sbuf_tensor(f"q_fm{i}", [128, S], BF16)) for i in range(2)]
    k_fm = C(nc.sbuf_tensor("k_fm", [128, S], BF16))
    v_fm = C(nc.sbuf_tensor("v_fm", [128, S], BF16))
    q_tm = C(nc.sbuf_tensor("q_tm", [128, S], BF16))
    k_tm = C(nc.sbuf_tensor("k_tm", [128, S], BF16))
    v_tm = C(nc.sbuf_tensor("v_tm", [128, S], BF16))
    nrm32 = C(nc.sbuf_tensor("nrm32", [128, 32], F32))
    rr32 = C(nc.sbuf_tensor("rr32", [128, 32], F32))
    rr32b = C(nc.sbuf_tensor("rr32b", [128, 32], F32))
    tanb = C(nc.sbuf_tensor("tanb", [128, 512], F32))
    sigb = C(nc.sbuf_tensor("sigb", [128, 512], F32))
    ub = C(nc.sbuf_tensor("ub", [128, 512], F32))
    dgb = C(nc.sbuf_tensor("dgb", [128, 512], F32))
    sgb2 = [C(nc.sbuf_tensor(f"sgb{i}", [128, 512], F32)) for i in range(2)]
    hbb = C(nc.sbuf_tensor("hbb", [128, 512], F32))
    dhb = C(nc.sbuf_tensor("dhb", [128, 512], F32))
    hidb2 = [C(nc.sbuf_tensor(f"hidb{i}", [128, 512], BF16)) for i in range(2)]
    dbmb = C(nc.sbuf_tensor("dbmb", [128, 512], BF16))
    dgab = C(nc.sbuf_tensor("dgab", [128, 512], BF16))
    hl1t = C(nc.sbuf_tensor("hl1t", [128, 512], BF16))
    dgat = C(nc.sbuf_tensor("dgat", [128, 512], BF16))
    dbmt = C(nc.sbuf_tensor("dbmt", [128, 512], BF16))
    kl0t = C(nc.sbuf_tensor("kl0t", [128, 512], BF16))
    kl2t = C(nc.sbuf_tensor("kl2t", [128, 512], BF16))
    Xb = [C(nc.sbuf_tensor(f"Xb{i}", [128, 128], BF16)) for i in range(3)]
    XTb = [C(nc.sbuf_tensor(f"XTb{i}", [128, 128], BF16)) for i in range(3)]
    Ab = [C(nc.sbuf_tensor(f"Ab{i}", [128, 128], BF16)) for i in range(3)]
    Abf = [C(nc.sbuf_tensor(f"Abf{i}", [128, 128], F32)) for i in range(3)]
    Bb = [C(nc.sbuf_tensor(f"Bb{i}", [128, 128], BF16)) for i in range(3)]
    dwssb = [C(nc.sbuf_tensor(f"dwssb{i}", [128, 1], BF16)) for i in range(3)]
    facb = [C(nc.sbuf_tensor(f"facb{i}", [128, 1], F32)) for i in range(3)]
    smll = [C(nc.sbuf_tensor(f"smll{i}", [1, 8], F32)) for i in range(3)]
    smlb = [C(nc.sbuf_tensor(f"smlb{i}", [1, 1], BF16)) for i in range(3)]
    sqG = C(nc.sbuf_tensor("sqG", [128, 384], F32))
    wn_h2 = [C(nc.sbuf_tensor(f"wn_h{i}", [128, 3], F32)) for i in range(2)]
    wnsq = C(nc.sbuf_tensor("wnsq", [128, 384], F32))
    rn3 = C(nc.sbuf_tensor("rn3", [128, 3], F32))
    Wf32 = C(nc.sbuf_tensor("Wf32", [128, 128], F32))
    Wnb = [C(nc.sbuf_tensor(f"Wnb{i}", [128, 128], BF16)) for i in range(3)]
    WnT = [C(nc.sbuf_tensor(f"WnT{i}", [128, 128], BF16)) for i in range(3)]
    o_bf = C(nc.sbuf_tensor("o_bf", [128, S], BF16))
    o2_bf = C(nc.sbuf_tensor("o2_bf", [128, S], BF16))
    o_tm = C(nc.sbuf_tensor("o_tm", [128, S], BF16))
    dsum = C(nc.sbuf_tensor("dsum", [128, 16], F32))
    rmfac = C(nc.sbuf_tensor("rmfac", [128, 16], F32))
    rmfacb = C(nc.sbuf_tensor("rmfacb", [128, 16], F32))

    hw_free = [(0, 0, 0), (0, 0, 0)]
    wnh_free = [0, 0]
    oscr_done = []

    def head_dmas(h):
        p = h % 2
        w = [("SP", hw_free[p][0]), ("SA", hw_free[p][1]), ("SD", hw_free[p][2])]
        for c in range(3):
            k.gps(lambda e, c=c, h=h, p=p: e.dma_start(
                out=qkvw[p][:, c * 2048:(c + 1) * 2048].rearrange("p (c d) -> p c d", d=128),
                in_=wqkv_d[:, c * 2048 + h * 128: c * 2048 + (h + 1) * 128]
                .rearrange("(c p) d -> p c d", p=128)), waits=w, inc=("SW", 16))
        k.gps(lambda e, h=h, p=p: e.dma_start(
            out=wT[p][:, :].rearrange("p (i d) -> p i d", i=2),
            in_=wt_d[:, h].rearrange("i p d -> p i d")), waits=w, inc=("SW", 16))
        k.gps(lambda e, h=h, p=p: e.dma_start(out=w1bf[p][:, :], in_=wnat_d[1, h]),
              waits=w, inc=("SW", 16))
        k.syn(lambda e, h=h, p=p: e.dma_start(
            out=wnat[p][:, :].rearrange("p (i d) -> p i d", i=3),
            in_=wnat_d[:, h].rearrange("i p d -> p i d")), waits=w, inc=("SX", 16))
        return k.v("SW"), k.v("SX")

    def transpose_batch(src_fn, dst_fn, n_tiles, b0, b1, src_ready, fuse_scale=None,
                        dst_dtype_bf=True):
        nb = (n_tiles + 3) // 4
        last_sa = None
        for g in range(nb):
            b = (b0, b1)[g % 2]
            cnt = min(4, n_tiles - g * 4)
            for j in range(cnt):
                t = g * 4 + j
                sp = mm(b, lambda b=b, j=j: bank[b][:, j * 128:(j + 1) * 128],
                        lambda t=t: src_fn(t), lambda: id_sb[:, :], True, True,
                        waits=[("SD", src_ready[0] if isinstance(src_ready, tuple) else src_ready),
                               (bank_free[b][0], bank_free[b][1])] +
                              ([("SA", src_ready[1])] if isinstance(src_ready, tuple) else []),
                        inc=("SP", 1) if j == cnt - 1 else None)
            use_act = (nb >= 2 and g % 2 == 1)
            if fuse_scale is None:
                if use_act:
                    sa = k.act(lambda e, b=b, g=g, cnt=cnt: e.activation(
                        dst_fn(g * 4, cnt), bank[b][:, 0:cnt * 128], AF.Identity),
                        waits=[("SP", sp)], inc=("SA", 1))
                else:
                    sd = k.dve(lambda e, b=b, g=g, cnt=cnt: e.tensor_copy(
                        dst_fn(g * 4, cnt), bank[b][:, 0:cnt * 128]),
                        waits=[("SP", sp)], inc=("SD", 1))
            else:
                for j in range(cnt):
                    t = g * 4 + j
                    if use_act:
                        sa = k.act(lambda e, b=b, j=j, t=t: e.activation(
                            dst_fn(t, None), bank[b][:, j * 128:(j + 1) * 128],
                            AF.Identity, scale=fuse_scale(t)),
                            waits=[("SP", sp)], inc=("SA", 1))
                    else:
                        sd = k.dve(lambda e, b=b, j=j, t=t: e.tensor_scalar(
                            dst_fn(t, None), bank[b][:, j * 128:(j + 1) * 128],
                            fuse_scale(t), None, ALU.mult),
                            waits=[("SP", sp)], inc=("SD", 1))
            if use_act:
                bank_free[b] = ("SA", sa)
                last_sa = sa
            else:
                bank_free[b] = ("SD", sd)
        if last_sa is not None:
            k.dve(lambda e: e.nop(), waits=[("SA", last_sa)], inc=("SD", 1))
        return k.v("SD")

    sw0, sx0 = head_dmas(0)
    head_w_ready = {0: (sw0, sx0)}

    qfm_free = [0, 0]

    def emit_front(h, ctx):
        p = h % 2
        q_fm = q_fm2[p]
        sgb = sgb2[p]
        hidb = hidb2[p]
        wn_h = wn_h2[p]
        sw_h, sx_h = head_w_ready[h]

        dst = (q_fm, k_fm, v_fm)
        qkv_ready_a = None
        for st in range(4):
            for c in range(3):
                b = c
                for kc in range(NK):
                    sp = mm(b, lambda b=b: bank[b][:, :],
                            lambda p=p, c=c, kc=kc: qkvw[p][:, c * 2048 + kc * 128: c * 2048 + (kc + 1) * 128],
                            lambda kc=kc, st=st: xn_fm[:, kc * S + st * 512: kc * S + (st + 1) * 512],
                            kc == 0, kc == NK - 1,
                            waits=[("SW", sw_h), ("SD", xn_ready[st * 4 + 3]),
                                   (bank_free[b][0], bank_free[b][1])],
                            inc=("SP", 1) if kc == NK - 1 else None)
                sa = k.act(lambda e, c=c, st=st, b=b, dst=dst: e.activation(
                    dst[c][:, st * 512:(st + 1) * 512], bank[b][:, :], AF.Silu),
                    waits=[("SP", sp), ("SP", qfm_free[p])], inc=("SA", 1))
                bank_free[b] = ("SA", sa)
                yield
        qkv_ready_a = k.v("SA")

        for src, dstt in ((q_fm, q_tm), (k_fm, k_tm), (v_fm, v_tm)):
            tm_ready = transpose_batch(
                lambda t, src=src: src[:, t * 128:(t + 1) * 128],
                lambda t0, cnt, dstt=dstt: dstt[:, t0 * 128:(t0 + cnt) * 128],
                16, 3, 7, (0, qkv_ready_a))
            yield

        sdr = None
        for src_t, base in ((q_tm, 0), (k_tm, 16)):
            sa = k.act(lambda e, src_t=src_t: e.activation(sq_scrap[:, :], src_t[:, :], AF.Square),
                       waits=[("SD", tm_ready), ("SD", sdr or 0)], inc=("SA", 1))
            for t in range(16):
                sdr = k.dve(lambda e, t=t, base=base: e.tensor_reduce(
                    nrm32[:, base + t: base + t + 1], sq_scrap[:, t * 128:(t + 1) * 128],
                    AX, ALU.add),
                    waits=[("SA", sa)], inc=("SD", 1) if t == 15 else None)
        if debug and h == 0:
            k.gps(lambda e: e.dma_start(out=nrm_dbg[:, 0:32], in_=nrm32[:, :]),
                  waits=[("SD", sdr)], inc=("SST", 16))
            k.barrier_sst()
        sa = k.act(lambda e: e.activation(nrm32[:, :], nrm32[:, :], AF.Sqrt),
                   waits=[("SD", sdr)], inc=("SA", 1))
        yield
        sa = k.act(lambda e: e.activation(nrm32[:, :], nrm32[:, :], AF.Identity,
                                          bias=cst_sb[:, 1:2]),
                   waits=[("SA", sa)], inc=("SA", 1))
        sd = k.dve(lambda e: e.reciprocal(rr32b[:, :], nrm32[:, :]),
                   waits=[("SA", sa)], inc=("SD", 1))
        sa = k.act(lambda e: e.activation(rr32[:, :], rr32b[:, :], AF.Identity),
                   waits=[("SD", sd)], inc=("SA", 1))
        for t in range(16):
            k.dve(lambda e, t=t: e.tensor_scalar(q_tm[:, t * 128:(t + 1) * 128],
                                                 q_tm[:, t * 128:(t + 1) * 128],
                                                 rr32[:, t:t + 1], None, ALU.mult),
                  waits=[("SA", sa)])
            sd = k.dve(lambda e, t=t: e.tensor_scalar(k_tm[:, t * 128:(t + 1) * 128],
                                                      k_tm[:, t * 128:(t + 1) * 128],
                                                      rr32[:, 16 + t:17 + t], None, ALU.mult),
                       inc=("SD", 1))
        if debug and h == 0:
            k.gps(lambda e: e.dma_start(out=qtm_dbg, in_=q_tm[:, :]),
                  waits=[("SD", sd)], inc=("SST", 16))
            k.gps(lambda e: e.dma_start(out=nrm_dbg[:, 32:64], in_=rr32[:, :]),
                  waits=[("SA", sa)], inc=("SST", 16))
            k.barrier_sst()
        yield
        if h < NT:
            emit_router_tile(h)
        for src, dstf in ((q_tm, q_fm), (k_tm, k_fm)):
            fm_ready = transpose_batch(
                lambda t, src=src: src[:, t * 128:(t + 1) * 128],
                lambda t0, cnt, dstf=dstf: dstf[:, t0 * 128:(t0 + cnt) * 128],
                16, 3, 7, sd)
            yield

        if debug and h == 0:
            k.gps(lambda e: e.dma_start(out=qkv_dbg[0], in_=q_fm[:, :]),
                  waits=[("SD", fm_ready)], inc=("SST", 16))
            k.gps(lambda e: e.dma_start(out=qkv_dbg[1], in_=k_fm[:, :]),
                  waits=[("SD", fm_ready)], inc=("SST", 16))
            k.gps(lambda e: e.dma_start(out=qkv_dbg[2], in_=v_fm[:, :]),
                  waits=[("SD", fm_ready)], inc=("SST", 16))
            k.barrier_sst()

        sa_wn = k.act(lambda e, p=p: e.activation(wnsq[:, :], wnat[p][:, :], AF.Square),
                      waits=[("SX", sx_h), ("SD", wnh_free[p])], inc=("SA", 1))
        sd_wn = None
        for i in range(3):
            sd_wn = k.dve(lambda e, i=i, wn_h=wn_h: e.tensor_reduce(wn_h[:, i:i + 1],
                                                         wnsq[:, i * 128:(i + 1) * 128],
                                                         AX, ALU.add),
                          waits=[("SA", sa_wn)], inc=("SD", 1) if i == 2 else None)
        sa_wn = k.act(lambda e, wn_h=wn_h: e.activation(wn_h[:, :], wn_h[:, :], AF.Sqrt),
                      waits=[("SD", sd_wn)], inc=("SA", 1))
        yield

        yield "drain"
        if h + 1 < n_heads:
            head_w_ready[h + 1] = head_dmas(h + 1)
        dw_prev = 0
        sgrp_free = 0
        for sc in range(4):
            cs = slice(sc * 512, (sc + 1) * 512)
            spg = mm(0, lambda: bank[0][:, :], lambda p=p: wT[p][:, 0:128],
                     lambda cs=cs: k_fm[:, cs], True, True,
                     waits=[("SD", fm_ready), ("SW", sw_h),
                            (bank_free[0][0], bank_free[0][1])], inc=("SP", 1))
            sph = mm(1, lambda: bank[1][:, :], lambda p=p: wT[p][:, 128:256],
                     lambda cs=cs: k_fm[:, cs], True, True,
                     waits=[(bank_free[1][0], bank_free[1][1])], inc=("SP", 1))
            spd = mm(2, lambda: bank[2][:, :], lambda p=p: w1bf[p][:, :],
                     lambda cs=cs: v_fm[:, cs], True, True,
                     waits=[(bank_free[2][0], bank_free[2][1])], inc=("SP", 1))
            sa1 = k.act(lambda e: e.activation(sgb[:, :], bank[0][:, :], AF.Silu),
                        waits=[("SP", spg), ("SD", sgrp_free)], inc=("SA", 1))
            sa2 = k.act(lambda e: e.activation(tanb[:, :], bank[0][:, :], AF.Tanh, scale=0.5),
                        inc=("SA", 1))
            sa3 = k.act(lambda e: e.activation(hbb[:, :], bank[1][:, :], AF.Copy),
                        waits=[("SP", sph)], inc=("SA", 1))
            sa4 = k.act(lambda e: e.activation(dhb[:, :], bank[2][:, :], AF.Copy),
                        waits=[("SP", spd)], inc=("SA", 1))
            bank_free[1] = ("SA", sa3)
            bank_free[2] = ("SA", sa4)
            k.dve(lambda e: e.tensor_mul(hidb[:, :], sgb[:, :], hbb[:, :]),
                  waits=[("SA", sa3), ("SP", dw_prev)])
            k.dve(lambda e: e.tensor_tensor(dgb[:, :], dhb[:, :], hbb[:, :], ALU.mult),
                  waits=[("SA", sa4)])
            k.dve(lambda e: e.tensor_mul(dbmb[:, :], dhb[:, :], sgb[:, :]))
            k.dve(lambda e: e.tensor_scalar(sigb[:, :], tanb[:, :], 0.5, 0.5, ALU.mult, ALU.add))
            k.dve(lambda e: e.tensor_scalar(tanb[:, :], tanb[:, :], -1.0, 1.0, ALU.mult, ALU.add))
            sdu = k.dve(lambda e: e.scalar_tensor_tensor(ub[:, :], bank[0][:, :], 0.5, tanb[:, :],
                                                         ALU.mult, ALU.mult), inc=("SD", 1))
            bank_free[0] = ("SD", sdu)
            k.dve(lambda e: e.scalar_tensor_tensor(sigb[:, :], ub[:, :], 1.0, sigb[:, :],
                                                   ALU.add, ALU.mult))
            sd_dga = k.dve(lambda e: e.tensor_tensor(dgab[:, :], dgb[:, :], sigb[:, :], ALU.mult),
                           inc=("SD", 1))
            sd_hid = sd_dga
            sgrp_free = sd_dga
            for j in range(4):
                t = sc * 4 + j
                k.dve(lambda e, j=j, t=t, h=h: e.tensor_scalar(
                    kl0t[:, j * 128:(j + 1) * 128], k_tm[:, t * 128:(t + 1) * 128],
                    lr_tm[:, t * 48 + h * 3: t * 48 + h * 3 + 1], None, ALU.mult),
                    waits=[("SD", lr_ready), ("SP", dw_prev)])
                sd_kl = k.dve(lambda e, j=j, t=t, h=h: e.tensor_scalar(
                    kl2t[:, j * 128:(j + 1) * 128], k_tm[:, t * 128:(t + 1) * 128],
                    lr_tm[:, t * 48 + h * 3 + 2: t * 48 + h * 3 + 3], None, ALU.mult),
                    inc=("SD", 1))
            transpose_batch(lambda t: hidb[:, (t % 4) * 128:(t % 4 + 1) * 128],
                            lambda t, _ : hl1t[:, (t % 4) * 128:(t % 4 + 1) * 128],
                            4, 3, 7, sd_hid,
                            fuse_scale=lambda t, sc=sc, h=h: lr_tm[:, (sc * 4 + t) * 48 + h * 3 + 1:
                                                                   (sc * 4 + t) * 48 + h * 3 + 2])
            transpose_batch(lambda t: dgab[:, (t % 4) * 128:(t % 4 + 1) * 128],
                            lambda t0, cnt: dgat[:, t0 * 128:(t0 + cnt) * 128],
                            4, 3, 7, sd_dga)
            sd_t = transpose_batch(lambda t: dbmb[:, (t % 4) * 128:(t % 4 + 1) * 128],
                                   lambda t0, cnt: dbmt[:, t0 * 128:(t0 + cnt) * 128],
                                   4, 3, 7, sd_dga)
            for j in range(4):
                st_flag = (sc == 0 and j == 0)
                sp_flag = (sc == 3 and j == 3)
                w0w = [("SD", max(sd_t, sd_kl)), (bank_free[4][0], bank_free[4][1])]
                mm(4, lambda: bank[4][:, 0:128],
                   lambda j=j: dgat[:, j * 128:(j + 1) * 128],
                   lambda j=j: kl0t[:, j * 128:(j + 1) * 128], st_flag, sp_flag,
                   waits=w0w, inc=("SP", 1) if sp_flag else None)
                mm(5, lambda: bank[5][:, 0:128],
                   lambda sc=sc, j=j: v_tm[:, (sc * 4 + j) * 128:(sc * 4 + j + 1) * 128],
                   lambda j=j: hl1t[:, j * 128:(j + 1) * 128], st_flag, sp_flag,
                   waits=[(bank_free[5][0], bank_free[5][1])],
                   inc=("SP", 1) if sp_flag else None)
                last = (j == 3)
                sp_dw = mm(6, lambda: bank[6][:, 0:128],
                           lambda j=j: dbmt[:, j * 128:(j + 1) * 128],
                           lambda j=j: kl2t[:, j * 128:(j + 1) * 128], st_flag, sp_flag,
                           waits=[(bank_free[6][0], bank_free[6][1])],
                           inc=("SP", 1) if last else None)
                if last:
                    dw_prev = sp_dw
            yield
        ctx.update(p=p, q_fm=q_fm, sgb=sgb, hidb=hidb, wn_h=wn_h, sp_dw=sp_dw,
                   sa_wn=sa_wn, h=h)

    def emit_back(h, ctx):
        p = ctx["p"]
        q_fm = ctx["q_fm"]
        sgb = ctx["sgb"]
        hidb = ctx["hidb"]
        wn_h = ctx["wn_h"]
        sp_dw = ctx["sp_dw"]
        sa_wn = ctx["sa_wn"]
        nsb = [(3, 4), (5, 6), (7, 3)]
        sd_x = None
        for i in range(3):
            sd_x = k.dve(lambda e, i=i: e.tensor_copy(Xb[i][:, :], bank[4 + i][:, 0:128]),
                         waits=[("SP", sp_dw)], inc=("SD", 1))
        for i in range(3):
            bank_free[4 + i] = ("SD", sd_x)
        if debug and h == 0:
            for i in range(3):
                k.gps(lambda e, i=i: e.dma_start(out=dw_dbg[i], in_=Xb[i][:, :]),
                      waits=[("SD", sd_x)], inc=("SST", 16))
                k.barrier_sst()
        sa_scale = [0, 0, 0]
        for i in range(3):
            ba = nsb[i][0]
            sa = k.act(lambda e, i=i: e.activation(sqG[:, i * 128:(i + 1) * 128], Xb[i][:, :],
                                                   AF.Square),
                       waits=[("SD", sd_x)], inc=("SA", 1))
            sd = k.dve(lambda e, i=i: e.tensor_reduce(facb[i][:, :], sqG[:, i * 128:(i + 1) * 128],
                                                      AX, ALU.add),
                       waits=[("SA", sa)], inc=("SD", 1))
            sd = k.dve(lambda e, i=i: e.tensor_copy(dwssb[i][:, :], facb[i][:, :]),
                       waits=[("SD", sd)], inc=("SD", 1))
            sp = mm(ba, lambda ba=ba: bank[ba][0:1, 0:1], lambda i=i: dwssb[i][:, :],
                    lambda: onc_sb[:, :], True, True,
                    waits=[("SD", sd), (bank_free[ba][0], bank_free[ba][1])], inc=("SP", 1))
            sa = k.act(lambda e, i=i, ba=ba: e.activation(smll[i][0:1, 0:1], bank[ba][0:1, 0:1],
                                                          AF.Sqrt),
                       waits=[("SP", sp)], inc=("SA", 1))
            sa = k.act(lambda e, i=i: e.activation(smll[i][0:1, 0:1], smll[i][0:1, 0:1],
                                                   AF.Identity, bias=cst_sb[0:1, 3:4]),
                       waits=[("SA", sa)], inc=("SA", 1))
            sd = k.dve(lambda e, i=i: e.reciprocal(smll[i][0:1, 1:2], smll[i][0:1, 0:1]),
                       waits=[("SA", sa)], inc=("SD", 1))
            sd = k.dve(lambda e, i=i: e.tensor_copy(smlb[i][0:1, 0:1], smll[i][0:1, 1:2]),
                       waits=[("SD", sd)], inc=("SD", 1))
            sp = mm(ba, lambda ba=ba: bank[ba][:, 128:129], lambda: onr_sb[:, :],
                    lambda i=i: smlb[i][0:1, 0:1], True, True,
                    waits=[("SD", sd)], inc=("SP", 1))
            sd = k.dve(lambda e, i=i, ba=ba: e.tensor_copy(facb[i][:, :], bank[ba][:, 128:129]),
                       waits=[("SP", sp)], inc=("SD", 1))
            bank_free[ba] = ("SD", sd)
            sa_scale[i] = k.act(lambda e, i=i: e.activation(Xb[i][:, :], Xb[i][:, :], AF.Identity,
                                                            scale=facb[i][:, 0:1]),
                                waits=[("SD", sd)], inc=("SA", 1))
            yield
        chain = [("SA", sa_scale[i]) for i in range(3)]
        for it, (ca, cb, cc) in enumerate(NS_COEFS):
            for i in range(3):
                ba, bb2 = nsb[i]
                sp = mm(ba, lambda ba=ba: bank[ba][:, 0:128], lambda i=i: Xb[i][:, :],
                        lambda: id_sb[:, :], True, True,
                        waits=[chain[i], (bank_free[ba][0], bank_free[ba][1])], inc=("SP", 1))
                sd = k.dve(lambda e, i=i, ba=ba: e.tensor_copy(XTb[i][:, :], bank[ba][:, 0:128]),
                           waits=[("SP", sp)], inc=("SD", 1))
                bank_free[ba] = ("SD", sd)
                yield
                sp = mm(bb2, lambda bb2=bb2: bank[bb2][:, 0:128], lambda i=i: XTb[i][:, :],
                        lambda i=i: XTb[i][:, :], True, True,
                        waits=[("SD", sd), (bank_free[bb2][0], bank_free[bb2][1])], inc=("SP", 1))
                sd = k.dve(lambda e, i=i, bb2=bb2: e.tensor_copy(Ab[i][:, :], bank[bb2][:, 0:128]),
                           waits=[("SP", sp)], inc=("SD", 1))
                bank_free[bb2] = ("SD", sd)
                yield
                sp = mm(ba, lambda ba=ba: bank[ba][:, 0:128], lambda i=i: Ab[i][:, :],
                        lambda i=i: Ab[i][:, :], True, True,
                        waits=[("SD", sd), (bank_free[ba][0], bank_free[ba][1])], inc=("SP", 1))
                k.dve(lambda e, i=i, cb=cb: e.tensor_scalar(Abf[i][:, :], Ab[i][:, :], cb, None,
                                                            ALU.mult))
                sd = k.dve(lambda e, i=i, cc=cc, ba=ba: e.scalar_tensor_tensor(
                    Bb[i][:, :], bank[ba][:, 0:128], cc, Abf[i][:, :], ALU.mult, ALU.add),
                    waits=[("SP", sp)], inc=("SD", 1))
                bank_free[ba] = ("SD", sd)
                yield
                sp = mm(bb2, lambda bb2=bb2: bank[bb2][:, 0:128], lambda i=i: Bb[i][:, :],
                        lambda i=i: Xb[i][:, :], True, True,
                        waits=[("SD", sd), (bank_free[bb2][0], bank_free[bb2][1])], inc=("SP", 1))
                sd = k.dve(lambda e, i=i, ca=ca, bb2=bb2: e.scalar_tensor_tensor(
                    Xb[i][:, :], Xb[i][:, :], ca, bank[bb2][:, 0:128], ALU.mult, ALU.add),
                    waits=[("SP", sp)], inc=("SD", 1))
                bank_free[bb2] = ("SD", sd)
                chain[i] = ("SD", sd)
                yield

        for i in range(3):
            sd = k.dve(lambda e, i=i, p=p: e.tensor_tensor(
                Wf32[:, :], wnat[p][:, i * 128:(i + 1) * 128], Xb[i][:, :], ALU.add),
                inc=("SD", 1))
            sa = k.act(lambda e, i=i: e.activation(sqG[:, i * 128:(i + 1) * 128], Wf32[:, :],
                                                   AF.Square),
                       waits=[("SD", sd)], inc=("SA", 1))
            sd = k.dve(lambda e, i=i: e.tensor_reduce(rn3[:, 0:1], sqG[:, i * 128:(i + 1) * 128],
                                                      AX, ALU.add),
                       waits=[("SA", sa)], inc=("SD", 1))
            sa = k.act(lambda e: e.activation(rn3[:, 1:2], rn3[:, 0:1], AF.Sqrt),
                       waits=[("SD", sd)], inc=("SA", 1))
            sa = k.act(lambda e: e.activation(rn3[:, 1:2], rn3[:, 1:2], AF.Identity,
                                              bias=cst_sb[:, 1:2]),
                       waits=[("SA", sa)], inc=("SA", 1))
            sd = k.dve(lambda e: e.reciprocal(rn3[:, 2:3], rn3[:, 1:2]),
                       waits=[("SA", sa)], inc=("SD", 1))
            sd = k.dve(lambda e, i=i: e.tensor_tensor(rn3[:, 2:3], rn3[:, 2:3], wn_h[:, i:i + 1],
                                                      ALU.mult),
                       waits=[("SA", sa_wn), ("SD", sd)], inc=("SD", 1))
            wnh_free[p] = sd
            sa = k.act(lambda e, i=i: e.activation(Wnb[i][:, :], Wf32[:, :], AF.Identity,
                                                   scale=rn3[:, 2:3]),
                       waits=[("SD", sd)], inc=("SA", 1))
            bh = (3, 5, 7)[i]
            sp = mm(bh, lambda bh=bh: bank[bh][:, 0:128], lambda i=i: Wnb[i][:, :],
                    lambda: id_sb[:, :],
                    True, True, waits=[("SA", sa), (bank_free[bh][0], bank_free[bh][1])],
                    inc=("SP", 1))
            sd = k.dve(lambda e, i=i, bh=bh: e.tensor_copy(WnT[i][:, :], bank[bh][:, 0:128]),
                       waits=[("SP", sp)], inc=("SD", 1))
            bank_free[bh] = ("SD", sd)
            yield
        wn_ready = k.v("SD")
        if debug and h == 0:
            for i in range(3):
                k.gps(lambda e, i=i: e.dma_start(out=wn_dbg[i], in_=Wnb[i][:, :]),
                      waits=[("SA", k.v("SA"))], inc=("SST", 16))
            k.barrier_sst()

        for st in range(4):
            cs = slice(st * 512, (st + 1) * 512)
            sp0 = mm(4, lambda: bank[4][:, :], lambda: WnT[0][:, :], lambda cs=cs, q_fm=q_fm: q_fm[:, cs],
                     True, True, waits=[("SD", wn_ready), (bank_free[4][0], bank_free[4][1])],
                     inc=("SP", 1))
            sp2 = mm(5, lambda: bank[5][:, :], lambda: WnT[2][:, :], lambda cs=cs, q_fm=q_fm: q_fm[:, cs],
                     True, True, waits=[(bank_free[5][0], bank_free[5][1])], inc=("SP", 1))
            sa = k.act(lambda e: e.activation(sgb[:, :], bank[4][:, :], AF.Silu),
                       waits=[("SP", sp0), ("SD", wn_ready)], inc=("SA", 1))
            bank_free[4] = ("SA", sa)
            sd = k.dve(lambda e: e.tensor_tensor(hidb[:, :], sgb[:, :], bank[5][:, :], ALU.mult),
                       waits=[("SA", sa), ("SP", sp2)], inc=("SD", 1))
            bank_free[5] = ("SD", sd)
            sp = mm(6, lambda: bank[6][:, :], lambda: WnT[1][:, :], lambda: hidb[:, :],
                    True, True, waits=[("SD", sd), (bank_free[6][0], bank_free[6][1])],
                    inc=("SP", 1))
            sa = k.act(lambda e, cs=cs: e.activation(o2_bf[:, cs], bank[6][:, :], AF.Square),
                       waits=[("SP", sp)], inc=("SA", 1))
            sd = k.dve(lambda e, cs=cs: e.tensor_copy(o_bf[:, cs], bank[6][:, :]),
                       waits=[("SA", sa)], inc=("SD", 1))
            bank_free[6] = ("SD", sd)
            yield
        qfm_free[p] = k.v("SP")
        o_ready = (k.v("SD"), k.v("SA"))
        if debug and h == 0:
            k.gps(lambda e: e.dma_start(out=o_dbg, in_=o_bf[:, :]),
                  waits=[("SD", o_ready[0])], inc=("SST", 16))
            k.barrier_sst()
        for t in range(16):
            sp = mm(7, lambda t=t: bank[7][:, t:t + 1],
                    lambda t=t: o2_bf[:, t * 128:(t + 1) * 128], lambda: onc_sb[:, :],
                    True, True,
                    waits=[("SA", o_ready[1]), (bank_free[7][0], bank_free[7][1])],
                    inc=("SP", 1) if t == 15 else None)
        sd = k.dve(lambda e: e.tensor_scalar(dsum[:, :], bank[7][:, 0:16], 1.0 / 128, 1e-5,
                                             ALU.mult, ALU.add),
                   waits=[("SP", sp)], inc=("SD", 1))
        bank_free[7] = ("SD", sd)
        sa = k.act(lambda e: e.activation(dsum[:, :], dsum[:, :], AF.Sqrt),
                   waits=[("SD", sd)], inc=("SA", 1))
        sd = k.dve(lambda e: e.reciprocal(rmfac[:, :], dsum[:, :]), waits=[("SA", sa)],
                   inc=("SD", 1))
        sa = k.act(lambda e: e.activation(rmfacb[:, :], rmfac[:, :], AF.Identity),
                   waits=[("SD", sd)], inc=("SA", 1))
        rm_ready = sa
        yield
        transpose_batch(lambda t: o_bf[:, t * 128:(t + 1) * 128],
                        lambda t, _: o_tm[:, t * 128:(t + 1) * 128],
                        16, 3, 7, (o_ready[0], rm_ready),
                        fuse_scale=lambda t: rmfacb[:, t:t + 1])
        yield
        sd = transpose_batch(lambda t: o_tm[:, t * 128:(t + 1) * 128],
                             lambda t0, cnt: o2_bf[:, t0 * 128:(t0 + cnt) * 128],
                             16, 3, 7, k.v("SD"))
        yield
        sx = k.syn(lambda e, h=h: e.dma_start(out=oscr_d[h], in_=o2_bf[:, :]),
                   waits=[("SD", sd)], inc=("SX", 16))
        oscr_done.append(sx)
        if debug:
            k.gps(lambda e, h=h: e.dma_start(out=oscr_dbg[h], in_=o2_bf[:, :]),
                  waits=[("SD", sd)], inc=("SST", 16))
        hw_free[p] = (k.v("SP"), k.v("SA"), k.v("SD"))

    back_it = None
    for h in range(n_heads):
        ctx = {}
        for tok in emit_front(h, ctx):
            if tok == "drain" and back_it is not None:
                for _ in back_it:
                    pass
                back_it = None
            elif back_it is not None:
                for _ in range(8):
                    if next(back_it, "_end") == "_end":
                        back_it = None
                        break
        if back_it is not None:
            for _ in back_it:
                pass
            back_it = None
        back_it = emit_back(h, ctx)
    if back_it is not None:
        for _ in back_it:
            pass
    for tt in range(n_heads, NT):
        emit_router_tile(tt)
    c_ready = k.v("SA")

    at_es.close()
    k.phase_barrier()

    op_es = contextlib.ExitStack()
    C = op_es.enter_context
    wob = [C(nc.sbuf_tensor(f"wob{i}", [128, NK * 512], BF16)) for i in range(2)]
    ostr = [C(nc.sbuf_tensor(f"ostr{i}", [128, H * 128], BF16)) for i in range(2)]
    oyb = [C(nc.sbuf_tensor(f"oyb{i}", [128, 512], F32)) for i in range(2)]
    wob_free = [0, 0]
    ostr_free = [0, 0]
    oyb_free = [0, 0]
    oyb_free_sem = [("SC", 0), ("SC", 0)]
    oscr_all = oscr_done[-1] if oscr_done else 0

    for m2t in range(4):
        pw = m2t % 2
        sw = k.gps(lambda e, m2t=m2t, pw=pw: e.dma_start(
            out=wob[pw][:, :].rearrange("p (c m) -> p c m", c=NK),
            in_=wo_d[:, m2t * 512:(m2t + 1) * 512].rearrange("(c p) m -> p c m", p=128)),
            waits=[("SP", wob_free[pw])], inc=("SW", 16))
        sd_w = k.dve(lambda e, pw=pw: e.tensor_scalar(wob[pw][:, :], wob[pw][:, :],
                                                      onw_sb[:, 0:1], None, ALU.mult),
                     waits=[("SW", sw), ("SX", sx_const)], inc=("SD", 1))
        for tt in range(NT):
            po = tt % 2
            sxo = k.syn(lambda e, tt=tt, po=po: e.dma_start(
                out=ostr[po][:, :].rearrange("p (hh t) -> p hh t", hh=H),
                in_=oscr_d[:, :, tt * 128:(tt + 1) * 128].rearrange("hh d t -> d hh t")),
                waits=[("SX", oscr_all), ("SP", ostr_free[po])], inc=("SX", 16))
            b = tt % 2
            for kc in range(H):
                sp = mm(b, lambda b=b: bank[b][:, :],
                        lambda po=po, kc=kc: ostr[po][:, kc * 128:(kc + 1) * 128],
                        lambda pw=pw, kc=kc: wob[pw][:, kc * 512:(kc + 1) * 512],
                        kc == 0, kc == H - 1,
                        waits=[("SX", sxo), ("SD", sd_w), (bank_free[b][0], bank_free[b][1])],
                        inc=("SP", 1) if kc == H - 1 else None)
            ostr_free[po] = sp
            wob_free[pw] = sp
            sd = k.dve(lambda e, b=b, po=po: e.tensor_copy(oyb[po][:, :], bank[b][:, :]),
                       waits=[("SP", sp), oyb_free_sem[po]], inc=("SD", 1))
            bank_free[b] = ("SD", sd)
            sc = k.syn(lambda e, tt=tt, m2t=m2t, po=po: e.dma_start(
                out=acc_d[tt * 128:(tt + 1) * 128, m2t * 512:(m2t + 1) * 512], in_=oyb[po][:, :]),
                waits=[("SD", sd)], inc=("SC", 16))
            oyb_free_sem[po] = ("SC", sc)
    op_es.close()
    k.phase_barrier()
    attn_acc_done = k.v("SC")

    ff_es = contextlib.ExitStack()
    C = ff_es.enter_context
    h_sb = C(nc.sbuf_tensor("h_sb", [128, NJ * S], BF16))
    wdnq = [C(nc.sbuf_tensor(f"wdnq{i}", [128, NJ * 512], BF16)) for i in range(2)]
    gpair = [C(nc.sbuf_tensor(f"gpair{i}", [128, NK * 256], BF16)) for i in range(2)]
    fsg = [C(nc.sbuf_tensor(f"fsg{i}", [128, 512], F32)) for i in range(2)]
    fsg_free = [0, 0]
    fyb = [C(nc.sbuf_tensor(f"fyb{i}", [128, 512], BF16)) for i in range(4)]
    gpair_free = [0, 0]
    wdnq_free = [0, 0]
    fyb_free = [("SC", 0)] * 4
    h_free = 0

    def gu_dma(u, j):
        pg = j % 2
        k.gps(lambda e, u=u, j=j, pg=pg: e.dma_start(
            out=gpair[pg][:, :].rearrange("p (c m) -> p c m", c=NK),
            in_=wgu_d[u, :, j * 256:(j + 1) * 256].rearrange("(c p) m -> p c m", p=128)),
            waits=[("SP", gpair_free[pg])], inc=("SW", 16))
        return k.v("SW")

    for u in range(n_units):
        sw_g = {0: gu_dma(u, 0)}
        for j in range(NJ):
            pg = j % 2
            if j + 1 < NJ:
                sw_g[j + 1] = gu_dma(u, j + 1)
            for st in range(4):
                bg, bu = st % 2, 2 + st % 2
                for kc in range(NK):
                    spg = mm(bg, lambda bg=bg: bank[bg][:, :],
                             lambda pg=pg, kc=kc: gpair[pg][:, kc * 256: kc * 256 + 128],
                             lambda kc=kc, st=st: xn_fm[:, kc * S + st * 512: kc * S + (st + 1) * 512],
                             kc == 0, kc == NK - 1,
                             waits=[("SW", sw_g[j]), (bank_free[bg][0], bank_free[bg][1])],
                             inc=("SP", 1) if kc == NK - 1 else None)
                for kc in range(NK):
                    spu = mm(bu, lambda bu=bu: bank[bu][:, :],
                             lambda pg=pg, kc=kc: gpair[pg][:, kc * 256 + 128: kc * 256 + 256],
                             lambda kc=kc, st=st: xn_fm[:, kc * S + st * 512: kc * S + (st + 1) * 512],
                             kc == 0, kc == NK - 1,
                             waits=[(bank_free[bu][0], bank_free[bu][1])],
                             inc=("SP", 1) if kc == NK - 1 else None)
                ps = st % 2
                sa = k.act(lambda e, ps=ps, bg=bg: e.activation(fsg[ps][:, :], bank[bg][:, :], AF.Silu),
                           waits=[("SP", spg), ("SD", fsg_free[ps])], inc=("SA", 1))
                bank_free[bg] = ("SA", sa)
                sd = k.dve(lambda e, ps=ps, bu=bu, j=j, st=st: e.tensor_tensor(
                    h_sb[:, j * S + st * 512: j * S + (st + 1) * 512], fsg[ps][:, :],
                    bank[bu][:, :], ALU.mult),
                    waits=[("SA", sa), ("SP", spu), ("SP", h_free)], inc=("SD", 1))
                bank_free[bu] = ("SD", sd)
                fsg_free[ps] = sd
            gpair_free[pg] = k.v("SP")
        h_ready = k.v("SD")
        if debug and u == 0:
            k.gps(lambda e: e.dma_start(out=h_dbg, in_=h_sb[:, :]),
                  waits=[("SD", h_ready)], inc=("SST", 16))
            k.barrier_sst()
        for m2t in range(4):
            pw = (u * 4 + m2t) % 2
            sw_d = k.gps(lambda e, u=u, m2t=m2t, pw=pw: e.dma_start(
                out=wdnq[pw][:, :].rearrange("p (c m) -> p c m", c=NJ),
                in_=wdn_d[u, :, m2t * 512:(m2t + 1) * 512].rearrange("(c p) m -> p c m", p=128)),
                waits=[("SP", wdnq_free[pw])], inc=("SW", 16))
            for tt in range(NT):
                b = 4 + tt % 2
                for j in range(NJ):
                    sp = mm(b, lambda b=b: bank[b][:, :],
                            lambda j=j, tt=tt: h_sb[:, j * S + tt * 128: j * S + (tt + 1) * 128],
                            lambda j=j, pw=pw: wdnq[pw][:, j * 512:(j + 1) * 512],
                            j == 0, j == NJ - 1,
                            waits=[("SW", sw_d), ("SD", h_ready),
                                   (bank_free[b][0], bank_free[b][1])],
                            inc=("SP", 1) if j == NJ - 1 else None)
                wdnq_free[pw] = sp
                py = tt % 4
                ccol = (lambda tt=tt, u=u: c_all[:, tt * 17 + u: tt * 17 + u + 1]) if u < 16 \
                    else (lambda tt=tt: c_all[:, tt * 17 + 16: tt * 17 + 17])
                sd = k.dve(lambda e, b=b, py=py, ccol=ccol: e.tensor_scalar(
                    fyb[py][:, :], bank[b][:, :], ccol(), None, ALU.mult),
                    waits=[("SP", sp), ("SA", c_ready), fyb_free[py]], inc=("SD", 1))
                bank_free[b] = ("SD", sd)
                sc = k.syn(lambda e, u=u, tt=tt, m2t=m2t, py=py: e.dma_start(
                    out=yscr_d[u, tt * 128:(tt + 1) * 128, m2t * 512:(m2t + 1) * 512],
                    in_=fyb[py][:, :]),
                    waits=[("SD", sd)], inc=("SC", 16))
                fyb_free[py] = ("SC", sc)
        h_free = k.v("SP")
    ff_es.close()
    k.phase_barrier()
    acc_all = k.v("SC")

    fn_es = contextlib.ExitStack()
    C = fn_es.enter_context
    fx = [C(nc.sbuf_tensor(f"fx{i}", [128, 2048], F32)) for i in range(2)]
    fa = [C(nc.sbuf_tensor(f"fa{i}", [128, 2048], F32)) for i in range(2)]
    fy = [C(nc.sbuf_tensor(f"fy{i}", [128, 2048], BF16)) for i in range(4)]
    f_free = [0, 0]
    fa_free = [0, 0]
    fy_free = [0] * 4
    for tt in range(NT):
        p = tt % 2
        sx1 = k.syn(lambda e, tt=tt, p=p: e.dma_start(out=fx[p][:, :],
                                                      in_=x_d[tt * 128:(tt + 1) * 128, :]),
                    waits=[("SST", f_free[p])], inc=("SX", 16))
        sx2 = k.syn(lambda e, tt=tt, p=p: e.dma_start(out=fa[p][:, :],
                                                      in_=acc_d[tt * 128:(tt + 1) * 128, :]),
                    waits=[("SC", acc_all), ("SD", fa_free[p])], inc=("SX", 16))
        sd = k.dve(lambda e, p=p: e.tensor_add(fx[p][:, :], fx[p][:, :], fa[p][:, :]),
                   waits=[("SX", sx2)], inc=("SD", 1))
        fa_free[p] = sd
        for u in range(n_units):
            uy = u % 4
            sxy = k.syn(lambda e, u=u, tt=tt, uy=uy: e.dma_start(
                out=fy[uy][:, :], in_=yscr_d[u, tt * 128:(tt + 1) * 128, :]),
                waits=[("SD", fy_free[uy])], inc=("SX", 16))
            sd = k.dve(lambda e, p=p, uy=uy: e.tensor_tensor(fx[p][:, :], fx[p][:, :],
                                                             fy[uy][:, :], ALU.add),
                       waits=[("SX", sxy)], inc=("SD", 1))
            fy_free[uy] = sd
        st = k.syn(lambda e, tt=tt, p=p: e.dma_start(out=out_d[tt * 128:(tt + 1) * 128, :],
                                                     in_=fx[p][:, :]),
                   waits=[("SD", sd)], inc=("SST", 16))
        f_free[p] = st
    fn_es.close()

    sst_final = k.v("SST")
    with nc.Block() as blk:
        blk.tensor(lambda e: [f(e) for f in k.ops["tensor"]])
        blk.scalar(lambda e: [f(e) for f in k.ops["scalar"]])
        blk.vector(lambda e: [f(e) for f in k.ops["vector"]])
        blk.gpsimd(lambda e: [f(e) for f in k.ops["gpsimd"]])

        def syncf(e):
            for f in k.ops["sync"]:
                f(e)
            e.wait_ge(k.sems["SST"], sst_final)
        blk.sync(syncf)
    return nc


def prep_inputs(L0_attn_x, L0_attn_to_qkv_w, L0_attn_o_proj_w, L0_attn_lr_proj_w,
                L0_attn_w0, L0_attn_w1, L0_attn_w2, L0_attn_o_norm_w,
                L0_ffn_gate_w, L0_ffn_expert_gate_up, L0_ffn_expert_down,
                L0_ffn_shared_gate_up, L0_ffn_shared_down, L0_ffn_shared_gate_vec):
    f = np.float32
    ac = np.ascontiguousarray
    wqkv_t = ac(np.asarray(L0_attn_to_qkv_w, f).T)
    wo_t = ac(np.asarray(L0_attn_o_proj_w, f).T)
    wlr_t = ac(np.asarray(L0_attn_lr_proj_w, f).T)
    w0 = np.asarray(L0_attn_w0, f)
    w1 = np.asarray(L0_attn_w1, f)
    w2 = np.asarray(L0_attn_w2, f)
    w_nat = ac(np.stack([w0, w1, w2], 0))
    w_t = ac(np.stack([w0.transpose(0, 2, 1), w2.transpose(0, 2, 1)], 0))
    onw = ac(np.asarray(L0_attn_o_norm_w, f).reshape(128, 1))
    wgate_t = np.zeros((D, 32), f)
    wgate_t[:, 0:16] = np.asarray(L0_ffn_gate_w, f).T
    wgate_t[:, 16:17] = np.asarray(L0_ffn_shared_gate_vec, f).T
    egu = np.asarray(L0_ffn_expert_gate_up, f)
    sgu = np.asarray(L0_ffn_shared_gate_up, f)
    edn = np.asarray(L0_ffn_expert_down, f)
    sdn = np.asarray(L0_ffn_shared_down, f)
    perm = np.empty(2816, np.int64)
    for j in range(NJ):
        perm[j * 256: j * 256 + 128] = np.arange(j * 128, (j + 1) * 128)
        perm[j * 256 + 128: (j + 1) * 256] = np.arange(EI + j * 128, EI + (j + 1) * 128)
    units_gu = np.empty((NU, D, 2816), f)
    units_dn = np.empty((NU, EI, D), f)
    for e in range(16):
        units_gu[e] = egu[e][perm].T
        units_dn[e] = edn[e].T
    for j in range(4):
        blk_g = sgu[j * EI:(j + 1) * EI]
        blk_u = sgu[5632 + j * EI: 5632 + (j + 1) * EI]
        gu = np.concatenate([blk_g, blk_u], 0)
        units_gu[16 + j] = gu[perm].T
        units_dn[16 + j] = sdn[:, j * EI:(j + 1) * EI].T
    shared = {
        "wqkv_t": wqkv_t, "wo_t": wo_t, "wlr_t": wlr_t, "w_nat": w_nat,
        "w_t": w_t, "onw": onw, "wgate_t": wgate_t,
        "wgu_t": ac(units_gu), "wdn_t": ac(units_dn),
    }
    xs = np.asarray(L0_attn_x, f)
    return xs, shared


def kernel(**inputs):
    xs, shared = prep_inputs(**inputs)
    nc = build()
    in_maps = [dict(shared, x=np.ascontiguousarray(xs[b])) for b in range(8)]
    res = run_bass_kernel_spmd(nc, in_maps, core_ids=list(range(8)))
    return np.stack([r["out"] for r in res.results], axis=0)


# version 14 (final)
# speedup vs baseline: 1.3138x; 1.0847x over previous
import math
import numpy as np
import ml_dtypes
import concourse.bass as bass
import concourse.mybir as mybir
from concourse.bass_utils import run_bass_kernel_spmd

F32 = mybir.dt.float32
BF16 = mybir.dt.bfloat16
AF = mybir.ActivationFunctionType
ALU = mybir.AluOpType
AX = mybir.AxisListType.X

S = 2048
D = 2048
H = 16
HD = 128
NT = S // 128
NK = D // 128
NU = 20
EI = 1408
NJ = EI // 128
BASE_LR_INV = float(0.01 + math.log(-math.expm1(-0.01)))
NS_COEFS = [(4.0848, -6.8946, 2.927), (3.9505, -6.3029, 2.6377),
            (3.7418, -5.5913, 2.3037), (2.8769, -3.1427, 1.2046),
            (2.8366, -3.0525, 1.2012)]

ENGS = ("tensor", "scalar", "vector", "sync", "gpsimd")


class Emit:

    def __init__(self, nc):
        self.nc = nc
        self.ops = {e: [] for e in ENGS}
        self.semv = {}
        self.sems = {}
        self.hw = {e: {} for e in ENGS}

    def sem(self, name):
        self.sems[name] = self.nc.alloc_semaphore(name)
        self.semv[name] = 0

    def _op(self, eng, fn, waits, inc):
        tgt = None
        if inc is not None:
            s, n = inc
            self.semv[s] += n
            tgt = self.semv[s]
        ws = []
        for s, v in waits or ():
            if v is None or v <= 0:
                continue
            if self.hw[eng].get(s, -1) >= v:
                continue
            self.hw[eng][s] = v
            ws.append((self.sems[s], v))
        inch = (self.sems[inc[0]], inc[1]) if inc is not None else None

        def run(e, ws=ws, fn=fn, inch=inch):
            for sh, v in ws:
                e.wait_ge(sh, v)
            ins = fn(e)
            if inch is not None:
                ins.then_inc(inch[0], inch[1])
        self.ops[eng].append(run)
        return tgt

    def pe(self, fn, waits=None, inc=None):
        return self._op("tensor", fn, waits, inc)

    def act(self, fn, waits=None, inc=None):
        return self._op("scalar", fn, waits, inc)

    def dve(self, fn, waits=None, inc=None):
        return self._op("vector", fn, waits, inc)

    def syn(self, fn, waits=None, inc=None):
        return self._op("sync", fn, waits, inc)

    def gps(self, fn, waits=None, inc=None):
        return self._op("gpsimd", fn, waits, inc)

    def v(self, s):
        return self.semv[s]

    def phase_barrier(self):
        ws = [(s, self.semv[s]) for s in ("SP", "SA", "SD", "SW", "SX", "SC")]
        for eng in ENGS:
            self._op(eng, lambda e: e.nop(), list(ws), None)

    def barrier_sst(self):
        tgt = self.semv["SST"]
        for eng in ENGS:
            self._op(eng, lambda e: e.nop(), [("SST", tgt)], None)


def build(n_heads=H, n_units=NU, debug=False):
    nc = bass.Bass("TRN2", target_bir_lowering=False)

    x_d = nc.dram_tensor("x", [S, D], F32, kind="ExternalInput").ap()
    wqkv_d = nc.dram_tensor("wqkv_t", [D, 3 * D], F32, kind="ExternalInput").ap()
    wo_d = nc.dram_tensor("wo_t", [D, D], F32, kind="ExternalInput").ap()
    wlr_d = nc.dram_tensor("wlr_t", [D, 48], F32, kind="ExternalInput").ap()
    wnat_d = nc.dram_tensor("w_nat", [3, H, 128, 128], F32, kind="ExternalInput").ap()
    wt_d = nc.dram_tensor("w_t", [2, H, 128, 128], F32, kind="ExternalInput").ap()
    onw_d = nc.dram_tensor("onw", [128, 1], F32, kind="ExternalInput").ap()
    wgate_d = nc.dram_tensor("wgate_t", [D, 32], F32, kind="ExternalInput").ap()
    wgu_d = nc.dram_tensor("wgu_t", [NU, D, 2816], F32, kind="ExternalInput").ap()
    wdn_d = nc.dram_tensor("wdn_t", [NU, EI, D], F32, kind="ExternalInput").ap()
    out_d = nc.dram_tensor("out", [S, D], F32, kind="ExternalOutput").ap()

    scr_kind = "ExternalOutput" if debug else "Internal"
    acc_d = nc.dram_tensor("acc", [S, D], F32, kind=scr_kind).ap()
    yscr_d = nc.dram_tensor("y_scr", [NU, S, D], BF16, kind="Internal").ap()
    oscr_d = nc.dram_tensor("o_scr", [H, 128, S], BF16, kind="Internal").ap()
    if debug:
        xn_dbg = nc.dram_tensor("xn_dbg", [NK, 128, S], F32, kind="ExternalOutput").ap()
        oscr_dbg = nc.dram_tensor("o_scr_dbg", [H, 128, S], F32, kind="ExternalOutput").ap()
        call_dbg = nc.dram_tensor("call_dbg", [128, NT * 17], F32, kind="ExternalOutput").ap()
        lr_dbg = nc.dram_tensor("lr_dbg", [128, NT * 48], F32, kind="ExternalOutput").ap()
        qkv_dbg = nc.dram_tensor("qkv_dbg", [3, 128, S], F32, kind="ExternalOutput").ap()
        e16_dbg = nc.dram_tensor("e16_dbg", [128, NT * 16], F32, kind="ExternalOutput").ap()
        rt_dbg = nc.dram_tensor("rt_dbg", [128, NT * 17], F32, kind="ExternalOutput").ap()
        wg_dbg = nc.dram_tensor("wg_dbg", [128, NK * 32], F32, kind="ExternalOutput").ap()
        nrm_dbg = nc.dram_tensor("nrm_dbg", [128, 64], F32, kind="ExternalOutput").ap()
        qtm_dbg = nc.dram_tensor("qtm_dbg", [128, S], F32, kind="ExternalOutput").ap()
        dw_dbg = nc.dram_tensor("dw_dbg", [3, 128, 128], F32, kind="ExternalOutput").ap()
        ns_dbg = nc.dram_tensor("ns_dbg", [3, 128, 128], F32, kind="ExternalOutput").ap()
        wn_dbg = nc.dram_tensor("wn_dbg", [3, 128, 128], F32, kind="ExternalOutput").ap()
        o_dbg = nc.dram_tensor("o_dbg", [128, S], F32, kind="ExternalOutput").ap()
        h_dbg = nc.dram_tensor("h_dbg", [128, NJ * S], F32, kind="ExternalOutput").ap()

    ident = nc.inline_tensor(np.eye(128, dtype=ml_dtypes.bfloat16).view(np.uint16),
                             name="ident_bf")
    identf = nc.inline_tensor(np.eye(128, dtype=np.float32), name="ident_f32")
    onesr = nc.inline_tensor(np.ones((1, 128), dtype=ml_dtypes.bfloat16).view(np.uint16),
                             name="ones_row_bf")
    onesc = nc.inline_tensor(np.ones((128, 1), dtype=ml_dtypes.bfloat16).view(np.uint16),
                             name="ones_col_bf")
    cst_np = np.tile(np.array([[1.0 / D, 1e-5, BASE_LR_INV, 1e-7]], np.float32), (128, 1))
    cst_t = nc.inline_tensor(cst_np, name="cst_f32")

    A = nc.alloc_sbuf_tensor
    xn_fm = A("xn_fm", [128, NK * S], BF16)
    c_all = A("c_all", [128, NT * 17], F32)
    lr_tm = A("lr_tm", [128, NT * 48], F32)


    wlr_sb = A("wlr_sb", [128, NK * 48], BF16)
    onw_sb = A("onw_sb", [128, 1], F32)
    id_sb = A("id_sb", [128, 128], BF16)
    idf_sb = A("idf_sb", [128, 128], F32)
    wgf_sb = A("wgf_sb", [128, NK * 32], F32)
    onr_sb = A("onr_sb", [1, 128], BF16)
    onc_sb = A("onc_sb", [128, 1], BF16)
    cst_sb = A("cst_sb", [128, 4], F32)
    rt_sb = A("rt_sb", [128, 17], F32)
    rt_all = A("rt_all", [128, NT * 17], F32)
    e16a = A("e16a", [128, 16], F32)
    e16b = A("e16b", [128, 16], F32)
    msk = A("msk", [128, 16], F32)
    nm_sb = A("nm_sb", [128, 1], F32)
    mr_sb = A("mr_sb", [128, 1], F32)
    mr2_sb = A("mr2_sb", [128, 1], F32)
    ns_sb = A("ns_sb", [128, 1], F32)
    tng = A("tng", [128, 1], F32)
    st_s = [A(f"st_s{i}", [128, 1], F32) for i in range(2)]
    st_m = [A(f"st_m{i}", [128, 1], F32) for i in range(2)]
    st_v = [A(f"st_v{i}", [128, 1], F32) for i in range(2)]
    st_r = [A(f"st_r{i}", [128, 1], F32) for i in range(2)]
    sq_scrap = A("sq_scrap", [128, 2048], F32)

    k = Emit(nc)
    for s in ("SX", "SW", "SP", "SA", "SD", "SC", "SST"):
        k.sem(s)

    bank = [nc.alloc_psum_tensor(f"bank{i}", [128, 512], F32) for i in range(8)]
    bank_free = [("SP", 0)] * 8

    def mm(b, out_ap_fn, lhsT_fn, rhs_fn, start, stop, waits=None, inc=None):
        def f(e):
            return e.matmul(out_ap_fn(), lhsT_fn(), rhs_fn(), start=start, stop=stop)
        return k.pe(f, waits=waits, inc=inc)

    k.syn(lambda e: e.dma_start(out=id_sb[:, :], in_=ident.ap().bitcast(BF16)), inc=("SX", 16))
    k.syn(lambda e: e.dma_start(out=idf_sb[:, :], in_=identf.ap()), inc=("SX", 16))
    k.syn(lambda e: e.dma_start(
        out=wgf_sb[:, :].rearrange("p (c m) -> p c m", c=NK),
        in_=wgate_d.rearrange("(c p) m -> p c m", p=128)), inc=("SX", 16))
    k.syn(lambda e: e.dma_start(out=onr_sb[:, :], in_=onesr.ap().bitcast(BF16)), inc=("SX", 16))
    k.syn(lambda e: e.dma_start(out=onc_sb[:, :], in_=onesc.ap().bitcast(BF16)), inc=("SX", 16))
    k.syn(lambda e: e.dma_start(out=onw_sb[:, :], in_=onw_d), inc=("SX", 16))
    k.syn(lambda e: e.dma_start(out=cst_sb[:, :], in_=cst_t.ap()), inc=("SX", 16))
    k.gps(lambda e: e.dma_start(
        out=wlr_sb[:, :].rearrange("p (c m) -> p c m", c=NK),
        in_=wlr_d.rearrange("(c p) m -> p c m", p=128)), inc=("SW", 16))
    sx_const = k.v("SX")
    sw_const = k.v("SW")

    import contextlib
    ln_es = contextlib.ExitStack()
    xt = [ln_es.enter_context(nc.sbuf_tensor(f"xt{i}", [128, 2048], F32)) for i in range(2)]
    xcb = [ln_es.enter_context(nc.sbuf_tensor(f"xcb{i}", [128, 2048], F32)) for i in range(2)]
    xnt = [ln_es.enter_context(nc.sbuf_tensor(f"xnt{i}", [128, 2048], BF16)) for i in range(2)]
    lnsq = [ln_es.enter_context(nc.sbuf_tensor(f"lnsq{i}", [128, 2048], F32)) for i in range(2)]
    xnf32 = ln_es.enter_context(nc.sbuf_tensor("xnf32", [128, NK * 128], F32))

    xt_free = [0, 0]
    rtf_done = 0
    rt_ready = [0] * NT
    st_m_free = 0
    xt_free_d = [0, 0]
    xcb_free = [0, 0]
    xcb_free_a = [0, 0]
    xnt_free = [0, 0]
    xn_ready = [0] * NT

    for tt in range(NT):
        p = tt % 2
        sx = k.syn(lambda e, tt=tt, p=p: e.dma_start(out=xt[p][:, :], in_=x_d[tt * 128:(tt + 1) * 128, :]),
                   waits=[("SD", xt_free[p])], inc=("SX", 16))
        sd = k.dve(lambda e, p=p: e.tensor_reduce(st_s[p][:, :], xt[p][:, :], AX, ALU.add, negate=True),
                   waits=[("SX", sx), ("SA", st_m_free)], inc=("SD", 1))
        sa = k.act(lambda e, p=p: e.activation(st_m[p][:, :], st_s[p][:, :], AF.Identity,
                                          scale=cst_sb[:, 0:1]),
                   waits=[("SD", sd), ("SX", sx_const)], inc=("SA", 1))
        sd = k.dve(lambda e, p=p: e.tensor_scalar(xcb[p][:, :], xt[p][:, :], st_m[p][:, 0:1],
                                                  None, ALU.add),
                   waits=[("SA", sa), ("SA", xcb_free_a[p])], inc=("SD", 1))
        xt_free[p] = sd
        st_m_free = k.v("SA")
        sa = k.act(lambda e, p=p: e.activation(lnsq[p][:, :], xcb[p][:, :], AF.Square),
                   waits=[("SD", sd)], inc=("SA", 1))
        sd = k.dve(lambda e, p=p: e.tensor_reduce(st_v[p][:, :], lnsq[p][:, :], AX, ALU.add),
                   waits=[("SA", sa)], inc=("SD", 1))
        sa = k.act(lambda e, p=p: e.activation(st_v[p][:, :], st_v[p][:, :], AF.Sqrt,
                                          scale=cst_sb[:, 0:1], bias=cst_sb[:, 1:2]),
                   waits=[("SD", sd)], inc=("SA", 1))
        sd = k.dve(lambda e, p=p: e.reciprocal(st_r[p][:, :], st_v[p][:, :]), waits=[("SA", sa)], inc=("SD", 1))
        sa = k.act(lambda e, p=p: e.activation(xnt[p][:, :], xcb[p][:, :], AF.Identity,
                                               scale=st_r[p][:, 0:1]),
                   waits=[("SD", sd), ("SP", xnt_free[p])], inc=("SA", 1))
        xcb_free[p] = k.v("SD")
        xcb_free_a[p] = sa
        xt_free_d[p] = sd
        last_sp = 0
        for g in range(4):
            b = g % 2
            for j in range(4):
                kc = g * 4 + j
                sp = mm(b, lambda b=b, j=j: bank[b][:, j * 128:(j + 1) * 128],
                        lambda p=p, kc=kc: xnt[p][:, kc * 128:(kc + 1) * 128],
                        lambda: id_sb[:, :], True, True,
                        waits=[("SA", sa), (bank_free[b][0], bank_free[b][1])],
                        inc=("SP", 1) if j == 3 else None)
            k.dve(lambda e, b=b, g=g, tt=tt: e.tensor_copy(
                xn_fm[:, :].rearrange("p (c s) -> p c s", c=NK)[:, g * 4:(g + 1) * 4, tt * 128:(tt + 1) * 128],
                bank[b][:, :].rearrange("p (c s) -> p c s", c=4)),
                waits=[("SP", sp), ("SP", rtf_done)], inc=("SD", 1))
            sdv = k.dve(lambda e, b=b, g=g: e.tensor_copy(
                xnf32[:, g * 512:(g + 1) * 512], bank[b][:, :]),
                inc=("SD", 1))
            bank_free[b] = ("SD", sdv)
            last_sp = sp
        xnt_free[p] = last_sp
        xn_ready[tt] = sdv
        for kc in range(NK):
            rtf_done = mm(2, lambda: bank[2][:, 0:32],
                          lambda kc=kc: xnf32[:, kc * 128:(kc + 1) * 128],
                          lambda kc=kc: wgf_sb[:, kc * 32:(kc + 1) * 32],
                          kc == 0, kc == NK - 1,
                          waits=[("SD", sdv), ("SX", sx_const),
                                 (bank_free[2][0], bank_free[2][1])],
                          inc=("SP", 1) if kc == NK - 1 else None)
        sa_rt = k.act(lambda e, tt=tt: e.activation(rt_all[:, tt * 17:(tt + 1) * 17],
                                                    bank[2][:, 0:17], AF.Identity),
                      waits=[("SP", rtf_done)], inc=("SA", 1))
        bank_free[2] = ("SA", sa_rt)
        rt_ready[tt] = sa_rt
    ln_es.close()
    k.phase_barrier()

    if debug:
        for kc in range(NK):
            k.gps(lambda e, kc=kc: e.dma_start(out=xn_dbg[kc], in_=xn_fm[:, kc * S:(kc + 1) * S]),
                  waits=[("SD", xn_ready[NT - 1])], inc=("SST", 16))

    rt_free_sd = 0
    for tt in range(NT):
        sa = k.act(lambda e, tt=tt: e.activation(rt_sb[:, :], rt_all[:, tt * 17:(tt + 1) * 17],
                                                 AF.Identity),
                   waits=[("SA", rt_ready[tt]), ("SD", rt_free_sd)], inc=("SA", 1))
        if debug:
            k.gps(lambda e, tt=tt: e.dma_start(out=rt_dbg[:, tt * 17:(tt + 1) * 17], in_=rt_sb[:, :]),
                  waits=[("SA", sa)], inc=("SST", 16))
            k.barrier_sst()
        sd = k.dve(lambda e: e.tensor_reduce(nm_sb[:, :], rt_sb[:, 0:16], AX, ALU.max, negate=True),
                   waits=[("SA", sa)], inc=("SD", 1))
        sa = k.act(lambda e: e.activation(e16a[:, :], rt_sb[:, 0:16], AF.Exp, bias=nm_sb[:, 0:1]),
                   waits=[("SD", sd)], inc=("SA", 1))
        if debug:
            k.gps(lambda e, tt=tt: e.dma_start(out=e16_dbg[:, tt * 16:(tt + 1) * 16], in_=e16a[:, :]),
                  waits=[("SA", sa)], inc=("SST", 16))
            k.barrier_sst()
        sa = k.act(lambda e: e.activation(tng[:, :], rt_sb[:, 16:17], AF.Tanh, scale=0.5),
                   inc=("SA", 1))
        csl = lambda tt=tt: c_all[:, tt * 17: tt * 17 + 16]
        k.dve(lambda e, tt=tt: e.tensor_scalar(c_all[:, tt * 17 + 16: tt * 17 + 17], tng[:, :],
                                               0.5, 0.5, ALU.mult, ALU.add),
              waits=[("SA", sa)])
        k.dve(lambda e, csl=csl: e.memset(csl(), 0.0))
        ebufs = [e16a, e16b]
        for r in range(4):
            ecur = ebufs[r % 2]
            enxt = ebufs[(r + 1) % 2]
            sd = k.dve(lambda e, ecur=ecur: e.tensor_reduce(mr_sb[:, :], ecur[:, :], AX, ALU.max),
                       inc=("SD", 1))
            sa = k.act(lambda e: e.activation(mr2_sb[:, :], mr_sb[:, :], AF.Identity),
                       waits=[("SD", sd)], inc=("SA", 1))
            k.dve(lambda e, ecur=ecur: e.tensor_scalar(msk[:, :], ecur[:, :], mr2_sb[:, 0:1],
                                                       None, ALU.is_ge),
                  waits=[("SA", sa)])
            k.dve(lambda e, csl=csl: e.scalar_tensor_tensor(csl(), msk[:, :], mr2_sb[:, 0:1], csl(),
                                                            ALU.mult, ALU.add))
            if r == 0:
                sd_ns_acc = k.dve(lambda e: e.tensor_copy(ns_sb[:, :], mr2_sb[:, :]), inc=("SD", 1))
            else:
                sd_ns_acc = k.dve(lambda e: e.tensor_add(ns_sb[:, :], ns_sb[:, :], mr2_sb[:, :]),
                                  inc=("SD", 1))
            if r < 3:
                sd = k.dve(lambda e, ecur=ecur, enxt=enxt: e.scalar_tensor_tensor(
                    enxt[:, :], msk[:, :], -1e30, ecur[:, :], ALU.mult, ALU.add),
                    inc=("SD", 1))
                sa = k.act(lambda e, enxt=enxt: e.activation(enxt[:, :], enxt[:, :], AF.Identity),
                           waits=[("SD", sd)], inc=("SA", 1))
                k.dve(lambda e: e.nop(), waits=[("SA", sa)])
        sa = k.act(lambda e: e.activation(mr2_sb[:, :], ns_sb[:, :], AF.Identity),
                   waits=[("SD", sd_ns_acc)], inc=("SA", 1))
        sd = k.dve(lambda e: e.reciprocal(mr_sb[:, :], mr2_sb[:, :]),
                   waits=[("SA", sa)], inc=("SD", 1))
        sd = k.act(lambda e, csl=csl: e.activation(csl(), csl(), AF.Identity,
                                                   scale=mr_sb[:, 0:1]),
                   waits=[("SD", sd)], inc=("SA", 1))
        rt_free_sd = k.v("SD")
    c_ready = k.v("SA")
    if debug:
        k.syn(lambda e: e.dma_start(out=call_dbg, in_=c_all[:, :]),
              waits=[("SA", c_ready)], inc=("SST", 16))

    import contextlib as _ctx
    lr_es = _ctx.ExitStack()
    lr_fm = lr_es.enter_context(nc.sbuf_tensor("lr_fm", [48, S], F32))
    lr_scrap_free = 0
    for st in range(4):
        sp = None
        for kc in range(NK):
            sp = mm(3, lambda: bank[3][0:48, :],
                    lambda kc=kc: wlr_sb[:, kc * 48:(kc + 1) * 48],
                    lambda kc=kc, st=st: xn_fm[:, kc * S + st * 512: kc * S + (st + 1) * 512],
                    kc == 0, kc == NK - 1,
                    waits=[("SD", xn_ready[NT - 1]), ("SW", sw_const),
                           (bank_free[3][0], bank_free[3][1])],
                    inc=("SP", 1) if kc == NK - 1 else None)
        sa = k.act(lambda e: e.activation(sq_scrap[0:48, 0:512], bank[3][0:48, :], AF.Abs,
                                          bias=cst_sb[0:48, 2:3]),
                   waits=[("SP", sp), ("SD", lr_scrap_free if st > 0 else 0)], inc=("SA", 1))
        sa = k.act(lambda e: e.activation(sq_scrap[0:48, 512:1024], sq_scrap[0:48, 0:512], AF.Exp,
                                          scale=-1.0), waits=[("SA", sa)], inc=("SA", 1))
        sa = k.act(lambda e: e.activation(sq_scrap[0:48, 512:1024], sq_scrap[0:48, 512:1024], AF.Ln,
                                          bias=1.0), waits=[("SA", sa)], inc=("SA", 1))
        sd = k.dve(lambda e: e.tensor_scalar(sq_scrap[0:48, 1024:1536], bank[3][0:48, :],
                                             BASE_LR_INV, 0.0, ALU.add, ALU.max),
                   waits=[("SA", sa)], inc=("SD", 1))
        bank_free[3] = ("SD", sd)
        sd = k.dve(lambda e, st=st: e.tensor_add(lr_fm[:, st * 512:(st + 1) * 512],
                                                 sq_scrap[0:48, 1024:1536], sq_scrap[0:48, 512:1024]),
                   inc=("SD", 1))
        lr_scrap_free = sd
    lr_fm_ready = sd
    for g in range(2):
        b = 4 + g
        for j in range(8):
            tt = g * 8 + j
            sp = mm(b, lambda b=b, j=j: bank[b][:, j * 48:(j + 1) * 48],
                    lambda tt=tt: lr_fm[:, tt * 128:(tt + 1) * 128],
                    lambda: idf_sb[0:48, 0:48], True, True,
                    waits=[("SD", lr_fm_ready), ("SX", sx_const),
                           (bank_free[b][0], bank_free[b][1])],
                    inc=("SP", 1) if j == 7 else None)
        sd = k.dve(lambda e, b=b, g=g: e.tensor_copy(lr_tm[:, g * 8 * 48:(g + 1) * 8 * 48],
                                                     bank[b][:, 0:384]),
                   waits=[("SP", sp)], inc=("SD", 1))
        bank_free[b] = ("SD", sd)
    lr_ready = k.v("SD")
    lr_es.close()
    k.phase_barrier()
    if debug:
        k.syn(lambda e: e.dma_start(out=lr_dbg, in_=lr_tm[:, :]),
              waits=[("SD", lr_ready)], inc=("SST", 16))

    at_es = contextlib.ExitStack()
    C = at_es.enter_context
    qkvw = [C(nc.sbuf_tensor(f"qkvw{i}", [128, 3 * NK * 128], BF16)) for i in range(2)]
    wnat = [C(nc.sbuf_tensor(f"wnat{i}", [128, 3 * 128], F32)) for i in range(2)]
    wT = [C(nc.sbuf_tensor(f"wT{i}", [128, 2 * 128], BF16)) for i in range(2)]
    w1bf = [C(nc.sbuf_tensor(f"w1bf{i}", [128, 128], BF16)) for i in range(2)]
    q_fm2 = [C(nc.sbuf_tensor(f"q_fm{i}", [128, S], BF16)) for i in range(2)]
    k_fm = C(nc.sbuf_tensor("k_fm", [128, S], BF16))
    v_fm = C(nc.sbuf_tensor("v_fm", [128, S], BF16))
    q_tm = C(nc.sbuf_tensor("q_tm", [128, S], BF16))
    k_tm = C(nc.sbuf_tensor("k_tm", [128, S], BF16))
    v_tm = C(nc.sbuf_tensor("v_tm", [128, S], BF16))
    nrm32 = C(nc.sbuf_tensor("nrm32", [128, 32], F32))
    rr32 = C(nc.sbuf_tensor("rr32", [128, 32], F32))
    rr32b = C(nc.sbuf_tensor("rr32b", [128, 32], F32))
    tanb = C(nc.sbuf_tensor("tanb", [128, 512], F32))
    sigb = C(nc.sbuf_tensor("sigb", [128, 512], F32))
    ub = C(nc.sbuf_tensor("ub", [128, 512], F32))
    dgb = C(nc.sbuf_tensor("dgb", [128, 512], F32))
    sgb2 = [C(nc.sbuf_tensor(f"sgb{i}", [128, 512], F32)) for i in range(2)]
    hbb = C(nc.sbuf_tensor("hbb", [128, 512], F32))
    dhb = C(nc.sbuf_tensor("dhb", [128, 512], F32))
    hidb2 = [C(nc.sbuf_tensor(f"hidb{i}", [128, 512], BF16)) for i in range(2)]
    dbmb = C(nc.sbuf_tensor("dbmb", [128, 512], BF16))
    dgab = C(nc.sbuf_tensor("dgab", [128, 512], BF16))
    hl1t = C(nc.sbuf_tensor("hl1t", [128, 512], BF16))
    dgat = C(nc.sbuf_tensor("dgat", [128, 512], BF16))
    dbmt = C(nc.sbuf_tensor("dbmt", [128, 512], BF16))
    kl0t = C(nc.sbuf_tensor("kl0t", [128, 512], BF16))
    kl2t = C(nc.sbuf_tensor("kl2t", [128, 512], BF16))
    Xb = [C(nc.sbuf_tensor(f"Xb{i}", [128, 128], BF16)) for i in range(3)]
    XTb = [C(nc.sbuf_tensor(f"XTb{i}", [128, 128], BF16)) for i in range(3)]
    Ab = [C(nc.sbuf_tensor(f"Ab{i}", [128, 128], BF16)) for i in range(3)]
    Abf = [C(nc.sbuf_tensor(f"Abf{i}", [128, 128], F32)) for i in range(3)]
    Bb = [C(nc.sbuf_tensor(f"Bb{i}", [128, 128], BF16)) for i in range(3)]
    dwssb = [C(nc.sbuf_tensor(f"dwssb{i}", [128, 1], BF16)) for i in range(3)]
    facb = [C(nc.sbuf_tensor(f"facb{i}", [128, 1], F32)) for i in range(3)]
    smll = [C(nc.sbuf_tensor(f"smll{i}", [1, 8], F32)) for i in range(3)]
    smlb = [C(nc.sbuf_tensor(f"smlb{i}", [1, 1], BF16)) for i in range(3)]
    sqG = C(nc.sbuf_tensor("sqG", [128, 384], F32))
    wn_h2 = [C(nc.sbuf_tensor(f"wn_h{i}", [128, 3], F32)) for i in range(2)]
    wnsq = C(nc.sbuf_tensor("wnsq", [128, 384], F32))
    rn3 = C(nc.sbuf_tensor("rn3", [128, 3], F32))
    Wf32 = C(nc.sbuf_tensor("Wf32", [128, 128], F32))
    Wnb = [C(nc.sbuf_tensor(f"Wnb{i}", [128, 128], BF16)) for i in range(3)]
    WnT = [C(nc.sbuf_tensor(f"WnT{i}", [128, 128], BF16)) for i in range(3)]
    o_bf = C(nc.sbuf_tensor("o_bf", [128, S], BF16))
    o2_bf = C(nc.sbuf_tensor("o2_bf", [128, S], BF16))
    o_tm = C(nc.sbuf_tensor("o_tm", [128, S], BF16))
    dsum = C(nc.sbuf_tensor("dsum", [128, 16], F32))
    rmfac = C(nc.sbuf_tensor("rmfac", [128, 16], F32))
    rmfacb = C(nc.sbuf_tensor("rmfacb", [128, 16], F32))

    hw_free = [(0, 0, 0), (0, 0, 0)]
    wnh_free = [0, 0]
    oscr_done = []

    def head_dmas(h):
        p = h % 2
        w = [("SP", hw_free[p][0]), ("SA", hw_free[p][1]), ("SD", hw_free[p][2])]
        for c in range(3):
            k.gps(lambda e, c=c, h=h, p=p: e.dma_start(
                out=qkvw[p][:, c * 2048:(c + 1) * 2048].rearrange("p (c d) -> p c d", d=128),
                in_=wqkv_d[:, c * 2048 + h * 128: c * 2048 + (h + 1) * 128]
                .rearrange("(c p) d -> p c d", p=128)), waits=w, inc=("SW", 16))
        k.gps(lambda e, h=h, p=p: e.dma_start(
            out=wT[p][:, :].rearrange("p (i d) -> p i d", i=2),
            in_=wt_d[:, h].rearrange("i p d -> p i d")), waits=w, inc=("SW", 16))
        k.gps(lambda e, h=h, p=p: e.dma_start(out=w1bf[p][:, :], in_=wnat_d[1, h]),
              waits=w, inc=("SW", 16))
        k.syn(lambda e, h=h, p=p: e.dma_start(
            out=wnat[p][:, :].rearrange("p (i d) -> p i d", i=3),
            in_=wnat_d[:, h].rearrange("i p d -> p i d")), waits=w, inc=("SX", 16))
        return k.v("SW"), k.v("SX")

    def transpose_batch(src_fn, dst_fn, n_tiles, b0, b1, src_ready, fuse_scale=None,
                        dst_dtype_bf=True):
        nb = (n_tiles + 3) // 4
        for g in range(nb):
            b = (b0, b1)[g % 2]
            cnt = min(4, n_tiles - g * 4)
            for j in range(cnt):
                t = g * 4 + j
                sp = mm(b, lambda b=b, j=j: bank[b][:, j * 128:(j + 1) * 128],
                        lambda t=t: src_fn(t), lambda: id_sb[:, :], True, True,
                        waits=[("SD", src_ready[0] if isinstance(src_ready, tuple) else src_ready),
                               (bank_free[b][0], bank_free[b][1])] +
                              ([("SA", src_ready[1])] if isinstance(src_ready, tuple) else []),
                        inc=("SP", 1) if j == cnt - 1 else None)
            if fuse_scale is None:
                sd = k.dve(lambda e, b=b, g=g, cnt=cnt: e.tensor_copy(
                    dst_fn(g * 4, cnt), bank[b][:, 0:cnt * 128]),
                    waits=[("SP", sp)], inc=("SD", 1))
            else:
                sd = None
                for j in range(cnt):
                    t = g * 4 + j
                    sd = k.dve(lambda e, b=b, j=j, t=t: e.tensor_scalar(
                        dst_fn(t, None), bank[b][:, j * 128:(j + 1) * 128],
                        fuse_scale(t), None, ALU.mult),
                        waits=[("SP", sp)], inc=("SD", 1))
            bank_free[b] = ("SD", sd)
        return k.v("SD")

    sw0, sx0 = head_dmas(0)
    head_w_ready = {0: (sw0, sx0)}

    qfm_free = [0, 0]

    def emit_front(h, ctx):
        p = h % 2
        q_fm = q_fm2[p]
        sgb = sgb2[p]
        hidb = hidb2[p]
        wn_h = wn_h2[p]
        sw_h, sx_h = head_w_ready[h]

        dst = (q_fm, k_fm, v_fm)
        qkv_ready_a = None
        for st in range(4):
            for c in range(3):
                b = c
                for kc in range(NK):
                    sp = mm(b, lambda b=b: bank[b][:, :],
                            lambda p=p, c=c, kc=kc: qkvw[p][:, c * 2048 + kc * 128: c * 2048 + (kc + 1) * 128],
                            lambda kc=kc, st=st: xn_fm[:, kc * S + st * 512: kc * S + (st + 1) * 512],
                            kc == 0, kc == NK - 1,
                            waits=[("SW", sw_h), ("SD", xn_ready[NT - 1]),
                                   (bank_free[b][0], bank_free[b][1])],
                            inc=("SP", 1) if kc == NK - 1 else None)
                sa = k.act(lambda e, c=c, st=st, b=b, dst=dst: e.activation(
                    dst[c][:, st * 512:(st + 1) * 512], bank[b][:, :], AF.Silu),
                    waits=[("SP", sp), ("SP", qfm_free[p])], inc=("SA", 1))
                bank_free[b] = ("SA", sa)
                yield
        qkv_ready_a = k.v("SA")

        for src, dstt in ((q_fm, q_tm), (k_fm, k_tm), (v_fm, v_tm)):
            tm_ready = transpose_batch(
                lambda t, src=src: src[:, t * 128:(t + 1) * 128],
                lambda t0, cnt, dstt=dstt: dstt[:, t0 * 128:(t0 + cnt) * 128],
                16, 3, 7, (0, qkv_ready_a))
            yield

        sdr = None
        for src_t, base in ((q_tm, 0), (k_tm, 16)):
            sa = k.act(lambda e, src_t=src_t: e.activation(sq_scrap[:, :], src_t[:, :], AF.Square),
                       waits=[("SD", tm_ready), ("SD", sdr or 0)], inc=("SA", 1))
            for t in range(16):
                sdr = k.dve(lambda e, t=t, base=base: e.tensor_reduce(
                    nrm32[:, base + t: base + t + 1], sq_scrap[:, t * 128:(t + 1) * 128],
                    AX, ALU.add),
                    waits=[("SA", sa)], inc=("SD", 1) if t == 15 else None)
        if debug and h == 0:
            k.gps(lambda e: e.dma_start(out=nrm_dbg[:, 0:32], in_=nrm32[:, :]),
                  waits=[("SD", sdr)], inc=("SST", 16))
            k.barrier_sst()
        sa = k.act(lambda e: e.activation(nrm32[:, :], nrm32[:, :], AF.Sqrt),
                   waits=[("SD", sdr)], inc=("SA", 1))
        yield
        sa = k.act(lambda e: e.activation(nrm32[:, :], nrm32[:, :], AF.Identity,
                                          bias=cst_sb[:, 1:2]),
                   waits=[("SA", sa)], inc=("SA", 1))
        sd = k.dve(lambda e: e.reciprocal(rr32b[:, :], nrm32[:, :]),
                   waits=[("SA", sa)], inc=("SD", 1))
        sa = k.act(lambda e: e.activation(rr32[:, :], rr32b[:, :], AF.Identity),
                   waits=[("SD", sd)], inc=("SA", 1))
        for t in range(16):
            k.dve(lambda e, t=t: e.tensor_scalar(q_tm[:, t * 128:(t + 1) * 128],
                                                 q_tm[:, t * 128:(t + 1) * 128],
                                                 rr32[:, t:t + 1], None, ALU.mult),
                  waits=[("SA", sa)])
            sd = k.dve(lambda e, t=t: e.tensor_scalar(k_tm[:, t * 128:(t + 1) * 128],
                                                      k_tm[:, t * 128:(t + 1) * 128],
                                                      rr32[:, 16 + t:17 + t], None, ALU.mult),
                       inc=("SD", 1))
        if debug and h == 0:
            k.gps(lambda e: e.dma_start(out=qtm_dbg, in_=q_tm[:, :]),
                  waits=[("SD", sd)], inc=("SST", 16))
            k.gps(lambda e: e.dma_start(out=nrm_dbg[:, 32:64], in_=rr32[:, :]),
                  waits=[("SA", sa)], inc=("SST", 16))
            k.barrier_sst()
        yield
        for src, dstf in ((q_tm, q_fm), (k_tm, k_fm)):
            fm_ready = transpose_batch(
                lambda t, src=src: src[:, t * 128:(t + 1) * 128],
                lambda t0, cnt, dstf=dstf: dstf[:, t0 * 128:(t0 + cnt) * 128],
                16, 3, 7, sd)
            yield

        if debug and h == 0:
            k.gps(lambda e: e.dma_start(out=qkv_dbg[0], in_=q_fm[:, :]),
                  waits=[("SD", fm_ready)], inc=("SST", 16))
            k.gps(lambda e: e.dma_start(out=qkv_dbg[1], in_=k_fm[:, :]),
                  waits=[("SD", fm_ready)], inc=("SST", 16))
            k.gps(lambda e: e.dma_start(out=qkv_dbg[2], in_=v_fm[:, :]),
                  waits=[("SD", fm_ready)], inc=("SST", 16))
            k.barrier_sst()

        sa_wn = k.act(lambda e, p=p: e.activation(wnsq[:, :], wnat[p][:, :], AF.Square),
                      waits=[("SX", sx_h), ("SD", wnh_free[p])], inc=("SA", 1))
        sd_wn = None
        for i in range(3):
            sd_wn = k.dve(lambda e, i=i, wn_h=wn_h: e.tensor_reduce(wn_h[:, i:i + 1],
                                                         wnsq[:, i * 128:(i + 1) * 128],
                                                         AX, ALU.add),
                          waits=[("SA", sa_wn)], inc=("SD", 1) if i == 2 else None)
        sa_wn = k.act(lambda e, wn_h=wn_h: e.activation(wn_h[:, :], wn_h[:, :], AF.Sqrt),
                      waits=[("SD", sd_wn)], inc=("SA", 1))
        yield

        yield "drain"
        if h + 1 < n_heads:
            head_w_ready[h + 1] = head_dmas(h + 1)
        dw_prev = 0
        sgrp_free = 0
        for sc in range(4):
            cs = slice(sc * 512, (sc + 1) * 512)
            spg = mm(0, lambda: bank[0][:, :], lambda p=p: wT[p][:, 0:128],
                     lambda cs=cs: k_fm[:, cs], True, True,
                     waits=[("SD", fm_ready), ("SW", sw_h),
                            (bank_free[0][0], bank_free[0][1])], inc=("SP", 1))
            sph = mm(1, lambda: bank[1][:, :], lambda p=p: wT[p][:, 128:256],
                     lambda cs=cs: k_fm[:, cs], True, True,
                     waits=[(bank_free[1][0], bank_free[1][1])], inc=("SP", 1))
            spd = mm(2, lambda: bank[2][:, :], lambda p=p: w1bf[p][:, :],
                     lambda cs=cs: v_fm[:, cs], True, True,
                     waits=[(bank_free[2][0], bank_free[2][1])], inc=("SP", 1))
            sa1 = k.act(lambda e: e.activation(sgb[:, :], bank[0][:, :], AF.Silu),
                        waits=[("SP", spg), ("SD", sgrp_free)], inc=("SA", 1))
            sa2 = k.act(lambda e: e.activation(tanb[:, :], bank[0][:, :], AF.Tanh, scale=0.5),
                        inc=("SA", 1))
            sa3 = k.act(lambda e: e.activation(hbb[:, :], bank[1][:, :], AF.Copy),
                        waits=[("SP", sph)], inc=("SA", 1))
            sa4 = k.act(lambda e: e.activation(dhb[:, :], bank[2][:, :], AF.Copy),
                        waits=[("SP", spd)], inc=("SA", 1))
            bank_free[1] = ("SA", sa3)
            bank_free[2] = ("SA", sa4)
            k.dve(lambda e: e.tensor_mul(hidb[:, :], sgb[:, :], hbb[:, :]),
                  waits=[("SA", sa3), ("SP", dw_prev)])
            k.dve(lambda e: e.tensor_tensor(dgb[:, :], dhb[:, :], hbb[:, :], ALU.mult),
                  waits=[("SA", sa4)])
            k.dve(lambda e: e.tensor_mul(dbmb[:, :], dhb[:, :], sgb[:, :]))
            k.dve(lambda e: e.tensor_scalar(sigb[:, :], tanb[:, :], 0.5, 0.5, ALU.mult, ALU.add))
            k.dve(lambda e: e.tensor_scalar(tanb[:, :], tanb[:, :], -1.0, 1.0, ALU.mult, ALU.add))
            sdu = k.dve(lambda e: e.scalar_tensor_tensor(ub[:, :], bank[0][:, :], 0.5, tanb[:, :],
                                                         ALU.mult, ALU.mult), inc=("SD", 1))
            bank_free[0] = ("SD", sdu)
            k.dve(lambda e: e.scalar_tensor_tensor(sigb[:, :], ub[:, :], 1.0, sigb[:, :],
                                                   ALU.add, ALU.mult))
            sd_dga = k.dve(lambda e: e.tensor_tensor(dgab[:, :], dgb[:, :], sigb[:, :], ALU.mult),
                           inc=("SD", 1))
            sd_hid = sd_dga
            sgrp_free = sd_dga
            for j in range(4):
                t = sc * 4 + j
                k.dve(lambda e, j=j, t=t, h=h: e.tensor_scalar(
                    kl0t[:, j * 128:(j + 1) * 128], k_tm[:, t * 128:(t + 1) * 128],
                    lr_tm[:, t * 48 + h * 3: t * 48 + h * 3 + 1], None, ALU.mult),
                    waits=[("SD", lr_ready), ("SP", dw_prev)])
                sd_kl = k.dve(lambda e, j=j, t=t, h=h: e.tensor_scalar(
                    kl2t[:, j * 128:(j + 1) * 128], k_tm[:, t * 128:(t + 1) * 128],
                    lr_tm[:, t * 48 + h * 3 + 2: t * 48 + h * 3 + 3], None, ALU.mult),
                    inc=("SD", 1))
            transpose_batch(lambda t: hidb[:, (t % 4) * 128:(t % 4 + 1) * 128],
                            lambda t, _ : hl1t[:, (t % 4) * 128:(t % 4 + 1) * 128],
                            4, 3, 7, sd_hid,
                            fuse_scale=lambda t, sc=sc, h=h: lr_tm[:, (sc * 4 + t) * 48 + h * 3 + 1:
                                                                   (sc * 4 + t) * 48 + h * 3 + 2])
            transpose_batch(lambda t: dgab[:, (t % 4) * 128:(t % 4 + 1) * 128],
                            lambda t0, cnt: dgat[:, t0 * 128:(t0 + cnt) * 128],
                            4, 3, 7, sd_dga)
            sd_t = transpose_batch(lambda t: dbmb[:, (t % 4) * 128:(t % 4 + 1) * 128],
                                   lambda t0, cnt: dbmt[:, t0 * 128:(t0 + cnt) * 128],
                                   4, 3, 7, sd_dga)
            for j in range(4):
                st_flag = (sc == 0 and j == 0)
                sp_flag = (sc == 3 and j == 3)
                w0w = [("SD", max(sd_t, sd_kl)), (bank_free[4][0], bank_free[4][1])]
                mm(4, lambda: bank[4][:, 0:128],
                   lambda j=j: dgat[:, j * 128:(j + 1) * 128],
                   lambda j=j: kl0t[:, j * 128:(j + 1) * 128], st_flag, sp_flag,
                   waits=w0w, inc=("SP", 1) if sp_flag else None)
                mm(5, lambda: bank[5][:, 0:128],
                   lambda sc=sc, j=j: v_tm[:, (sc * 4 + j) * 128:(sc * 4 + j + 1) * 128],
                   lambda j=j: hl1t[:, j * 128:(j + 1) * 128], st_flag, sp_flag,
                   waits=[(bank_free[5][0], bank_free[5][1])],
                   inc=("SP", 1) if sp_flag else None)
                last = (j == 3)
                sp_dw = mm(6, lambda: bank[6][:, 0:128],
                           lambda j=j: dbmt[:, j * 128:(j + 1) * 128],
                           lambda j=j: kl2t[:, j * 128:(j + 1) * 128], st_flag, sp_flag,
                           waits=[(bank_free[6][0], bank_free[6][1])],
                           inc=("SP", 1) if last else None)
                if last:
                    dw_prev = sp_dw
            yield
        ctx.update(p=p, q_fm=q_fm, sgb=sgb, hidb=hidb, wn_h=wn_h, sp_dw=sp_dw,
                   sa_wn=sa_wn, h=h)

    def emit_back(h, ctx):
        p = ctx["p"]
        q_fm = ctx["q_fm"]
        sgb = ctx["sgb"]
        hidb = ctx["hidb"]
        wn_h = ctx["wn_h"]
        sp_dw = ctx["sp_dw"]
        sa_wn = ctx["sa_wn"]
        nsb = [(3, 4), (5, 6), (7, 3)]
        sd_x = None
        for i in range(3):
            sd_x = k.dve(lambda e, i=i: e.tensor_copy(Xb[i][:, :], bank[4 + i][:, 0:128]),
                         waits=[("SP", sp_dw)], inc=("SD", 1))
        for i in range(3):
            bank_free[4 + i] = ("SD", sd_x)
        if debug and h == 0:
            for i in range(3):
                k.gps(lambda e, i=i: e.dma_start(out=dw_dbg[i], in_=Xb[i][:, :]),
                      waits=[("SD", sd_x)], inc=("SST", 16))
                k.barrier_sst()
        sa_scale = [0, 0, 0]
        for i in range(3):
            ba = nsb[i][0]
            sa = k.act(lambda e, i=i: e.activation(sqG[:, i * 128:(i + 1) * 128], Xb[i][:, :],
                                                   AF.Square),
                       waits=[("SD", sd_x)], inc=("SA", 1))
            sd = k.dve(lambda e, i=i: e.tensor_reduce(facb[i][:, :], sqG[:, i * 128:(i + 1) * 128],
                                                      AX, ALU.add),
                       waits=[("SA", sa)], inc=("SD", 1))
            sd = k.dve(lambda e, i=i: e.tensor_copy(dwssb[i][:, :], facb[i][:, :]),
                       waits=[("SD", sd)], inc=("SD", 1))
            sp = mm(ba, lambda ba=ba: bank[ba][0:1, 0:1], lambda i=i: dwssb[i][:, :],
                    lambda: onc_sb[:, :], True, True,
                    waits=[("SD", sd), (bank_free[ba][0], bank_free[ba][1])], inc=("SP", 1))
            sa = k.act(lambda e, i=i, ba=ba: e.activation(smll[i][0:1, 0:1], bank[ba][0:1, 0:1],
                                                          AF.Sqrt),
                       waits=[("SP", sp)], inc=("SA", 1))
            sa = k.act(lambda e, i=i: e.activation(smll[i][0:1, 0:1], smll[i][0:1, 0:1],
                                                   AF.Identity, bias=cst_sb[0:1, 3:4]),
                       waits=[("SA", sa)], inc=("SA", 1))
            sd = k.dve(lambda e, i=i: e.reciprocal(smll[i][0:1, 1:2], smll[i][0:1, 0:1]),
                       waits=[("SA", sa)], inc=("SD", 1))
            sd = k.dve(lambda e, i=i: e.tensor_copy(smlb[i][0:1, 0:1], smll[i][0:1, 1:2]),
                       waits=[("SD", sd)], inc=("SD", 1))
            sp = mm(ba, lambda ba=ba: bank[ba][:, 128:129], lambda: onr_sb[:, :],
                    lambda i=i: smlb[i][0:1, 0:1], True, True,
                    waits=[("SD", sd)], inc=("SP", 1))
            sd = k.dve(lambda e, i=i, ba=ba: e.tensor_copy(facb[i][:, :], bank[ba][:, 128:129]),
                       waits=[("SP", sp)], inc=("SD", 1))
            bank_free[ba] = ("SD", sd)
            sa_scale[i] = k.act(lambda e, i=i: e.activation(Xb[i][:, :], Xb[i][:, :], AF.Identity,
                                                            scale=facb[i][:, 0:1]),
                                waits=[("SD", sd)], inc=("SA", 1))
            yield
        chain = [("SA", sa_scale[i]) for i in range(3)]
        for it, (ca, cb, cc) in enumerate(NS_COEFS):
            for i in range(3):
                ba, bb2 = nsb[i]
                sp = mm(ba, lambda ba=ba: bank[ba][:, 0:128], lambda i=i: Xb[i][:, :],
                        lambda: id_sb[:, :], True, True,
                        waits=[chain[i], (bank_free[ba][0], bank_free[ba][1])], inc=("SP", 1))
                sd = k.dve(lambda e, i=i, ba=ba: e.tensor_copy(XTb[i][:, :], bank[ba][:, 0:128]),
                           waits=[("SP", sp)], inc=("SD", 1))
                bank_free[ba] = ("SD", sd)
                yield
                sp = mm(bb2, lambda bb2=bb2: bank[bb2][:, 0:128], lambda i=i: XTb[i][:, :],
                        lambda i=i: XTb[i][:, :], True, True,
                        waits=[("SD", sd), (bank_free[bb2][0], bank_free[bb2][1])], inc=("SP", 1))
                sd = k.dve(lambda e, i=i, bb2=bb2: e.tensor_copy(Ab[i][:, :], bank[bb2][:, 0:128]),
                           waits=[("SP", sp)], inc=("SD", 1))
                bank_free[bb2] = ("SD", sd)
                yield
                sp = mm(ba, lambda ba=ba: bank[ba][:, 0:128], lambda i=i: Ab[i][:, :],
                        lambda i=i: Ab[i][:, :], True, True,
                        waits=[("SD", sd), (bank_free[ba][0], bank_free[ba][1])], inc=("SP", 1))
                k.dve(lambda e, i=i, cb=cb: e.tensor_scalar(Abf[i][:, :], Ab[i][:, :], cb, None,
                                                            ALU.mult))
                sd = k.dve(lambda e, i=i, cc=cc, ba=ba: e.scalar_tensor_tensor(
                    Bb[i][:, :], bank[ba][:, 0:128], cc, Abf[i][:, :], ALU.mult, ALU.add),
                    waits=[("SP", sp)], inc=("SD", 1))
                bank_free[ba] = ("SD", sd)
                yield
                sp = mm(bb2, lambda bb2=bb2: bank[bb2][:, 0:128], lambda i=i: Bb[i][:, :],
                        lambda i=i: Xb[i][:, :], True, True,
                        waits=[("SD", sd), (bank_free[bb2][0], bank_free[bb2][1])], inc=("SP", 1))
                sd = k.dve(lambda e, i=i, ca=ca, bb2=bb2: e.scalar_tensor_tensor(
                    Xb[i][:, :], Xb[i][:, :], ca, bank[bb2][:, 0:128], ALU.mult, ALU.add),
                    waits=[("SP", sp)], inc=("SD", 1))
                bank_free[bb2] = ("SD", sd)
                chain[i] = ("SD", sd)
                yield

        for i in range(3):
            sd = k.dve(lambda e, i=i, p=p: e.tensor_tensor(
                Wf32[:, :], wnat[p][:, i * 128:(i + 1) * 128], Xb[i][:, :], ALU.add),
                inc=("SD", 1))
            sa = k.act(lambda e, i=i: e.activation(sqG[:, i * 128:(i + 1) * 128], Wf32[:, :],
                                                   AF.Square),
                       waits=[("SD", sd)], inc=("SA", 1))
            sd = k.dve(lambda e, i=i: e.tensor_reduce(rn3[:, 0:1], sqG[:, i * 128:(i + 1) * 128],
                                                      AX, ALU.add),
                       waits=[("SA", sa)], inc=("SD", 1))
            sa = k.act(lambda e: e.activation(rn3[:, 1:2], rn3[:, 0:1], AF.Sqrt),
                       waits=[("SD", sd)], inc=("SA", 1))
            sa = k.act(lambda e: e.activation(rn3[:, 1:2], rn3[:, 1:2], AF.Identity,
                                              bias=cst_sb[:, 1:2]),
                       waits=[("SA", sa)], inc=("SA", 1))
            sd = k.dve(lambda e: e.reciprocal(rn3[:, 2:3], rn3[:, 1:2]),
                       waits=[("SA", sa)], inc=("SD", 1))
            sd = k.dve(lambda e, i=i: e.tensor_tensor(rn3[:, 2:3], rn3[:, 2:3], wn_h[:, i:i + 1],
                                                      ALU.mult),
                       waits=[("SA", sa_wn), ("SD", sd)], inc=("SD", 1))
            wnh_free[p] = sd
            sa = k.act(lambda e, i=i: e.activation(Wnb[i][:, :], Wf32[:, :], AF.Identity,
                                                   scale=rn3[:, 2:3]),
                       waits=[("SD", sd)], inc=("SA", 1))
            bh = (3, 5, 7)[i]
            sp = mm(bh, lambda bh=bh: bank[bh][:, 0:128], lambda i=i: Wnb[i][:, :],
                    lambda: id_sb[:, :],
                    True, True, waits=[("SA", sa), (bank_free[bh][0], bank_free[bh][1])],
                    inc=("SP", 1))
            sd = k.dve(lambda e, i=i, bh=bh: e.tensor_copy(WnT[i][:, :], bank[bh][:, 0:128]),
                       waits=[("SP", sp)], inc=("SD", 1))
            bank_free[bh] = ("SD", sd)
            yield
        wn_ready = k.v("SD")
        if debug and h == 0:
            for i in range(3):
                k.gps(lambda e, i=i: e.dma_start(out=wn_dbg[i], in_=Wnb[i][:, :]),
                      waits=[("SA", k.v("SA"))], inc=("SST", 16))
            k.barrier_sst()

        for st in range(4):
            cs = slice(st * 512, (st + 1) * 512)
            sp0 = mm(4, lambda: bank[4][:, :], lambda: WnT[0][:, :], lambda cs=cs, q_fm=q_fm: q_fm[:, cs],
                     True, True, waits=[("SD", wn_ready), (bank_free[4][0], bank_free[4][1])],
                     inc=("SP", 1))
            sp2 = mm(5, lambda: bank[5][:, :], lambda: WnT[2][:, :], lambda cs=cs, q_fm=q_fm: q_fm[:, cs],
                     True, True, waits=[(bank_free[5][0], bank_free[5][1])], inc=("SP", 1))
            sa = k.act(lambda e: e.activation(sgb[:, :], bank[4][:, :], AF.Silu),
                       waits=[("SP", sp0), ("SD", wn_ready)], inc=("SA", 1))
            bank_free[4] = ("SA", sa)
            sd = k.dve(lambda e: e.tensor_tensor(hidb[:, :], sgb[:, :], bank[5][:, :], ALU.mult),
                       waits=[("SA", sa), ("SP", sp2)], inc=("SD", 1))
            bank_free[5] = ("SD", sd)
            sp = mm(6, lambda: bank[6][:, :], lambda: WnT[1][:, :], lambda: hidb[:, :],
                    True, True, waits=[("SD", sd), (bank_free[6][0], bank_free[6][1])],
                    inc=("SP", 1))
            sa = k.act(lambda e, cs=cs: e.activation(o2_bf[:, cs], bank[6][:, :], AF.Square),
                       waits=[("SP", sp)], inc=("SA", 1))
            sd = k.dve(lambda e, cs=cs: e.tensor_copy(o_bf[:, cs], bank[6][:, :]),
                       waits=[("SA", sa)], inc=("SD", 1))
            bank_free[6] = ("SD", sd)
            yield
        qfm_free[p] = k.v("SP")
        o_ready = (k.v("SD"), k.v("SA"))
        if debug and h == 0:
            k.gps(lambda e: e.dma_start(out=o_dbg, in_=o_bf[:, :]),
                  waits=[("SD", o_ready[0])], inc=("SST", 16))
            k.barrier_sst()
        for t in range(16):
            sp = mm(7, lambda t=t: bank[7][:, t:t + 1],
                    lambda t=t: o2_bf[:, t * 128:(t + 1) * 128], lambda: onc_sb[:, :],
                    True, True,
                    waits=[("SA", o_ready[1]), (bank_free[7][0], bank_free[7][1])],
                    inc=("SP", 1) if t == 15 else None)
        sd = k.dve(lambda e: e.tensor_scalar(dsum[:, :], bank[7][:, 0:16], 1.0 / 128, 1e-5,
                                             ALU.mult, ALU.add),
                   waits=[("SP", sp)], inc=("SD", 1))
        bank_free[7] = ("SD", sd)
        sa = k.act(lambda e: e.activation(dsum[:, :], dsum[:, :], AF.Sqrt),
                   waits=[("SD", sd)], inc=("SA", 1))
        sd = k.dve(lambda e: e.reciprocal(rmfac[:, :], dsum[:, :]), waits=[("SA", sa)],
                   inc=("SD", 1))
        sa = k.act(lambda e: e.activation(rmfacb[:, :], rmfac[:, :], AF.Identity),
                   waits=[("SD", sd)], inc=("SA", 1))
        rm_ready = sa
        yield
        transpose_batch(lambda t: o_bf[:, t * 128:(t + 1) * 128],
                        lambda t, _: o_tm[:, t * 128:(t + 1) * 128],
                        16, 3, 7, (o_ready[0], rm_ready),
                        fuse_scale=lambda t: rmfacb[:, t:t + 1])
        yield
        sd = transpose_batch(lambda t: o_tm[:, t * 128:(t + 1) * 128],
                             lambda t0, cnt: o2_bf[:, t0 * 128:(t0 + cnt) * 128],
                             16, 3, 7, k.v("SD"))
        yield
        sx = k.syn(lambda e, h=h: e.dma_start(out=oscr_d[h], in_=o2_bf[:, :]),
                   waits=[("SD", sd)], inc=("SX", 16))
        oscr_done.append(sx)
        if debug:
            k.gps(lambda e, h=h: e.dma_start(out=oscr_dbg[h], in_=o2_bf[:, :]),
                  waits=[("SD", sd)], inc=("SST", 16))
        hw_free[p] = (k.v("SP"), k.v("SA"), k.v("SD"))

    back_it = None
    for h in range(n_heads):
        ctx = {}
        for tok in emit_front(h, ctx):
            if tok == "drain" and back_it is not None:
                for _ in back_it:
                    pass
                back_it = None
            elif back_it is not None:
                for _ in range(8):
                    if next(back_it, "_end") == "_end":
                        back_it = None
                        break
        if back_it is not None:
            for _ in back_it:
                pass
            back_it = None
        back_it = emit_back(h, ctx)
    if back_it is not None:
        for _ in back_it:
            pass

    at_es.close()
    k.phase_barrier()

    op_es = contextlib.ExitStack()
    C = op_es.enter_context
    wob = [C(nc.sbuf_tensor(f"wob{i}", [128, NK * 512], BF16)) for i in range(2)]
    ostr16 = C(nc.sbuf_tensor("ostr16", [128, NT * H * 128], BF16))
    oyb = [C(nc.sbuf_tensor(f"oyb{i}", [128, 512], F32)) for i in range(2)]
    wob_free = [0, 0]
    oyb_free_sem = [("SC", 0), ("SC", 0)]
    oscr_all = oscr_done[-1] if oscr_done else 0

    sxo16 = []
    for tt in range(NT):
        sxo16.append(k.syn(lambda e, tt=tt: e.dma_start(
            out=ostr16[:, tt * 2048:(tt + 1) * 2048].rearrange("p (hh t) -> p hh t", hh=H),
            in_=oscr_d[:, :, tt * 128:(tt + 1) * 128].rearrange("hh d t -> d hh t")),
            waits=[("SX", oscr_all)], inc=("SX", 16)))

    for m2t in range(4):
        pw = m2t % 2
        sw = k.gps(lambda e, m2t=m2t, pw=pw: e.dma_start(
            out=wob[pw][:, :].rearrange("p (c m) -> p c m", c=NK),
            in_=wo_d[:, m2t * 512:(m2t + 1) * 512].rearrange("(c p) m -> p c m", p=128)),
            waits=[("SP", wob_free[pw])], inc=("SW", 16))
        sd_w = k.dve(lambda e, pw=pw: e.tensor_scalar(wob[pw][:, :], wob[pw][:, :],
                                                      onw_sb[:, 0:1], None, ALU.mult),
                     waits=[("SW", sw), ("SX", sx_const)], inc=("SD", 1))
        for tt in range(NT):
            b = tt % 2
            for kc in range(H):
                sp = mm(b, lambda b=b: bank[b][:, :],
                        lambda tt=tt, kc=kc: ostr16[:, tt * 2048 + kc * 128:
                                                    tt * 2048 + (kc + 1) * 128],
                        lambda pw=pw, kc=kc: wob[pw][:, kc * 512:(kc + 1) * 512],
                        kc == 0, kc == H - 1,
                        waits=[("SX", sxo16[tt]), ("SD", sd_w),
                               (bank_free[b][0], bank_free[b][1])],
                        inc=("SP", 1) if kc == H - 1 else None)
            wob_free[pw] = sp
            po = tt % 2
            sd = k.dve(lambda e, b=b, po=po: e.tensor_copy(oyb[po][:, :], bank[b][:, :]),
                       waits=[("SP", sp), oyb_free_sem[po]], inc=("SD", 1))
            bank_free[b] = ("SD", sd)
            sc = k.syn(lambda e, tt=tt, m2t=m2t, po=po: e.dma_start(
                out=acc_d[tt * 128:(tt + 1) * 128, m2t * 512:(m2t + 1) * 512], in_=oyb[po][:, :]),
                waits=[("SD", sd)], inc=("SC", 16))
            oyb_free_sem[po] = ("SC", sc)
    op_es.close()
    k.phase_barrier()
    attn_acc_done = k.v("SC")

    ff_es = contextlib.ExitStack()
    C = ff_es.enter_context
    h_sb = C(nc.sbuf_tensor("h_sb", [128, NJ * S], BF16))
    wdnq = [C(nc.sbuf_tensor(f"wdnq{i}", [128, NJ * 512], BF16)) for i in range(2)]
    gpair = [C(nc.sbuf_tensor(f"gpair{i}", [128, NK * 256], BF16)) for i in range(2)]
    fsg = [C(nc.sbuf_tensor(f"fsg{i}", [128, 512], F32)) for i in range(2)]
    fsg_free = [0, 0]
    fyb = [C(nc.sbuf_tensor(f"fyb{i}", [128, 512], BF16)) for i in range(4)]
    gpair_free = [0, 0]
    wdnq_free = [0, 0]
    fyb_free = [("SC", 0)] * 4
    h_free = 0

    def gu_dma(u, j):
        pg = j % 2
        k.gps(lambda e, u=u, j=j, pg=pg: e.dma_start(
            out=gpair[pg][:, :].rearrange("p (c m) -> p c m", c=NK),
            in_=wgu_d[u, :, j * 256:(j + 1) * 256].rearrange("(c p) m -> p c m", p=128)),
            waits=[("SP", gpair_free[pg])], inc=("SW", 16))
        return k.v("SW")

    for u in range(n_units):
        sw_g = {0: gu_dma(u, 0)}
        for j in range(NJ):
            pg = j % 2
            if j + 1 < NJ:
                sw_g[j + 1] = gu_dma(u, j + 1)
            for st in range(4):
                bg, bu = st % 2, 2 + st % 2
                for kc in range(NK):
                    spg = mm(bg, lambda bg=bg: bank[bg][:, :],
                             lambda pg=pg, kc=kc: gpair[pg][:, kc * 256: kc * 256 + 128],
                             lambda kc=kc, st=st: xn_fm[:, kc * S + st * 512: kc * S + (st + 1) * 512],
                             kc == 0, kc == NK - 1,
                             waits=[("SW", sw_g[j]), (bank_free[bg][0], bank_free[bg][1])],
                             inc=("SP", 1) if kc == NK - 1 else None)
                for kc in range(NK):
                    spu = mm(bu, lambda bu=bu: bank[bu][:, :],
                             lambda pg=pg, kc=kc: gpair[pg][:, kc * 256 + 128: kc * 256 + 256],
                             lambda kc=kc, st=st: xn_fm[:, kc * S + st * 512: kc * S + (st + 1) * 512],
                             kc == 0, kc == NK - 1,
                             waits=[(bank_free[bu][0], bank_free[bu][1])],
                             inc=("SP", 1) if kc == NK - 1 else None)
                ps = st % 2
                sa = k.act(lambda e, ps=ps, bg=bg: e.activation(fsg[ps][:, :], bank[bg][:, :], AF.Silu),
                           waits=[("SP", spg), ("SD", fsg_free[ps])], inc=("SA", 1))
                bank_free[bg] = ("SA", sa)
                sd = k.dve(lambda e, ps=ps, bu=bu, j=j, st=st: e.tensor_tensor(
                    h_sb[:, j * S + st * 512: j * S + (st + 1) * 512], fsg[ps][:, :],
                    bank[bu][:, :], ALU.mult),
                    waits=[("SA", sa), ("SP", spu), ("SP", h_free)], inc=("SD", 1))
                bank_free[bu] = ("SD", sd)
                fsg_free[ps] = sd
            gpair_free[pg] = k.v("SP")
        h_ready = k.v("SD")
        if debug and u == 0:
            k.gps(lambda e: e.dma_start(out=h_dbg, in_=h_sb[:, :]),
                  waits=[("SD", h_ready)], inc=("SST", 16))
            k.barrier_sst()
        for m2t in range(4):
            pw = (u * 4 + m2t) % 2
            sw_d = k.gps(lambda e, u=u, m2t=m2t, pw=pw: e.dma_start(
                out=wdnq[pw][:, :].rearrange("p (c m) -> p c m", c=NJ),
                in_=wdn_d[u, :, m2t * 512:(m2t + 1) * 512].rearrange("(c p) m -> p c m", p=128)),
                waits=[("SP", wdnq_free[pw])], inc=("SW", 16))
            for tt in range(NT):
                b = 4 + tt % 2
                for j in range(NJ):
                    sp = mm(b, lambda b=b: bank[b][:, :],
                            lambda j=j, tt=tt: h_sb[:, j * S + tt * 128: j * S + (tt + 1) * 128],
                            lambda j=j, pw=pw: wdnq[pw][:, j * 512:(j + 1) * 512],
                            j == 0, j == NJ - 1,
                            waits=[("SW", sw_d), ("SD", h_ready),
                                   (bank_free[b][0], bank_free[b][1])],
                            inc=("SP", 1) if j == NJ - 1 else None)
                wdnq_free[pw] = sp
                py = tt % 4
                ccol = (lambda tt=tt, u=u: c_all[:, tt * 17 + u: tt * 17 + u + 1]) if u < 16 \
                    else (lambda tt=tt: c_all[:, tt * 17 + 16: tt * 17 + 17])
                sd = k.dve(lambda e, b=b, py=py, ccol=ccol: e.tensor_scalar(
                    fyb[py][:, :], bank[b][:, :], ccol(), None, ALU.mult),
                    waits=[("SP", sp), ("SA", c_ready), fyb_free[py]], inc=("SD", 1))
                bank_free[b] = ("SD", sd)
                sc = k.syn(lambda e, u=u, tt=tt, m2t=m2t, py=py: e.dma_start(
                    out=yscr_d[u, tt * 128:(tt + 1) * 128, m2t * 512:(m2t + 1) * 512],
                    in_=fyb[py][:, :]),
                    waits=[("SD", sd)], inc=("SC", 16))
                fyb_free[py] = ("SC", sc)
        h_free = k.v("SP")
    ff_es.close()
    k.phase_barrier()
    acc_all = k.v("SC")

    fn_es = contextlib.ExitStack()
    C = fn_es.enter_context
    fx = [C(nc.sbuf_tensor(f"fx{i}", [128, 2048], F32)) for i in range(2)]
    fa = [C(nc.sbuf_tensor(f"fa{i}", [128, 2048], F32)) for i in range(2)]
    fy = [C(nc.sbuf_tensor(f"fy{i}", [128, 2048], BF16)) for i in range(4)]
    f_free = [0, 0]
    fa_free = [0, 0]
    fy_free = [0] * 4
    for tt in range(NT):
        p = tt % 2
        sx1 = k.syn(lambda e, tt=tt, p=p: e.dma_start(out=fx[p][:, :],
                                                      in_=x_d[tt * 128:(tt + 1) * 128, :]),
                    waits=[("SST", f_free[p])], inc=("SX", 16))
        sx2 = k.syn(lambda e, tt=tt, p=p: e.dma_start(out=fa[p][:, :],
                                                      in_=acc_d[tt * 128:(tt + 1) * 128, :]),
                    waits=[("SC", acc_all), ("SD", fa_free[p])], inc=("SX", 16))
        sd = k.dve(lambda e, p=p: e.tensor_add(fx[p][:, :], fx[p][:, :], fa[p][:, :]),
                   waits=[("SX", sx2)], inc=("SD", 1))
        fa_free[p] = sd
        for u in range(n_units):
            uy = u % 4
            sxy = k.syn(lambda e, u=u, tt=tt, uy=uy: e.dma_start(
                out=fy[uy][:, :], in_=yscr_d[u, tt * 128:(tt + 1) * 128, :]),
                waits=[("SD", fy_free[uy])], inc=("SX", 16))
            sd = k.dve(lambda e, p=p, uy=uy: e.tensor_tensor(fx[p][:, :], fx[p][:, :],
                                                             fy[uy][:, :], ALU.add),
                       waits=[("SX", sxy)], inc=("SD", 1))
            fy_free[uy] = sd
        st = k.syn(lambda e, tt=tt, p=p: e.dma_start(out=out_d[tt * 128:(tt + 1) * 128, :],
                                                     in_=fx[p][:, :]),
                   waits=[("SD", sd)], inc=("SST", 16))
        f_free[p] = st
    fn_es.close()

    sst_final = k.v("SST")
    with nc.Block() as blk:
        blk.tensor(lambda e: [f(e) for f in k.ops["tensor"]])
        blk.scalar(lambda e: [f(e) for f in k.ops["scalar"]])
        blk.vector(lambda e: [f(e) for f in k.ops["vector"]])
        blk.gpsimd(lambda e: [f(e) for f in k.ops["gpsimd"]])

        def syncf(e):
            for f in k.ops["sync"]:
                f(e)
            e.wait_ge(k.sems["SST"], sst_final)
        blk.sync(syncf)
    return nc


def prep_inputs(L0_attn_x, L0_attn_to_qkv_w, L0_attn_o_proj_w, L0_attn_lr_proj_w,
                L0_attn_w0, L0_attn_w1, L0_attn_w2, L0_attn_o_norm_w,
                L0_ffn_gate_w, L0_ffn_expert_gate_up, L0_ffn_expert_down,
                L0_ffn_shared_gate_up, L0_ffn_shared_down, L0_ffn_shared_gate_vec):
    f = np.float32
    ac = np.ascontiguousarray
    wqkv_t = ac(np.asarray(L0_attn_to_qkv_w, f).T)
    wo_t = ac(np.asarray(L0_attn_o_proj_w, f).T)
    wlr_t = ac(np.asarray(L0_attn_lr_proj_w, f).T)
    w0 = np.asarray(L0_attn_w0, f)
    w1 = np.asarray(L0_attn_w1, f)
    w2 = np.asarray(L0_attn_w2, f)
    w_nat = ac(np.stack([w0, w1, w2], 0))
    w_t = ac(np.stack([w0.transpose(0, 2, 1), w2.transpose(0, 2, 1)], 0))
    onw = ac(np.asarray(L0_attn_o_norm_w, f).reshape(128, 1))
    wgate_t = np.zeros((D, 32), f)
    wgate_t[:, 0:16] = np.asarray(L0_ffn_gate_w, f).T
    wgate_t[:, 16:17] = np.asarray(L0_ffn_shared_gate_vec, f).T
    egu = np.asarray(L0_ffn_expert_gate_up, f)
    sgu = np.asarray(L0_ffn_shared_gate_up, f)
    edn = np.asarray(L0_ffn_expert_down, f)
    sdn = np.asarray(L0_ffn_shared_down, f)
    perm = np.empty(2816, np.int64)
    for j in range(NJ):
        perm[j * 256: j * 256 + 128] = np.arange(j * 128, (j + 1) * 128)
        perm[j * 256 + 128: (j + 1) * 256] = np.arange(EI + j * 128, EI + (j + 1) * 128)
    units_gu = np.empty((NU, D, 2816), f)
    units_dn = np.empty((NU, EI, D), f)
    for e in range(16):
        units_gu[e] = egu[e][perm].T
        units_dn[e] = edn[e].T
    for j in range(4):
        blk_g = sgu[j * EI:(j + 1) * EI]
        blk_u = sgu[5632 + j * EI: 5632 + (j + 1) * EI]
        gu = np.concatenate([blk_g, blk_u], 0)
        units_gu[16 + j] = gu[perm].T
        units_dn[16 + j] = sdn[:, j * EI:(j + 1) * EI].T
    shared = {
        "wqkv_t": wqkv_t, "wo_t": wo_t, "wlr_t": wlr_t, "w_nat": w_nat,
        "w_t": w_t, "onw": onw, "wgate_t": wgate_t,
        "wgu_t": ac(units_gu), "wdn_t": ac(units_dn),
    }
    xs = np.asarray(L0_attn_x, f)
    return xs, shared


def kernel(**inputs):
    xs, shared = prep_inputs(**inputs)
    nc = build()
    in_maps = [dict(shared, x=np.ascontiguousarray(xs[b])) for b in range(8)]
    res = run_bass_kernel_spmd(nc, in_maps, core_ids=list(range(8)))
    return np.stack([r["out"] for r in res.results], axis=0)
